# version 10
# baseline (speedup 1.0000x reference)
import numpy as np
import ml_dtypes
from contextlib import ExitStack

import concourse.bass as bass
import concourse.mybir as mybir
from concourse.bass_utils import run_bass_kernel_spmd

F32 = mybir.dt.float32
BF16 = mybir.dt.bfloat16
AF = mybir.ActivationFunctionType
ALU = mybir.AluOpType
AX = mybir.AxisListType

T, D, HD, NKV, NQH = 2048, 4096, 128, 8, 32
ITR, RANK = 14336, 16
KC = D // 128
NO = 8
NTOK = NO * 128
QT = ([0, 3, 4, 7, 8, 11, 12, 15], [1, 2, 5, 6, 9, 10, 13, 14])
LP = [256 * (o + 1) for o in range(NO)]
NEG = -1.0e30
EPS = 1e-5
NRS = ITR // 512
NKO = ITR // 128


PHASES = []


class Sem:
    def __init__(self, h, name=""):
        self.h = h
        self.name = name
        self.v = 0


class Ph:

    def __init__(self, nc, name, no_gpsimd_drain=False):
        self.nc = nc
        self.name = name
        self.no_gpsimd_drain = no_gpsimd_drain
        self.q = {k: [] for k in ("tensor", "vector", "scalar", "gpsimd", "sync")}
        self.meta = {k: [] for k in self.q}

    def op(self, eng, meth, *args, inc=None, inck=1, waits=(), **kw):
        wts = tuple((s.h, v) for (s, v) in waits if v > 0)

        def f(e, meth=meth, args=args, kw=kw, inc=inc, inck=inck, wts=wts):
            for (h, v) in wts:
                e.wait_ge(h, v)
            r = getattr(e, meth)(*args, **kw)
            if inc is not None:
                r.then_inc(inc.h, inck)

        self.q[eng].append(f)
        self.meta[eng].append((tuple((s.name, v) for (s, v) in waits if v > 0),
                               inc.name if inc is not None else None, inck, meth))
        if inc is not None:
            inc.v += inck

    def wait(self, eng, sem, v):
        if v > 0:
            self.q[eng].append(lambda e, h=sem.h, v=v: e.wait_ge(h, v))
            self.meta[eng].append((((sem.name, v),), None, 0, "wait"))

    def run(self):
        PHASES.append((self.name, self.meta))
        with self.nc.Block(name=self.name,
                           no_gpsimd_drain=self.no_gpsimd_drain) as blk:
            for en in ("sync", "gpsimd", "tensor", "vector", "scalar"):
                fns = self.q[en]
                if fns:
                    def body(e, fns=fns):
                        for fn in fns:
                            fn(e)
                    getattr(blk, en)(body)


def build_nc():
    nc = bass.Bass("TRN2", target_bir_lowering=False)

    xb = nc.dram_tensor("xb", [T, D], F32, kind="ExternalInput")
    x_own = nc.dram_tensor("x_own", [NTOK, D], F32, kind="ExternalInput")
    wq_t = nc.dram_tensor("wq_t", [NQH * 128, D], F32, kind="ExternalInput")
    wk_img = nc.dram_tensor("wk_img", [128, KC * NKV * 128], F32, kind="ExternalInput")
    wv_img = nc.dram_tensor("wv_img", [128, KC * NKV * 128], F32, kind="ExternalInput")
    woT = nc.dram_tensor("woT", [D, D], F32, kind="ExternalInput")
    w1T = nc.dram_tensor("w1T", [D, ITR], F32, kind="ExternalInput")
    w2t_in = nc.dram_tensor("w2t_in", [ITR, D], F32, kind="ExternalInput")
    c1_img = nc.dram_tensor("c1_img", [128, KC * RANK], F32, kind="ExternalInput")
    c2T = nc.dram_tensor("c2T", [RANK, ITR], F32, kind="ExternalInput")
    cosk = nc.dram_tensor("cosk", [128, T], F32, kind="ExternalInput")
    sink = nc.dram_tensor("sink", [128, T], F32, kind="ExternalInput")
    cosq = nc.dram_tensor("cosq", [128, NTOK], F32, kind="ExternalInput")
    sinq = nc.dram_tensor("sinq", [128, NTOK], F32, kind="ExternalInput")
    cmask = nc.dram_tensor("cmask", [128, NO * 256], F32, kind="ExternalInput")
    idb_in = nc.dram_tensor("idb_in", [128, 128], BF16, kind="ExternalInput")
    idf_in = nc.dram_tensor("idf_in", [128, 128], F32, kind="ExternalInput")
    rotT_in = nc.dram_tensor("rotT_in", [128, 128], F32, kind="ExternalInput")
    iota_in = nc.dram_tensor("iota_in", [128, 256], F32, kind="ExternalInput")
    out = nc.dram_tensor("out", [NTOK, D], F32, kind="ExternalOutput")
    x_bf = nc.dram_tensor("x_bf", [T, D], BF16)
    xo_bf = nc.dram_tensor("xo_bf", [NTOK, D], BF16)
    k_scr = nc.dram_tensor("k_scr", [NKV * 128, T], BF16)
    v_scr = nc.dram_tensor("v_scr", [T, NKV * 128], BF16)
    q_scr = nc.dram_tensor("q_scr", [NO * 128, NQH * 128], BF16)
    ao_scr = nc.dram_tensor("ao_scr", [NTOK, D], BF16)
    x1_scr = nc.dram_tensor("x1_scr", [NTOK, D], BF16)
    s_scr = nc.dram_tensor("s_scr", [NTOK, ITR], BF16)
    c1_bfs = nc.dram_tensor("c1_bfs", [128, KC * RANK], BF16)
    c2_bfs = nc.dram_tensor("c2_bfs", [RANK, ITR], BF16)
    wo_bf = nc.dram_tensor("wo_bf", [D, D], BF16)
    w1_bf = nc.dram_tensor("w1_bf", [D, ITR], BF16)
    w2_bf = nc.dram_tensor("w2_bf", [ITR, D], BF16)

    es = ExitStack()
    sem = {}
    for name in ("d_const d_cast d_castc d_wk d_xt d_wv d_xo t_kmm s_kev t_krot "
                 "v_krope s_kst0 s_kst1 t_vmm s_vev t_qmm s_qev t_qrot v_qrope "
                 "d_k0 d_k1 d_v0 d_v1 d_q0 d_q1 t_smm v_msk s_scp v_ngm s_exp "
                 "v_rcp t_pt s_pte t_av v_aoe "
                 "d_ao0 d_ao1 d_ao2 d_wo0 d_wo1 d_wo2 d_xr0 d_xr1 d_xr2 d_xr3 "
                 "t_wmm v_wadd s_wst v_wsta v_var s_sqrt v_ln s_x1 "
                 "d_x1t d_c1 d_c2 t_r16 s_r16 t_r16t s_r16t t_imm v_idx "
                 "d_w10 d_w11 d_w12 d_w13 d_w14 d_w15 t_zmm s_zrl v_zmk "
                 "d_bwo d_bw1 d_bw2 "
                 "t_omm v_oad s_ost v_osta v_var2 s_sqrt2 v_ln2"
                 ).split():
        sem[name] = Sem(es.enter_context(nc.semaphore(name)), name)
    for base, n in (("s_kst", 2), ("d_k", 2), ("d_v", 2), ("d_q", 2),
                    ("d_ao", 3), ("d_wo", 3), ("d_xr", 4), ("d_w1", 6)):
        sem[base] = [sem[f"{base}{i}"] for i in range(n)]
    for alias in ("s_vst", "s_qst", "s_aod", "d_ss", "d_out"):
        sem[alias] = sem["s_kst"]
    sem["d_st"] = sem["d_ao"] + sem["d_wo"]
    sem["d_w2"] = sem["d_w1"]
    sem["d_wq"] = sem["d_w1"][0:2]

    idb = es.enter_context(nc.sbuf_tensor("idb", [128, 128], BF16))
    idf = es.enter_context(nc.sbuf_tensor("idf", [128, 128], F32))
    rotT = es.enter_context(nc.sbuf_tensor("rotT", [128, 128], F32))
    iota = es.enter_context(nc.sbuf_tensor("iota", [128, 256], F32))
    cmk = es.enter_context(nc.sbuf_tensor("cmk", [128, NO * 256], F32))
    idxs = es.enter_context(nc.sbuf_tensor("idxs", [128, NO * 56], F32))

    p = Ph(nc, "init")
    S = sem
    for dst, src in ((idb, idb_in), (idf, idf_in), (rotT, rotT_in),
                     (iota, iota_in), (cmk, cmask)):
        p.op("sync", "dma_start", dst[:, :], src[:, :], inc=S["d_const"], inck=16)
    p.op("gpsimd", "dma_start", x_bf[:, :], xb[:, :], inc=S["d_cast"], inck=16)
    p.op("gpsimd", "dma_start", xo_bf[:, :], x_own[:, :], inc=S["d_cast"], inck=16)
    p.op("gpsimd", "dma_start", c1_bfs[:, :], c1_img[:, :], inc=S["d_castc"], inck=16)
    p.op("gpsimd", "dma_start", c2_bfs[:, :], c2T[:, :], inc=S["d_castc"], inck=16)
    p.wait("sync", S["d_const"], S["d_const"].v)
    p.wait("gpsimd", S["d_cast"], S["d_cast"].v)
    p.wait("gpsimd", S["d_castc"], S["d_castc"].v)
    p.run()

    _st1 = ExitStack()
    xt_sb = _st1.enter_context(nc.sbuf_tensor("xt_sb", [128, KC * 1024], BF16))
    cosk_sb = _st1.enter_context(nc.sbuf_tensor("cosk_sb", [128, T], F32))
    sink_sb = _st1.enter_context(nc.sbuf_tensor("sink_sb", [128, T], F32))
    if True:

        _st2 = ExitStack()
        wk_sb = _st2.enter_context(nc.sbuf_tensor("wk_sb", [128, KC * NKV * 128], BF16))
        kraw0 = _st2.enter_context(nc.sbuf_tensor("kraw0", [128, 512], F32))
        kraw1 = _st2.enter_context(nc.sbuf_tensor("kraw1", [128, 512], F32))
        ktmp1 = _st2.enter_context(nc.sbuf_tensor("ktmp1", [128, 512], F32))
        ktmp2 = _st2.enter_context(nc.sbuf_tensor("ktmp2", [128, 512], F32))
        kbf0 = _st2.enter_context(nc.sbuf_tensor("kbf0", [128, 512], BF16))
        kbf1 = _st2.enter_context(nc.sbuf_tensor("kbf1", [128, 512], BF16))
        kps0 = _st2.enter_context(nc.psum_tensor("kps0", [128, 512], F32))
        kps1 = _st2.enter_context(nc.psum_tensor("kps1", [128, 512], F32))
        krt0 = _st2.enter_context(nc.psum_tensor("krt0", [128, 512], F32))
        krt1 = _st2.enter_context(nc.psum_tensor("krt1", [128, 512], F32))
        if True:
            kraw = (kraw0, kraw1)
            kbf = (kbf0, kbf1)
            kps = (kps0, kps1)
            krt = (krt0, krt1)
            p = Ph(nc, "kproj")
            p.op("sync", "dma_start", cosk_sb[:, :], cosk[:, :], inc=S["d_const"], inck=16)
            p.op("sync", "dma_start", sink_sb[:, :], sink[:, :], inc=S["d_const"], inck=16)
            cos_base = S["d_const"].v
            p.op("gpsimd", "dma_start", wk_sb[:, :], wk_img[:, :],
                 waits=[(S["d_cast"], 32)], inc=S["d_wk"], inck=16)

            units = [(half, h, tc) for half in range(2) for h in range(NKV)
                     for tc in range(2)]
            for half in range(2):
                for k in range(KC):
                    w = []
                    if half == 1 and k == 0:
                        w.append((S["t_kmm"], 16))
                    if half == 0 and k == 0:
                        w.append((S["d_cast"], 32))
                    p.op("sync", "dma_start", xt_sb[:, k * 1024:(k + 1) * 1024],
                         x_bf[half * 1024:(half + 1) * 1024, k * 128:(k + 1) * 128],
                         transpose=True, waits=w, inc=S["d_xt"], inck=16)
            for u, (half, h, tc) in enumerate(units):
                gpos = half * 1024 + tc * 512
                w = [(S["s_kev"], u - 1)]
                if u == 0 or (half == 1 and h == 0 and tc == 0):
                    w.append((S["d_xt"], 16 * 32 * (half + 1)))
                    w.append((S["d_wk"], 16))
                for k in range(KC):
                    p.op("tensor", "matmul", kps[u % 2][:, :],
                         wk_sb[:, k * 1024 + h * 128: k * 1024 + (h + 1) * 128],
                         xt_sb[:, k * 1024 + tc * 512: k * 1024 + tc * 512 + 512],
                         start=(k == 0), stop=(k == KC - 1),
                         waits=(w if k == 0 else ()),
                         inc=(S["t_kmm"] if k == KC - 1 else None))
                if u >= 1:
                    p.op("tensor", "matmul", krt[(u - 1) % 2][:, :], rotT[:, :],
                         kraw[(u - 1) % 2][:, :], start=True, stop=True,
                         waits=[(S["s_kev"], u), (S["v_krope"], u - 2)],
                         inc=S["t_krot"])
                p.op("scalar", "copy", kraw[u % 2][:, :], kps[u % 2][:, :],
                     waits=[(S["t_kmm"], u + 1), (S["t_krot"], u - 1),
                            (S["v_krope"], u - 1)], inc=S["s_kev"])
                if u >= 1:
                    v = u - 1
                    halfv, hv, tcv = units[v]
                    gposv = halfv * 1024 + tcv * 512
                    p.op("vector", "tensor_tensor", ktmp1[:, :], kraw[v % 2][:, :],
                         cosk_sb[:, gposv:gposv + 512], ALU.mult,
                         waits=[(S["s_kev"], v + 1), (S["t_krot"], v + 1),
                                (S["d_const"], cos_base)])
                    p.op("vector", "tensor_tensor", ktmp2[:, :], krt[v % 2][:, :],
                         sink_sb[:, gposv:gposv + 512], ALU.mult)
                    p.op("vector", "tensor_tensor", kbf[v % 2][:, :], ktmp1[:, :],
                         ktmp2[:, :], ALU.add,
                         waits=[(S["s_kst"][v % 2], S["s_kst"][v % 2].v)],
                         inc=S["v_krope"])
                    p.op("scalar", "dma_start",
                         k_scr[hv * 128:(hv + 1) * 128, gposv:gposv + 512],
                         kbf[v % 2][:, :],
                         waits=[(S["v_krope"], v + 1)], inc=S["s_kst"][v % 2], inck=16)
            v = len(units) - 1
            halfv, hv, tcv = units[v]
            gposv = halfv * 1024 + tcv * 512
            p.op("tensor", "matmul", krt[v % 2][:, :], rotT[:, :],
                 kraw[v % 2][:, :], start=True, stop=True,
                 waits=[(S["s_kev"], v + 1), (S["v_krope"], v - 1)],
                 inc=S["t_krot"])
            p.op("vector", "tensor_tensor", ktmp1[:, :], kraw[v % 2][:, :],
                 cosk_sb[:, gposv:gposv + 512], ALU.mult,
                 waits=[(S["t_krot"], v + 1)])
            p.op("vector", "tensor_tensor", ktmp2[:, :], krt[v % 2][:, :],
                 sink_sb[:, gposv:gposv + 512], ALU.mult)
            p.op("vector", "tensor_tensor", kbf[v % 2][:, :], ktmp1[:, :],
                 ktmp2[:, :], ALU.add, inc=S["v_krope"])
            p.op("scalar", "dma_start",
                 k_scr[hv * 128:(hv + 1) * 128, gposv:gposv + 512],
                 kbf[v % 2][:, :], waits=[(S["v_krope"], v + 1)],
                 inc=S["s_kst"][v % 2], inck=16)
            p.wait("scalar", S["s_kst"][0], S["s_kst"][0].v)
            p.wait("scalar", S["s_kst"][1], S["s_kst"][1].v)
            p.wait("sync", S["d_xt"], S["d_xt"].v)
            p.wait("gpsimd", S["d_wk"], S["d_wk"].v)
            p.run()

        _st2.close()
        _st3 = ExitStack()
        wv_sb = _st3.enter_context(nc.sbuf_tensor("wv_sb", [128, KC * NKV * 128], BF16))
        vst0 = _st3.enter_context(nc.sbuf_tensor("vst0", [128, 1024], BF16))
        vst1 = _st3.enter_context(nc.sbuf_tensor("vst1", [128, 1024], BF16))
        vps0 = _st3.enter_context(nc.psum_tensor("vps0", [128, 512], F32))
        vps1 = _st3.enter_context(nc.psum_tensor("vps1", [128, 512], F32))
        if True:
            vst = (vst0, vst1)
            vps = (vps0, vps1)
            p = Ph(nc, "vproj")
            p.op("gpsimd", "dma_start", wv_sb[:, :], wv_img[:, :],
                 inc=S["d_wv"], inck=16)
            xt_base = S["d_xt"].v
            for half in range(2):
                for k in range(KC):
                    w = []
                    if half == 1 and k == 0:
                        w.append((S["t_vmm"], 16))
                    p.op("sync", "dma_start", xt_sb[:, k * 1024:(k + 1) * 1024],
                         x_bf[half * 1024:(half + 1) * 1024, k * 128:(k + 1) * 128],
                         transpose=True, waits=w, inc=S["d_xt"], inck=16)
            units = [(half, tl, hg) for half in range(2) for tl in range(8)
                     for hg in range(2)]
            for u, (half, tl, hg) in enumerate(units):
                ttg = half * 8 + tl
                w = [(S["s_vev"], u - 1)]
                if tl == 0 and hg == 0:
                    w.append((S["d_xt"], xt_base + 16 * 32 * (half + 1)))
                    w.append((S["d_wv"], 16))
                for k in range(KC):
                    p.op("tensor", "matmul", vps[u % 2][:, :],
                         xt_sb[:, k * 1024 + tl * 128: k * 1024 + (tl + 1) * 128],
                         wv_sb[:, k * 1024 + hg * 512: k * 1024 + hg * 512 + 512],
                         start=(k == 0), stop=(k == KC - 1),
                         waits=(w if k == 0 else ()),
                         inc=(S["t_vmm"] if k == KC - 1 else None))
                p.op("scalar", "copy", vst[ttg % 2][:, hg * 512:hg * 512 + 512],
                     vps[u % 2][:, :],
                     waits=[(S["t_vmm"], u + 1),
                            (S["s_vst"][ttg % 2], S["s_vst"][ttg % 2].v)],
                     inc=S["s_vev"])
                if hg == 1:
                    p.op("scalar", "dma_start",
                         v_scr[ttg * 128:(ttg + 1) * 128, :], vst[ttg % 2][:, :],
                         inc=S["s_vst"][ttg % 2], inck=16)
            p.wait("scalar", S["s_vst"][0], S["s_vst"][0].v)
            p.wait("scalar", S["s_vst"][1], S["s_vst"][1].v)
            p.wait("sync", S["d_xt"], S["d_xt"].v)
            p.wait("gpsimd", S["d_wv"], S["d_wv"].v)
            p.run()

        _st3.close()
    _st1.close()
    _st4 = ExitStack()
    xTown = _st4.enter_context(nc.sbuf_tensor("xTown", [128, KC * 1024], BF16))
    cosq_sb = _st4.enter_context(nc.sbuf_tensor("cosq_sb", [128, NTOK], F32))
    sinq_sb = _st4.enter_context(nc.sbuf_tensor("sinq_sb", [128, NTOK], F32))
    wq_sb0 = _st4.enter_context(nc.sbuf_tensor("wq_sb0", [128, D], BF16))
    wq_sb1 = _st4.enter_context(nc.sbuf_tensor("wq_sb1", [128, D], BF16))
    qraw0 = _st4.enter_context(nc.sbuf_tensor("qraw0", [128, 512], F32))
    qraw1 = _st4.enter_context(nc.sbuf_tensor("qraw1", [128, 512], F32))
    qtmp1 = _st4.enter_context(nc.sbuf_tensor("qtmp1", [128, 512], F32))
    qtmp2 = _st4.enter_context(nc.sbuf_tensor("qtmp2", [128, 512], F32))
    qbf0 = _st4.enter_context(nc.sbuf_tensor("qbf0", [128, 512], BF16))
    qbf1 = _st4.enter_context(nc.sbuf_tensor("qbf1", [128, 512], BF16))
    qps0 = _st4.enter_context(nc.psum_tensor("qps0", [128, 512], F32))
    qps1 = _st4.enter_context(nc.psum_tensor("qps1", [128, 512], F32))
    qrt0 = _st4.enter_context(nc.psum_tensor("qrt0", [128, 512], F32))
    qrt1 = _st4.enter_context(nc.psum_tensor("qrt1", [128, 512], F32))
    if True:
        wq_sb = (wq_sb0, wq_sb1)
        qraw = (qraw0, qraw1)
        qbf = (qbf0, qbf1)
        qps = (qps0, qps1)
        qrt = (qrt0, qrt1)
        p = Ph(nc, "qproj")
        p.op("sync", "dma_start", cosq_sb[:, :], cosq[:, :], inc=S["d_const"], inck=16)
        p.op("sync", "dma_start", sinq_sb[:, :], sinq[:, :], inc=S["d_const"], inck=16)
        for k in range(KC):
            p.op("sync", "dma_start", xTown[:, k * 1024:(k + 1) * 1024],
                 xo_bf[:, k * 128:(k + 1) * 128], transpose=True,
                 waits=([(S["d_cast"], 32)] if k == 0 else ()),
                 inc=S["d_xo"], inck=16)
        wqv = []
        for head in range(NQH):
            p.op("gpsimd", "dma_start", wq_sb[head % 2][:, :],
                 wq_t[head * 128:(head + 1) * 128, :],
                 waits=[(S["t_qmm"], 2 * (head - 1))],
                 inc=S["d_wq"][head % 2], inck=16)
            wqv.append(S["d_wq"][head % 2].v)
        cq_base = S["d_const"].v

        def emit_qrope(v):
            headv, tgv = divmod(v, 2)
            p.op("vector", "tensor_tensor", qtmp1[:, :], qraw[v % 2][:, :],
                 cosq_sb[:, tgv * 512:tgv * 512 + 512], ALU.mult,
                 waits=[(S["t_qrot"], v + 1), (S["d_const"], cq_base)])
            p.op("vector", "tensor_tensor", qtmp2[:, :], qrt[v % 2][:, :],
                 sinq_sb[:, tgv * 512:tgv * 512 + 512], ALU.mult)
            p.op("vector", "tensor_tensor", qbf[v % 2][:, :], qtmp1[:, :],
                 qtmp2[:, :], ALU.add,
                 waits=[(S["s_qst"][v % 2], S["s_qst"][v % 2].v)],
                 inc=S["v_qrope"])
            for j in range(4):
                slot = tgv * 4 + j
                p.op("scalar", "dma_start",
                     q_scr[slot * 128:(slot + 1) * 128,
                           headv * 128:(headv + 1) * 128],
                     qbf[v % 2][:, j * 128:(j + 1) * 128],
                     waits=([(S["v_qrope"], v + 1)] if j == 0 else ()),
                     inc=S["s_qst"][v % 2], inck=16)

        for u in range(NQH * 2):
            head, tg = divmod(u, 2)
            w = [(S["s_qev"], u - 1)]
            if tg == 0:
                w.append((S["d_wq"][head % 2], wqv[head]))
            if u == 0:
                w.append((S["d_xo"], 16 * KC))
            for k in range(KC):
                p.op("tensor", "matmul", qps[u % 2][:, :],
                     wq_sb[head % 2][:, k * 128:(k + 1) * 128],
                     xTown[:, k * 1024 + tg * 512: k * 1024 + tg * 512 + 512],
                     start=(k == 0), stop=(k == KC - 1),
                     waits=(w if k == 0 else ()),
                     inc=(S["t_qmm"] if k == KC - 1 else None))
            if u >= 1:
                p.op("tensor", "matmul", qrt[(u - 1) % 2][:, :], rotT[:, :],
                     qraw[(u - 1) % 2][:, :], start=True, stop=True,
                     waits=[(S["s_qev"], u), (S["v_qrope"], u - 2)],
                     inc=S["t_qrot"])
            p.op("scalar", "copy", qraw[u % 2][:, :], qps[u % 2][:, :],
                 waits=[(S["t_qmm"], u + 1), (S["t_qrot"], u - 1),
                        (S["v_qrope"], u - 1)], inc=S["s_qev"])
            if u >= 1:
                emit_qrope(u - 1)
        v = NQH * 2 - 1
        p.op("tensor", "matmul", qrt[v % 2][:, :], rotT[:, :], qraw[v % 2][:, :],
             start=True, stop=True,
             waits=[(S["s_qev"], v + 1), (S["v_qrope"], v - 1)], inc=S["t_qrot"])
        emit_qrope(v)
        p.wait("scalar", S["s_qst"][0], S["s_qst"][0].v)
        p.wait("scalar", S["s_qst"][1], S["s_qst"][1].v)
        p.wait("sync", S["d_xo"], S["d_xo"].v)
        p.wait("gpsimd", S["d_wq"][0], S["d_wq"][0].v)
        p.wait("gpsimd", S["d_wq"][1], S["d_wq"][1].v)
        p.run()

    _st4.close()
    _st5 = ExitStack()
    ksb0 = _st5.enter_context(nc.sbuf_tensor("ksb0", [128, NKV * 2048], BF16))
    ksb1 = _st5.enter_context(nc.sbuf_tensor("ksb1", [128, NKV * 2048], BF16))
    vsb0 = _st5.enter_context(nc.sbuf_tensor("vsb0", [128, 16 * 1024], BF16))
    vsb1 = _st5.enter_context(nc.sbuf_tensor("vsb1", [128, 16 * 1024], BF16))
    qsb0 = _st5.enter_context(nc.sbuf_tensor("qsb0", [128, NQH * 128], BF16))
    qsb1 = _st5.enter_context(nc.sbuf_tensor("qsb1", [128, NQH * 128], BF16))
    Psb0 = _st5.enter_context(nc.sbuf_tensor("Psb0", [128, 2048], BF16))
    Psb1 = _st5.enter_context(nc.sbuf_tensor("Psb1", [128, 2048], BF16))
    Psb2 = _st5.enter_context(nc.sbuf_tensor("Psb2", [128, 2048], BF16))
    Ssb0 = _st5.enter_context(nc.sbuf_tensor("Ssb0", [128, 2048], F32))
    Ssb1 = _st5.enter_context(nc.sbuf_tensor("Ssb1", [128, 2048], F32))
    Ssb2 = _st5.enter_context(nc.sbuf_tensor("Ssb2", [128, 2048], F32))
    PTsb0 = _st5.enter_context(nc.sbuf_tensor("PTsb0", [128, 512], BF16))
    PTsb1 = _st5.enter_context(nc.sbuf_tensor("PTsb1", [128, 512], BF16))
    negm4 = _st5.enter_context(nc.sbuf_tensor("negm4", [128, 4], F32))
    lsum4 = _st5.enter_context(nc.sbuf_tensor("lsum4", [128, 4], F32))
    rinv0 = _st5.enter_context(nc.sbuf_tensor("rinv0", [128, 4], F32))
    rinv1 = _st5.enter_context(nc.sbuf_tensor("rinv1", [128, 4], F32))
    aoq0 = _st5.enter_context(nc.sbuf_tensor("aoq0", [128, 512], BF16))
    aoq1 = _st5.enter_context(nc.sbuf_tensor("aoq1", [128, 512], BF16))
    Sps = _st5.enter_context(nc.psum_tensor("Sps", [128, 1024], F32))
    PTps0 = _st5.enter_context(nc.psum_tensor("PTps0", [128, 1024], BF16))
    PTps1 = _st5.enter_context(nc.psum_tensor("PTps1", [128, 1024], BF16))
    AOps0 = _st5.enter_context(nc.psum_tensor("AOps0", [128, 512], F32))
    AOps1 = _st5.enter_context(nc.psum_tensor("AOps1", [128, 512], F32))
    if True:
        ksb = (ksb0, ksb1)
        vsb = (vsb0, vsb1)
        qsb = (qsb0, qsb1)
        Psb = (Psb0, Psb1, Psb2)
        Ssb = (Ssb0, Ssb1, Ssb2)
        PTsb = (PTsb0, PTsb1)
        rinv = (rinv0, rinv1)
        aoq = (aoq0, aoq1)
        PTps = (PTps0, PTps1)
        AOps = (AOps0, AOps1)
        p = Ph(nc, "attn", no_gpsimd_drain=True)
        p.op("gpsimd", "dma_start", wo_bf[:, :], woT[:, :], inc=S["d_bwo"], inck=16)
        smm_end = [0] * NO
        av_end = [0] * NO
        grp_glob = 0
        pt_end_unit = {}
        kst_done = [(S["s_kst"][i], S["s_kst"][i].v) for i in range(2)]
        vst_done = [(S["s_vst"][i], S["s_vst"][i].v) for i in range(2)]
        qst_done = [(S["s_qst"][i], S["s_qst"][i].v) for i in range(2)]

        def unit_groups(o):
            nj = 2 * (o + 1)
            return [(gs, min(4, nj - gs * 4)) for gs in range((nj + 3) // 4)]

        def emit_ptav(v):
            nonlocal grp_glob
            o, rem = divmod(v, 32)
            h, g = divmod(rem, 4)
            H = o * 8 + h
            groups = unit_groups(o)
            ng = len(groups)
            base = grp_glob
            for gi, (gs, width) in enumerate(groups):
                gg = base + gi
                for jj in range(width):
                    w = []
                    if jj == 0:
                        w.append((S["s_exp"], v + 1))
                        w.append((S["s_pte"], gg - 1))
                    p.op("tensor", "transpose",
                         PTps[gg % 2][:, jj * 128:(jj + 1) * 128],
                         Psb[v % 3][:, (gs * 4 + jj) * 128:(gs * 4 + jj + 1) * 128],
                         idb[:, :], waits=w,
                         inc=(S["t_pt"] if jj == width - 1 else None))
                p.op("vector", "tensor_copy", PTsb[gg % 2][:, 0:width * 128],
                     PTps[gg % 2][:, 0:width * 128],
                     waits=[(S["t_pt"], gg + 1), (S["t_av"], gg - 1)],
                     inc=S["s_pte"])
                for jj in range(width):
                    j = gs * 4 + jj
                    w = []
                    if jj == 0:
                        w.append((S["s_pte"], gg + 1))
                    if gi == 0 and jj == 0 and g == 0:
                        w.append((S["v_aoe"], H - 1))
                    p.op("tensor", "matmul", AOps[H % 2][:, g * 128:(g + 1) * 128],
                         PTsb[gg % 2][:, jj * 128:(jj + 1) * 128],
                         vsb[o % 2][:, j * 1024 + h * 128: j * 1024 + (h + 1) * 128],
                         start=(gi == 0 and jj == 0),
                         stop=(gi == ng - 1 and jj == width - 1),
                         waits=w,
                         inc=(S["t_av"] if jj == width - 1 else None))
            grp_glob = base + ng
            pt_end_unit[v] = S["t_pt"].v
            p.op("vector", "reciprocal", rinv[H % 2][:, g:g + 1],
                 lsum4[:, v % 4:v % 4 + 1],
                 waits=[(S["s_exp"], v + 1)], inc=S["v_rcp"])
            p.op("vector", "drain")
            if g == 3:
                for gg2 in range(4):
                    w = []
                    if gg2 == 0:
                        w = [(S["t_av"], S["t_av"].v),
                             (S["s_aod"][H % 2], S["s_aod"][H % 2].v)]
                    p.op("vector", "tensor_scalar",
                         aoq[H % 2][:, gg2 * 128:(gg2 + 1) * 128],
                         AOps[H % 2][:, gg2 * 128:(gg2 + 1) * 128],
                         rinv[H % 2][:, gg2:gg2 + 1], None, ALU.mult,
                         waits=w, inc=(S["v_aoe"] if gg2 == 3 else None))
                p.op("scalar", "dma_start",
                     ao_scr[o * 128:(o + 1) * 128, h * 512:(h + 1) * 512],
                     aoq[H % 2][:, :], waits=[(S["v_aoe"], H + 1)],
                     inc=S["s_aod"][H % 2], inck=16)

        gchunk = [0]

        def emit_exp(v):
            o2 = v // 32
            w = [(S["v_ngm"], v + 1), (S["v_rcp"], v - 3)]
            if v >= 3:
                w.append((S["t_pt"], pt_end_unit[v - 3]))
            p.op("scalar", "activation", Psb[v % 3][:, 0:LP[o2]],
                 Ssb[v % 3][:, 0:LP[o2]], AF.Exp,
                 bias=negm4[:, v % 4:v % 4 + 1], scale=1.0,
                 accum_out=lsum4[:, v % 4:v % 4 + 1], waits=w, inc=S["s_exp"])

        for o in range(NO):
            for hh in range(NKV):
                w = []
                if hh == 0:
                    w = [(S["t_smm"], smm_end[o - 2] if o >= 2 else 0)]
                    if o == 0:
                        w += kst_done
                    w = [(s, v) for (s, v) in w if v > 0]
                p.op("sync", "dma_start",
                     ksb[o % 2][:, hh * 2048: hh * 2048 + LP[o]],
                     k_scr[hh * 128:(hh + 1) * 128, 0:LP[o]],
                     waits=w, inc=S["d_k"][o % 2], inck=16)
            for j in range(2 * (o + 1)):
                w = []
                if j == 0:
                    w = [(S["t_av"], av_end[o - 2] if o >= 2 else 0)]
                    if o == 0:
                        w += vst_done
                    w = [(s, v) for (s, v) in w if v > 0]
                p.op("sync", "dma_start", vsb[o % 2][:, j * 1024:(j + 1) * 1024],
                     v_scr[j * 128:(j + 1) * 128, :], waits=w,
                     inc=S["d_v"][o % 2], inck=16)
            qw = [(S["t_smm"], smm_end[o - 2] if o >= 2 else 0)]
            if o == 0:
                qw += qst_done
            p.op("sync", "dma_start", qsb[o % 2][:, :],
                 q_scr[o * 128:(o + 1) * 128, :],
                 waits=[(s, v) for (s, v) in qw if v > 0],
                 inc=S["d_q"][o % 2], inck=16)
            k_ck = (S["d_k"][o % 2], S["d_k"][o % 2].v)
            v_ck = (S["d_v"][o % 2], S["d_v"][o % 2].v)
            q_ck = (S["d_q"][o % 2], S["d_q"][o % 2].v)
            ns = (LP[o] + 511) // 512
            for h in range(NKV):
                for g in range(4):
                    u = o * 32 + h * 4 + g
                    if u > o * 32 + 1:
                        emit_ptav(u - 2)
                    if u > o * 32:
                        emit_exp(u - 1)
                    for c in range(ns):
                        cw = min(512, LP[o] - c * 512)
                        gc = gchunk[0]
                        w = [(S["s_scp"], gc - 1)]
                        if c == 0 and h == 0 and g == 0:
                            w += [k_ck, v_ck, q_ck]
                        p.op("tensor", "matmul",
                             Sps[:, (gc % 2) * 512:(gc % 2) * 512 + cw],
                             qsb[o % 2][:, (h * 4 + g) * 128:(h * 4 + g + 1) * 128],
                             ksb[o % 2][:, h * 2048 + c * 512: h * 2048 + c * 512 + cw],
                             start=True, stop=True, waits=w, inc=S["t_smm"])
                        w2 = [(S["t_smm"], S["t_smm"].v)]
                        if c == 0:
                            w2.append((S["s_exp"], u - 2))
                        p.op("scalar", "copy",
                             Ssb[u % 3][:, c * 512:c * 512 + cw],
                             Sps[:, (gc % 2) * 512:(gc % 2) * 512 + cw],
                             waits=w2, inc=S["s_scp"])
                        gchunk[0] += 1
                    p.op("vector", "tensor_tensor",
                         Ssb[u % 3][:, LP[o] - 256:LP[o]],
                         Ssb[u % 3][:, LP[o] - 256:LP[o]],
                         cmk[:, o * 256:(o + 1) * 256], ALU.add,
                         waits=[(S["s_scp"], S["s_scp"].v)], inc=S["v_msk"])
                    p.op("vector", "reduce_max", negm4[:, u % 4:u % 4 + 1],
                         Ssb[u % 3][:, 0:LP[o]], axis=AX.X, negate=True,
                         waits=[(S["s_exp"], u - 3)], inc=S["v_ngm"])
            emit_exp(o * 32 + 31)
            emit_ptav(o * 32 + 30)
            emit_ptav(o * 32 + 31)
            smm_end[o] = S["t_smm"].v
            av_end[o] = S["t_av"].v
        p.wait("scalar", S["s_aod"][0], S["s_aod"][0].v)
        p.wait("scalar", S["s_aod"][1], S["s_aod"][1].v)
        for i in range(2):
            p.wait("sync", S["d_k"][i], S["d_k"][i].v)
            p.wait("sync", S["d_v"][i], S["d_v"][i].v)
            p.wait("sync", S["d_q"][i], S["d_q"][i].v)
        p.run()

    _st5.close()
    with nc.sbuf_tensor("r1sb", [128, NO * D], BF16) as r1sb:

        _st6 = ExitStack()
        aoc0 = _st6.enter_context(nc.sbuf_tensor("aoc0", [128, 1024], BF16))
        aoc1 = _st6.enter_context(nc.sbuf_tensor("aoc1", [128, 1024], BF16))
        aoc2 = _st6.enter_context(nc.sbuf_tensor("aoc2", [128, 1024], BF16))
        woc0 = _st6.enter_context(nc.sbuf_tensor("woc0", [128, 512], BF16))
        woc1 = _st6.enter_context(nc.sbuf_tensor("woc1", [128, 512], BF16))
        woc2 = _st6.enter_context(nc.sbuf_tensor("woc2", [128, 512], BF16))
        xr0 = _st6.enter_context(nc.sbuf_tensor("xr0", [128, 512], F32))
        xr1 = _st6.enter_context(nc.sbuf_tensor("xr1", [128, 512], F32))
        xr2 = _st6.enter_context(nc.sbuf_tensor("xr2", [128, 512], F32))
        xr3 = _st6.enter_context(nc.sbuf_tensor("xr3", [128, 512], F32))
        tw0 = _st6.enter_context(nc.sbuf_tensor("tw0", [128, 512], F32))
        tw1 = _st6.enter_context(nc.sbuf_tensor("tw1", [128, 512], F32))
        sqd0 = _st6.enter_context(nc.sbuf_tensor("sqd0", [128, 512], BF16))
        sqd1 = _st6.enter_context(nc.sbuf_tensor("sqd1", [128, 512], BF16))
        ws0 = _st6.enter_context(nc.sbuf_tensor("ws0", [128, 2], F32))
        ws1 = _st6.enter_context(nc.sbuf_tensor("ws1", [128, 2], F32))
        sacc = _st6.enter_context(nc.sbuf_tensor("sacc", [128, 8], F32))
        qacc = _st6.enter_context(nc.sbuf_tensor("qacc", [128, 8], F32))
        mn = _st6.enter_context(nc.sbuf_tensor("mn", [128, 1], F32))
        mnn = _st6.enter_context(nc.sbuf_tensor("mnn", [128, 1], F32))
        vr = _st6.enter_context(nc.sbuf_tensor("vr", [128, 1], F32))
        sd = _st6.enter_context(nc.sbuf_tensor("sd", [128, 1], F32))
        rs = _st6.enter_context(nc.sbuf_tensor("rs", [128, 1], F32))
        wp0 = _st6.enter_context(nc.psum_tensor("wp0", [128, 512], F32))
        wp1 = _st6.enter_context(nc.psum_tensor("wp1", [128, 512], F32))
        wp2 = _st6.enter_context(nc.psum_tensor("wp2", [128, 512], F32))
        wp3 = _st6.enter_context(nc.psum_tensor("wp3", [128, 512], F32))
        wp4 = _st6.enter_context(nc.psum_tensor("wp4", [128, 512], F32))
        wp5 = _st6.enter_context(nc.psum_tensor("wp5", [128, 512], F32))
        wp6 = _st6.enter_context(nc.psum_tensor("wp6", [128, 512], F32))
        wp7 = _st6.enter_context(nc.psum_tensor("wp7", [128, 512], F32))
        if True:
            aoc = (aoc0, aoc1, aoc2)
            woc = (woc0, woc1, woc2)
            xr = (xr0, xr1, xr2, xr3)
            tw = (tw0, tw1)
            sqd = (sqd0, sqd1)
            ws = (ws0, ws1)
            wp = (wp0, wp1, wp2, wp3, wp4, wp5, wp6, wp7)
            p = Ph(nc, "wo")
            p.op("vector", "memset", sacc[:, :], 0.0)
            p.op("vector", "memset", qacc[:, :], 0.0)
            aod_done = ([(S["s_aod"][0], S["s_aod"][0].v)] +
                        [(S["s_aod"][1], S["s_aod"][1].v)])
            for n in range(NO):
                for k in range(KC):
                    c = n * KC + k
                    p.op("sync", "dma_start", aoc[c % 3][:, :],
                         ao_scr[:, k * 128:(k + 1) * 128], transpose=True,
                         waits=[(S["t_wmm"], c - 2)] +
                               (aod_done if c == 0 else []),
                         inc=S["d_ao"][c % 3], inck=16)
                    aov = (S["d_ao"][c % 3], S["d_ao"][c % 3].v)
                    p.op("sync", "dma_start", woc[c % 3][:, :],
                         wo_bf[k * 128:(k + 1) * 128, n * 512:(n + 1) * 512],
                         waits=[(S["t_wmm"], c - 2)] +
                               ([(S["d_bwo"], 16)] if c == 0 else []),
                         inc=S["d_wo"][c % 3], inck=16)
                    wov = (S["d_wo"][c % 3], S["d_wo"][c % 3].v)
                    for t in range(NO):
                        w = []
                        if t == 0:
                            w = [aov, wov]
                        if k == 0:
                            w.append((S["v_wadd"], (n - 1) * 8 + t + 1 if n >= 1 else 0))
                        p.op("tensor", "matmul", wp[t][:, :],
                             aoc[c % 3][:, t * 128:(t + 1) * 128], woc[c % 3][:, :],
                             start=(k == 0), stop=(k == KC - 1), waits=w,
                             inc=(S["t_wmm"] if t == NO - 1 else None))
                for t in range(NO):
                    e = n * 8 + t
                    p.op("scalar", "dma_start", xr[e % 4][:, :],
                         x_own[t * 128:(t + 1) * 128, n * 512:(n + 1) * 512],
                         waits=[(S["v_wadd"], e - 3)], inc=S["d_xr"][e % 4], inck=16)
                    xrv = (S["d_xr"][e % 4], S["d_xr"][e % 4].v)
                    p.op("vector", "tensor_tensor", tw[e % 2][:, :], wp[t][:, :],
                         xr[e % 4][:, :], ALU.add,
                         waits=[(S["t_wmm"], (n + 1) * KC), xrv,
                                (S["s_wst"], e - 1)], inc=S["v_wadd"])
                    if e >= 1:
                        p.op("vector", "tensor_tensor", sacc[:, (e - 1) % 8:(e - 1) % 8 + 1],
                             sacc[:, (e - 1) % 8:(e - 1) % 8 + 1], ws[(e - 1) % 2][:, 0:1],
                             ALU.add, waits=[(S["s_wst"], e)])
                        p.op("vector", "tensor_tensor", qacc[:, (e - 1) % 8:(e - 1) % 8 + 1],
                             qacc[:, (e - 1) % 8:(e - 1) % 8 + 1], ws[(e - 1) % 2][:, 1:2],
                             ALU.add, inc=S["v_wsta"])
                    p.op("scalar", "activation", r1sb[:, t * D + n * 512: t * D + (n + 1) * 512],
                         tw[e % 2][:, :], AF.Copy, bias=0.0, scale=1.0,
                         accum_out=ws[e % 2][:, 0:1],
                         waits=[(S["v_wadd"], e + 1), (S["v_wsta"], e - 1)])
                    p.op("scalar", "activation", sqd[e % 2][:, :], tw[e % 2][:, :],
                         AF.Square, bias=0.0, scale=1.0,
                         accum_out=ws[e % 2][:, 1:2], inc=S["s_wst"])
            e = NO * 8 - 1
            p.op("vector", "tensor_tensor", sacc[:, e % 8:e % 8 + 1],
                 sacc[:, e % 8:e % 8 + 1], ws[e % 2][:, 0:1], ALU.add,
                 waits=[(S["s_wst"], e + 1)])
            p.op("vector", "tensor_tensor", qacc[:, e % 8:e % 8 + 1],
                 qacc[:, e % 8:e % 8 + 1], ws[e % 2][:, 1:2], ALU.add,
                 inc=S["v_wsta"])
            for t in range(NO):
                p.op("vector", "tensor_scalar", mn[:, :], sacc[:, t:t + 1],
                     1.0 / D, None, ALU.mult,
                     waits=([(S["v_wsta"], NO * 8)] if t == 0 else
                            [(S["s_sqrt"], t)]))
                p.op("vector", "tensor_scalar", vr[:, :], qacc[:, t:t + 1],
                     1.0 / D, EPS, ALU.mult, ALU.add)
                p.op("vector", "tensor_tensor", sd[:, :], mn[:, :], mn[:, :],
                     ALU.mult)
                p.op("vector", "tensor_scalar", mnn[:, :], mn[:, :], -1.0, None,
                     ALU.mult)
                p.op("vector", "tensor_tensor", vr[:, :], vr[:, :], sd[:, :],
                     ALU.subtract, inc=S["v_var"])
                p.op("scalar", "activation", sd[:, :], vr[:, :], AF.Sqrt,
                     bias=0.0, scale=1.0, waits=[(S["v_var"], t + 1)],
                     inc=S["s_sqrt"])
                p.op("vector", "reciprocal", rs[:, :], sd[:, :],
                     waits=[(S["s_sqrt"], t + 1)])
                p.op("vector", "drain")
                p.op("vector", "tensor_scalar", r1sb[:, t * D:(t + 1) * D],
                     r1sb[:, t * D:(t + 1) * D], mnn[:, :], rs[:, :],
                     ALU.add, ALU.mult, inc=S["v_ln"])
                p.op("scalar", "dma_start", x1_scr[t * 128:(t + 1) * 128, :],
                     r1sb[:, t * D:(t + 1) * D], waits=[(S["v_ln"], t + 1)],
                     inc=S["s_x1"], inck=16)
            p.wait("scalar", S["s_x1"], S["s_x1"].v)
            for i in range(4):
                p.wait("scalar", S["d_xr"][i], S["d_xr"][i].v)
            for i in range(3):
                p.wait("sync", S["d_ao"][i], S["d_ao"][i].v)
                p.wait("sync", S["d_wo"][i], S["d_wo"][i].v)
            p.run()

        _st6.close()
        with nc.sbuf_tensor("x1t_sb", [128, KC * 1024], BF16) as x1t_sb:
            _st7 = ExitStack()
            c1b = _st7.enter_context(nc.sbuf_tensor("c1b", [128, KC * RANK], BF16))
            c2sb = _st7.enter_context(nc.sbuf_tensor("c2sb", [RANK, ITR], BF16))
            r16f = _st7.enter_context(nc.sbuf_tensor("r16f", [128, 8 * RANK], F32))
            r16t_sb = _st7.enter_context(nc.sbuf_tensor("r16t_sb", [RANK, 1024], BF16))
            tmpidx0 = _st7.enter_context(nc.sbuf_tensor("tmpidx0", [128, 256], F32))
            tmpidx1 = _st7.enter_context(nc.sbuf_tensor("tmpidx1", [128, 256], F32))
            bmx = _st7.enter_context(nc.sbuf_tensor("bmx", [128, 2], F32))
            r16ps = _st7.enter_context(nc.psum_tensor("r16ps", [128, 512], F32))
            ips0 = _st7.enter_context(nc.psum_tensor("ips0", [128, 512], F32))
            ips1 = _st7.enter_context(nc.psum_tensor("ips1", [128, 512], F32))
            r16tps = _st7.enter_context(nc.psum_tensor("r16tps", [RANK, 1024], F32))
            if True:
                ips = (ips0, ips1)
                p = Ph(nc, "router")
                x1_done = S["s_x1"].v
                for k in range(KC):
                    p.op("sync", "dma_start", x1t_sb[:, k * 1024:(k + 1) * 1024],
                         x1_scr[:, k * 128:(k + 1) * 128], transpose=True,
                         waits=([(S["s_x1"], x1_done)] if k == 0 else ()),
                         inc=S["d_x1t"], inck=16)
                p.op("sync", "dma_start", c1b[:, :], c1_bfs[:, :],
                     waits=[(S["d_castc"], 32)], inc=S["d_c1"], inck=16)
                p.op("sync", "dma_start", c2sb[:, :], c2_bfs[:, :],
                     inc=S["d_c2"], inck=16)
                for t in range(NO):
                    for k in range(KC):
                        w = []
                        if k == 0:
                            w = [(S["s_r16"], t), (S["d_x1t"], 16 * KC),
                                 (S["d_c1"], 16)]
                        p.op("tensor", "matmul", r16ps[:, 0:RANK],
                             x1t_sb[:, k * 1024 + t * 128: k * 1024 + (t + 1) * 128],
                             c1b[:, k * RANK:(k + 1) * RANK],
                             start=(k == 0), stop=(k == KC - 1), waits=w,
                             inc=(S["t_r16"] if k == KC - 1 else None))
                    p.op("scalar", "copy", r16f[:, t * RANK:(t + 1) * RANK],
                         r16ps[:, 0:RANK], waits=[(S["t_r16"], t + 1)],
                         inc=S["s_r16"])
                for t in range(NO):
                    p.op("tensor", "transpose", r16tps[:, t * 128:(t + 1) * 128],
                         r16f[:, t * RANK:(t + 1) * RANK], idf[:, :],
                         waits=([(S["s_r16"], NO)] if t == 0 else ()),
                         inc=(S["t_r16t"] if t == NO - 1 else None))
                p.op("scalar", "copy", r16t_sb[:, :], r16tps[:, :],
                     waits=[(S["t_r16t"], 1)], inc=S["s_r16t"])
                for r in range(NRS):
                    for t in range(NO):
                        u = r * 8 + t
                        p.op("tensor", "matmul", ips[u % 2][:, :],
                             r16t_sb[:, t * 128:(t + 1) * 128],
                             c2sb[:, r * 512:(r + 1) * 512],
                             start=True, stop=True,
                             waits=[(S["v_idx"], u - 1), (S["s_r16t"], 1),
                                    (S["d_c2"], 16)],
                             inc=S["t_imm"])
                        tmpidx = (tmpidx0, tmpidx1)
                        for b in range(2):
                            p.op("vector", "reduce_max", bmx[:, b:b + 1],
                                 ips[u % 2][:, b * 256:(b + 1) * 256], axis=AX.X,
                                 waits=([(S["t_imm"], u + 1)] if b == 0 else ()))
                        for b in range(2):
                            p.op("vector", "scalar_tensor_tensor", tmpidx[b][:, :],
                                 ips[u % 2][:, b * 256:(b + 1) * 256], bmx[:, b:b + 1],
                                 iota[:, :], ALU.is_ge, ALU.mult)
                        for b in range(2):
                            p.op("vector", "reduce_sum",
                                 idxs[:, t * 56 + r * 2 + b: t * 56 + r * 2 + b + 1],
                                 tmpidx[b][:, :], axis=AX.X,
                                 inc=(S["v_idx"] if b == 1 else None))
                p.wait("sync", S["d_x1t"], S["d_x1t"].v)
                p.wait("sync", S["d_c2"], S["d_c2"].v)
                p.wait("sync", S["d_c1"], S["d_c1"].v)
                p.run()

            _st7.close()
            _st8 = ExitStack()
            w1c0 = _st8.enter_context(nc.sbuf_tensor("w1c0", [128, 512], BF16))
            w1c1 = _st8.enter_context(nc.sbuf_tensor("w1c1", [128, 512], BF16))
            w1c2 = _st8.enter_context(nc.sbuf_tensor("w1c2", [128, 512], BF16))
            w1c3 = _st8.enter_context(nc.sbuf_tensor("w1c3", [128, 512], BF16))
            w1c4 = _st8.enter_context(nc.sbuf_tensor("w1c4", [128, 512], BF16))
            w1c5 = _st8.enter_context(nc.sbuf_tensor("w1c5", [128, 512], BF16))
            zr0 = _st8.enter_context(nc.sbuf_tensor("zr0", [128, 512], BF16))
            zr1 = _st8.enter_context(nc.sbuf_tensor("zr1", [128, 512], BF16))
            ssb0 = _st8.enter_context(nc.sbuf_tensor("ssb0", [128, 512], BF16))
            ssb1 = _st8.enter_context(nc.sbuf_tensor("ssb1", [128, 512], BF16))
            zp0 = _st8.enter_context(nc.psum_tensor("zp0", [128, 512], F32))
            zp1 = _st8.enter_context(nc.psum_tensor("zp1", [128, 512], F32))
            zp2 = _st8.enter_context(nc.psum_tensor("zp2", [128, 512], F32))
            zp3 = _st8.enter_context(nc.psum_tensor("zp3", [128, 512], F32))
            zp4 = _st8.enter_context(nc.psum_tensor("zp4", [128, 512], F32))
            zp5 = _st8.enter_context(nc.psum_tensor("zp5", [128, 512], F32))
            zp6 = _st8.enter_context(nc.psum_tensor("zp6", [128, 512], F32))
            zp7 = _st8.enter_context(nc.psum_tensor("zp7", [128, 512], F32))
            if True:
                w1c = (w1c0, w1c1, w1c2, w1c3, w1c4, w1c5)
                zr = (zr0, zr1)
                ssb = (ssb0, ssb1)
                zp = (zp0, zp1, zp2, zp3, zp4, zp5, zp6, zp7)
                p = Ph(nc, "zmm")
                for r in range(NRS):
                    for k in range(KC):
                        c = r * KC + k
                        p.op("gpsimd", "dma_start", w1c[c % 6][:, :],
                             w1T[k * 128:(k + 1) * 128, r * 512:(r + 1) * 512],
                             waits=[(S["t_zmm"], c - 5)],
                             inc=S["d_w1"][c % 6], inck=16)
                        w1v = (S["d_w1"][c % 6], S["d_w1"][c % 6].v)
                        for t in range(NO):
                            w = []
                            if t == 0:
                                w.append(w1v)
                            if k == 0:
                                w.append((S["s_zrl"], (r - 1) * 8 + t + 1 if r >= 1 else 0))
                            p.op("tensor", "matmul", zp[t][:, :],
                                 x1t_sb[:, k * 1024 + t * 128: k * 1024 + (t + 1) * 128],
                                 w1c[c % 6][:, :],
                                 start=(k == 0), stop=(k == KC - 1), waits=w,
                                 inc=(S["t_zmm"] if t == NO - 1 else None))
                    for t in range(NO):
                        e = r * 8 + t
                        p.op("scalar", "activation", zr[e % 2][:, :], zp[t][:, :],
                             AF.Relu, bias=0.0, scale=1.0,
                             waits=[(S["t_zmm"], r * KC + KC), (S["v_zmk"], e - 1)],
                             inc=S["s_zrl"])
                        for b in range(2):
                            p.op("vector", "scalar_tensor_tensor",
                                 ssb[e % 2][:, b * 256:(b + 1) * 256], iota[:, :],
                                 idxs[:, t * 56 + r * 2 + b: t * 56 + r * 2 + b + 1],
                                 zr[e % 2][:, b * 256:(b + 1) * 256],
                                 ALU.is_equal, ALU.mult,
                                 waits=([(S["s_zrl"], e + 1),
                                         (S["d_ss"][e % 2], S["d_ss"][e % 2].v)]
                                        if b == 0 else ()),
                                 inc=(S["v_zmk"] if b == 1 else None))
                        p.op("scalar", "dma_start",
                             s_scr[t * 128:(t + 1) * 128, r * 512:(r + 1) * 512],
                             ssb[e % 2][:, :], waits=[(S["v_zmk"], e + 1)],
                             inc=S["d_ss"][e % 2], inck=16)
                p.wait("scalar", S["d_ss"][0], S["d_ss"][0].v)
                p.wait("scalar", S["d_ss"][1], S["d_ss"][1].v)
                for i in range(6):
                    p.wait("gpsimd", S["d_w1"][i], S["d_w1"][i].v)
                p.run()

            _st8.close()
        _st9 = ExitStack()
        w2c0 = _st9.enter_context(nc.sbuf_tensor("w2c0", [128, 512], BF16))
        w2c1 = _st9.enter_context(nc.sbuf_tensor("w2c1", [128, 512], BF16))
        w2c2 = _st9.enter_context(nc.sbuf_tensor("w2c2", [128, 512], BF16))
        w2c3 = _st9.enter_context(nc.sbuf_tensor("w2c3", [128, 512], BF16))
        w2c4 = _st9.enter_context(nc.sbuf_tensor("w2c4", [128, 512], BF16))
        w2c5 = _st9.enter_context(nc.sbuf_tensor("w2c5", [128, 512], BF16))
        stc0 = _st9.enter_context(nc.sbuf_tensor("stc0", [128, 1024], BF16))
        stc1 = _st9.enter_context(nc.sbuf_tensor("stc1", [128, 1024], BF16))
        stc2 = _st9.enter_context(nc.sbuf_tensor("stc2", [128, 1024], BF16))
        stc3 = _st9.enter_context(nc.sbuf_tensor("stc3", [128, 1024], BF16))
        stc4 = _st9.enter_context(nc.sbuf_tensor("stc4", [128, 1024], BF16))
        stc5 = _st9.enter_context(nc.sbuf_tensor("stc5", [128, 1024], BF16))
        to0 = _st9.enter_context(nc.sbuf_tensor("to0", [128, 512], F32))
        to1 = _st9.enter_context(nc.sbuf_tensor("to1", [128, 512], F32))
        sqe0 = _st9.enter_context(nc.sbuf_tensor("sqe0", [128, 512], BF16))
        sqe1 = _st9.enter_context(nc.sbuf_tensor("sqe1", [128, 512], BF16))
        os0 = _st9.enter_context(nc.sbuf_tensor("os0", [128, 2], F32))
        os1 = _st9.enter_context(nc.sbuf_tensor("os1", [128, 2], F32))
        sac2 = _st9.enter_context(nc.sbuf_tensor("sac2", [128, 8], F32))
        qac2 = _st9.enter_context(nc.sbuf_tensor("qac2", [128, 8], F32))
        mn2 = _st9.enter_context(nc.sbuf_tensor("mn2", [128, 1], F32))
        mnn2 = _st9.enter_context(nc.sbuf_tensor("mnn2", [128, 1], F32))
        vr2 = _st9.enter_context(nc.sbuf_tensor("vr2", [128, 1], F32))
        sd2 = _st9.enter_context(nc.sbuf_tensor("sd2", [128, 1], F32))
        rs2 = _st9.enter_context(nc.sbuf_tensor("rs2", [128, 1], F32))
        outsb0 = _st9.enter_context(nc.sbuf_tensor("outsb0", [128, D], F32))
        outsb1 = _st9.enter_context(nc.sbuf_tensor("outsb1", [128, D], F32))
        op0 = _st9.enter_context(nc.psum_tensor("op0", [128, 512], F32))
        op1 = _st9.enter_context(nc.psum_tensor("op1", [128, 512], F32))
        op2 = _st9.enter_context(nc.psum_tensor("op2", [128, 512], F32))
        op3 = _st9.enter_context(nc.psum_tensor("op3", [128, 512], F32))
        op4 = _st9.enter_context(nc.psum_tensor("op4", [128, 512], F32))
        op5 = _st9.enter_context(nc.psum_tensor("op5", [128, 512], F32))
        op6 = _st9.enter_context(nc.psum_tensor("op6", [128, 512], F32))
        op7 = _st9.enter_context(nc.psum_tensor("op7", [128, 512], F32))
        if True:
            w2c = (w2c0, w2c1, w2c2, w2c3, w2c4, w2c5)
            stc = (stc0, stc1, stc2, stc3, stc4, stc5)
            to = (to0, to1)
            sqe = (sqe0, sqe1)
            osb = (os0, os1)
            outsb = (outsb0, outsb1)
            opb = (op0, op1, op2, op3, op4, op5, op6, op7)
            p = Ph(nc, "outmm")
            p.op("vector", "memset", sac2[:, :], 0.0)
            p.op("vector", "memset", qac2[:, :], 0.0)
            ss_done = [(S["d_ss"][i], S["d_ss"][i].v) for i in range(2)]
            for n in range(NO):
                for k in range(NKO):
                    c = n * NKO + k
                    p.op("gpsimd", "dma_start", w2c[c % 6][:, :],
                         w2t_in[k * 128:(k + 1) * 128, n * 512:(n + 1) * 512],
                         waits=[(S["t_omm"], c - 5)],
                         inc=S["d_w2"][c % 6], inck=16)
                    w2v = (S["d_w2"][c % 6], S["d_w2"][c % 6].v)
                    p.op("sync", "dma_start", stc[c % 6][:, :],
                         s_scr[:, k * 128:(k + 1) * 128], transpose=True,
                         waits=[(S["t_omm"], c - 5)] +
                               (ss_done if c == 0 else []),
                         inc=S["d_st"][c % 6], inck=16)
                    stv = (S["d_st"][c % 6], S["d_st"][c % 6].v)
                    for t in range(NO):
                        w = []
                        if t == 0:
                            w = [w2v, stv]
                        if k == 0:
                            w.append((S["v_oad"], (n - 1) * 8 + t + 1 if n >= 1 else 0))
                        p.op("tensor", "matmul", opb[t][:, :],
                             stc[c % 6][:, t * 128:(t + 1) * 128], w2c[c % 6][:, :],
                             start=(k == 0), stop=(k == NKO - 1), waits=w,
                             inc=(S["t_omm"] if t == NO - 1 else None))
                for t in range(NO):
                    e = n * 8 + t
                    p.op("vector", "tensor_tensor", to[e % 2][:, :], opb[t][:, :],
                         r1sb[:, t * D + n * 512: t * D + (n + 1) * 512], ALU.add,
                         waits=[(S["t_omm"], (n + 1) * NKO), (S["s_ost"], e - 1)],
                         inc=S["v_oad"])
                    if e >= 1:
                        p.op("vector", "tensor_tensor",
                             sac2[:, (e - 1) % 8:(e - 1) % 8 + 1],
                             sac2[:, (e - 1) % 8:(e - 1) % 8 + 1],
                             osb[(e - 1) % 2][:, 0:1], ALU.add,
                             waits=[(S["s_ost"], e)])
                        p.op("vector", "tensor_tensor",
                             qac2[:, (e - 1) % 8:(e - 1) % 8 + 1],
                             qac2[:, (e - 1) % 8:(e - 1) % 8 + 1],
                             osb[(e - 1) % 2][:, 1:2], ALU.add, inc=S["v_osta"])
                    p.op("scalar", "activation",
                         r1sb[:, t * D + n * 512: t * D + (n + 1) * 512],
                         to[e % 2][:, :], AF.Copy, bias=0.0, scale=1.0,
                         accum_out=osb[e % 2][:, 0:1],
                         waits=[(S["v_oad"], e + 1), (S["v_osta"], e - 1)])
                    p.op("scalar", "activation", sqe[e % 2][:, :], to[e % 2][:, :],
                         AF.Square, bias=0.0, scale=1.0,
                         accum_out=osb[e % 2][:, 1:2], inc=S["s_ost"])
            e = NO * 8 - 1
            p.op("vector", "tensor_tensor", sac2[:, e % 8:e % 8 + 1],
                 sac2[:, e % 8:e % 8 + 1], osb[e % 2][:, 0:1], ALU.add,
                 waits=[(S["s_ost"], e + 1)])
            p.op("vector", "tensor_tensor", qac2[:, e % 8:e % 8 + 1],
                 qac2[:, e % 8:e % 8 + 1], osb[e % 2][:, 1:2], ALU.add,
                 inc=S["v_osta"])
            for t in range(NO):
                p.op("vector", "tensor_scalar", mn2[:, :], sac2[:, t:t + 1],
                     1.0 / D, None, ALU.mult,
                     waits=([(S["v_osta"], NO * 8)] if t == 0 else
                            [(S["s_sqrt2"], t)]))
                p.op("vector", "tensor_scalar", vr2[:, :], qac2[:, t:t + 1],
                     1.0 / D, EPS, ALU.mult, ALU.add)
                p.op("vector", "tensor_tensor", sd2[:, :], mn2[:, :], mn2[:, :],
                     ALU.mult)
                p.op("vector", "tensor_scalar", mnn2[:, :], mn2[:, :], -1.0, None,
                     ALU.mult)
                p.op("vector", "tensor_tensor", vr2[:, :], vr2[:, :], sd2[:, :],
                     ALU.subtract, inc=S["v_var2"])
                p.op("scalar", "activation", sd2[:, :], vr2[:, :], AF.Sqrt,
                     bias=0.0, scale=1.0, waits=[(S["v_var2"], t + 1)],
                     inc=S["s_sqrt2"])
                p.op("vector", "reciprocal", rs2[:, :], sd2[:, :],
                     waits=[(S["s_sqrt2"], t + 1)])
                p.op("vector", "drain")
                p.op("vector", "tensor_scalar", outsb[t % 2][:, :],
                     r1sb[:, t * D:(t + 1) * D], mnn2[:, :], rs2[:, :],
                     ALU.add, ALU.mult,
                     waits=[(S["d_out"][t % 2], S["d_out"][t % 2].v)],
                     inc=S["v_ln2"])
                p.op("scalar", "dma_start", out[t * 128:(t + 1) * 128, :],
                     outsb[t % 2][:, :], waits=[(S["v_ln2"], t + 1)],
                     inc=S["d_out"][t % 2], inck=16)
            for i in range(6):
                p.wait("gpsimd", S["d_w2"][i], S["d_w2"][i].v)
                p.wait("sync", S["d_st"][i], S["d_st"][i].v)
            p.wait("scalar", S["d_out"][0], S["d_out"][0].v)
            p.wait("scalar", S["d_out"][1], S["d_out"][1].v)
            p.wait("gpsimd", S["d_bwo"], S["d_bwo"].v)
            p.wait("gpsimd", S["d_bw1"], S["d_bw1"].v)
            p.wait("gpsimd", S["d_bw2"], S["d_bw2"].v)
            p.run()

        _st9.close()
    es.close()
    return nc


def _host_tables():
    rope_theta, rope_factor, high_ff, low_ff, orig_max = 500000.0, 8.0, 4.0, 1.0, 8192
    inv = 1.0 / rope_theta ** (np.arange(0, HD, 2, dtype=np.float32) / HD)
    wavelen = 2.0 * np.pi / inv
    low_wl = orig_max / low_ff
    high_wl = orig_max / high_ff
    inv_scaled = np.where(wavelen > low_wl, inv / rope_factor, inv)
    smooth = (orig_max / wavelen - low_ff) / (high_ff - low_ff)
    smoothed = (1.0 - smooth) / rope_factor * inv + smooth * inv
    is_medium = (wavelen <= low_wl) & (wavelen >= high_wl)
    inv_freq = np.where(is_medium, smoothed, inv_scaled).astype(np.float32)
    pos = np.arange(T, dtype=np.float32)
    freqs = pos[:, None] * inv_freq[None, :]
    emb = np.concatenate([freqs, freqs], axis=-1)
    cos_t = np.ascontiguousarray(np.cos(emb).T.astype(np.float32))
    sin_t = np.ascontiguousarray(np.sin(emb).T.astype(np.float32))
    return cos_t, sin_t


def kernel(L0_attn_x, L0_attn_Wq, L0_attn_Wk, L0_attn_Wv, L0_attn_Wo,
           L0_ffn_c1_w, L0_ffn_c2_w, L0_ffn_w1, L0_ffn_w2t):
    x = np.ascontiguousarray(np.asarray(L0_attn_x, dtype=np.float32))
    Wq = np.asarray(L0_attn_Wq, dtype=np.float32)
    Wk = np.asarray(L0_attn_Wk, dtype=np.float32)
    Wv = np.asarray(L0_attn_Wv, dtype=np.float32)
    Wo = np.asarray(L0_attn_Wo, dtype=np.float32)
    c1 = np.asarray(L0_ffn_c1_w, dtype=np.float32)
    c2 = np.asarray(L0_ffn_c2_w, dtype=np.float32)
    w1 = np.asarray(L0_ffn_w1, dtype=np.float32)
    w2t = np.ascontiguousarray(np.asarray(L0_ffn_w2t, dtype=np.float32))

    wqT = Wq.T
    wq_t = np.ascontiguousarray(
        wqT.reshape(KC, 128, NQH, 128).transpose(2, 1, 0, 3).reshape(NQH * 128, D))
    wk_img = np.ascontiguousarray(
        Wk.T.reshape(KC, 128, NKV, 128).transpose(1, 0, 2, 3).reshape(128, KC * NKV * 128))
    wv_img = np.ascontiguousarray(
        Wv.T.reshape(KC, 128, NKV, 128).transpose(1, 0, 2, 3).reshape(128, KC * NKV * 128))
    woT = np.ascontiguousarray(Wo.T)
    w1T = np.ascontiguousarray(w1.T)
    c1_img = np.ascontiguousarray(
        c1.T.reshape(KC, 128, RANK).transpose(1, 0, 2).reshape(128, KC * RANK))
    c2T = np.ascontiguousarray(c2.T)

    cos_t, sin_t = _host_tables()
    scale = 1.0 / np.sqrt(np.float32(HD))

    idb = np.eye(128, dtype=ml_dtypes.bfloat16)
    idf = np.eye(128, dtype=np.float32)
    R = np.zeros((128, 128), dtype=np.float32)
    R[np.arange(64), np.arange(64) + 64] = -1.0
    R[np.arange(64) + 64, np.arange(64)] = 1.0
    rotT = np.ascontiguousarray(R.T)
    iota256 = np.tile(np.arange(256, dtype=np.float32), (128, 1))

    halves = []
    for hh in range(2):
        pos_own = np.concatenate([np.arange(t0 * 128, (t0 + 1) * 128)
                                  for t0 in QT[hh]])
        cosq = np.ascontiguousarray(cos_t[:, pos_own] * scale)
        sinq = np.ascontiguousarray(sin_t[:, pos_own] * scale)
        cm = np.zeros((128, NO * 256), dtype=np.float32)
        for o in range(NO):
            i = QT[hh][o]
            lp = LP[o]
            qpos = i * 128 + np.arange(128)[:, None]
            kpos = lp - 256 + np.arange(256)[None, :]
            cm[:, o * 256:(o + 1) * 256] = np.where(kpos <= qpos, 0.0, NEG)
        halves.append((cosq, sinq, cm))

    nc = build_nc()
    in_maps = []
    for core in range(8):
        b, hh = core // 2, core % 2
        cosq, sinq, cm = halves[hh]
        xb = np.ascontiguousarray(x[b])
        x_own = np.ascontiguousarray(
            np.concatenate([x[b, t0 * 128:(t0 + 1) * 128] for t0 in QT[hh]], axis=0))
        in_maps.append({
            "xb": xb, "x_own": x_own, "wq_t": wq_t, "wk_img": wk_img,
            "wv_img": wv_img, "woT": woT, "w1T": w1T, "w2t_in": w2t,
            "c1_img": c1_img, "c2T": c2T, "cosk": cos_t, "sink": sin_t,
            "cosq": cosq, "sinq": sinq, "cmask": cm, "idb_in": idb,
            "idf_in": idf, "rotT_in": rotT, "iota_in": iota256,
        })
    res = run_bass_kernel_spmd(nc, in_maps, core_ids=list(range(8)))
    outp = np.empty((4, T, D), dtype=np.float32)
    for core in range(8):
        b, hh = core // 2, core % 2
        oc = res.results[core]["out"]
        for o in range(NO):
            t0 = QT[hh][o]
            outp[b, t0 * 128:(t0 + 1) * 128] = oc[o * 128:(o + 1) * 128]
    return outp


# version 12
# speedup vs baseline: 1.0255x; 1.0255x over previous
import numpy as np
import ml_dtypes
from contextlib import ExitStack

import concourse.bass as bass
import concourse.mybir as mybir
from concourse.bass_utils import run_bass_kernel_spmd

F32 = mybir.dt.float32
BF16 = mybir.dt.bfloat16
AF = mybir.ActivationFunctionType
ALU = mybir.AluOpType
AX = mybir.AxisListType

T, D, HD, NKV, NQH = 2048, 4096, 128, 8, 32
ITR, RANK = 14336, 16
KC = D // 128
NO = 8
NTOK = NO * 128
QT = ([0, 3, 4, 7, 8, 11, 12, 15], [1, 2, 5, 6, 9, 10, 13, 14])
LP = [256 * (o + 1) for o in range(NO)]
NEG = -1.0e30
EPS = 1e-5
NRS = ITR // 512
NKO = ITR // 128


PHASES = []


class Sem:
    def __init__(self, h, name=""):
        self.h = h
        self.name = name
        self.v = 0


class Ph:

    def __init__(self, nc, name, no_gpsimd_drain=False):
        self.nc = nc
        self.name = name
        self.no_gpsimd_drain = no_gpsimd_drain
        self.q = {k: [] for k in ("tensor", "vector", "scalar", "gpsimd", "sync")}
        self.meta = {k: [] for k in self.q}

    def op(self, eng, meth, *args, inc=None, inck=1, waits=(), **kw):
        wts = tuple((s.h, v) for (s, v) in waits if v > 0)

        def f(e, meth=meth, args=args, kw=kw, inc=inc, inck=inck, wts=wts):
            for (h, v) in wts:
                e.wait_ge(h, v)
            r = getattr(e, meth)(*args, **kw)
            if inc is not None:
                r.then_inc(inc.h, inck)

        self.q[eng].append(f)
        self.meta[eng].append((tuple((s.name, v) for (s, v) in waits if v > 0),
                               inc.name if inc is not None else None, inck, meth))
        if inc is not None:
            inc.v += inck

    def wait(self, eng, sem, v):
        if v > 0:
            self.q[eng].append(lambda e, h=sem.h, v=v: e.wait_ge(h, v))
            self.meta[eng].append((((sem.name, v),), None, 0, "wait"))

    def run(self):
        PHASES.append((self.name, self.meta))
        with self.nc.Block(name=self.name,
                           no_gpsimd_drain=self.no_gpsimd_drain) as blk:
            for en in ("sync", "gpsimd", "tensor", "vector", "scalar"):
                fns = self.q[en]
                if fns:
                    def body(e, fns=fns):
                        for fn in fns:
                            fn(e)
                    getattr(blk, en)(body)


def build_nc():
    nc = bass.Bass("TRN2", target_bir_lowering=False)

    xb = nc.dram_tensor("xb", [T, D], F32, kind="ExternalInput")
    x_own = nc.dram_tensor("x_own", [NTOK, D], F32, kind="ExternalInput")
    wq_t = nc.dram_tensor("wq_t", [NQH * 128, D], F32, kind="ExternalInput")
    wk_img = nc.dram_tensor("wk_img", [128, KC * NKV * 128], F32, kind="ExternalInput")
    wv_img = nc.dram_tensor("wv_img", [128, KC * NKV * 128], F32, kind="ExternalInput")
    woT = nc.dram_tensor("woT", [D, D], F32, kind="ExternalInput")
    w1T = nc.dram_tensor("w1T", [D, ITR], F32, kind="ExternalInput")
    w2t_in = nc.dram_tensor("w2t_in", [ITR, D], F32, kind="ExternalInput")
    c1_img = nc.dram_tensor("c1_img", [128, KC * RANK], F32, kind="ExternalInput")
    c2T = nc.dram_tensor("c2T", [RANK, ITR], F32, kind="ExternalInput")
    cosk = nc.dram_tensor("cosk", [128, T], F32, kind="ExternalInput")
    sink = nc.dram_tensor("sink", [128, T], F32, kind="ExternalInput")
    cosq = nc.dram_tensor("cosq", [128, NTOK], F32, kind="ExternalInput")
    sinq = nc.dram_tensor("sinq", [128, NTOK], F32, kind="ExternalInput")
    cmask = nc.dram_tensor("cmask", [128, NO * 512], F32, kind="ExternalInput")
    idb_in = nc.dram_tensor("idb_in", [128, 128], BF16, kind="ExternalInput")
    idf_in = nc.dram_tensor("idf_in", [128, 128], F32, kind="ExternalInput")
    rotT_in = nc.dram_tensor("rotT_in", [128, 128], F32, kind="ExternalInput")
    iota_in = nc.dram_tensor("iota_in", [128, 256], F32, kind="ExternalInput")
    out = nc.dram_tensor("out", [NTOK, D], F32, kind="ExternalOutput")
    x_bf = nc.dram_tensor("x_bf", [T, D], BF16)
    xo_bf = nc.dram_tensor("xo_bf", [NTOK, D], BF16)
    k_scr = nc.dram_tensor("k_scr", [NKV * 128, T], BF16)
    v_scr = nc.dram_tensor("v_scr", [T, NKV * 128], BF16)
    q_scr = nc.dram_tensor("q_scr", [NO * 128, NQH * 128], BF16)
    ao_scr = nc.dram_tensor("ao_scr", [NTOK, D], BF16)
    x1_scr = nc.dram_tensor("x1_scr", [NTOK, D], BF16)
    s_scr = nc.dram_tensor("s_scr", [NTOK, ITR], BF16)
    c1_bfs = nc.dram_tensor("c1_bfs", [128, KC * RANK], BF16)
    c2_bfs = nc.dram_tensor("c2_bfs", [RANK, ITR], BF16)
    wo_bf = nc.dram_tensor("wo_bf", [D, D], BF16)
    w1_bf = nc.dram_tensor("w1_bf", [D, ITR], BF16)
    w2_bf = nc.dram_tensor("w2_bf", [ITR, D], BF16)

    es = ExitStack()
    sem = {}
    for name in ("d_const d_cast d_castc d_wk d_xt d_wv d_xo t_kmm s_kev t_krot "
                 "v_krope s_kst0 s_kst1 t_vmm s_vev t_qmm s_qev t_qrot v_qrope "
                 "d_k0 d_k1 d_v0 d_v1 d_q0 d_q1 t_smm v_msk s_scp v_ngm s_exp "
                 "v_rcp t_pt s_pte t_av v_aoe "
                 "d_ao0 d_ao1 d_ao2 d_wo0 d_wo1 d_wo2 d_xr0 d_xr1 d_xr2 d_xr3 "
                 "t_wmm v_wadd s_wst v_wsta v_var s_sqrt v_ln s_x1 "
                 "d_x1t d_c1 d_c2 t_r16 s_r16 t_r16t s_r16t t_imm v_idx "
                 "d_w10 d_w11 d_w12 d_w13 d_w14 d_w15 t_zmm s_zrl v_zmk "
                 "d_bwo d_bw1 d_bw2 "
                 "t_omm v_oad s_ost v_osta v_var2 s_sqrt2 v_ln2"
                 ).split():
        sem[name] = Sem(es.enter_context(nc.semaphore(name)), name)
    for base, n in (("s_kst", 2), ("d_k", 2), ("d_v", 2), ("d_q", 2),
                    ("d_ao", 3), ("d_wo", 3), ("d_xr", 4), ("d_w1", 6)):
        sem[base] = [sem[f"{base}{i}"] for i in range(n)]
    for alias in ("s_vst", "s_qst", "s_aod", "d_ss", "d_out"):
        sem[alias] = sem["s_kst"]
    sem["d_st"] = sem["d_ao"] + sem["d_wo"]
    sem["d_w2"] = sem["d_w1"]
    sem["d_wq"] = sem["d_w1"][0:2]

    idb = es.enter_context(nc.sbuf_tensor("idb", [128, 128], BF16))
    idf = es.enter_context(nc.sbuf_tensor("idf", [128, 128], F32))
    rotT = es.enter_context(nc.sbuf_tensor("rotT", [128, 128], F32))
    iota = es.enter_context(nc.sbuf_tensor("iota", [128, 256], F32))
    cmk = es.enter_context(nc.sbuf_tensor("cmk", [128, NO * 512], F32))
    idxs = es.enter_context(nc.sbuf_tensor("idxs", [128, NO * 56], F32))

    p = Ph(nc, "init")
    S = sem
    for dst, src in ((idb, idb_in), (idf, idf_in), (rotT, rotT_in),
                     (iota, iota_in), (cmk, cmask)):
        p.op("sync", "dma_start", dst[:, :], src[:, :], inc=S["d_const"], inck=16)
    p.op("gpsimd", "dma_start", x_bf[:, :], xb[:, :], inc=S["d_cast"], inck=16)
    p.op("gpsimd", "dma_start", xo_bf[:, :], x_own[:, :], inc=S["d_cast"], inck=16)
    p.op("gpsimd", "dma_start", c1_bfs[:, :], c1_img[:, :], inc=S["d_castc"], inck=16)
    p.op("gpsimd", "dma_start", c2_bfs[:, :], c2T[:, :], inc=S["d_castc"], inck=16)
    p.wait("sync", S["d_const"], S["d_const"].v)
    p.wait("gpsimd", S["d_cast"], S["d_cast"].v)
    p.wait("gpsimd", S["d_castc"], S["d_castc"].v)
    p.run()

    _st1 = ExitStack()
    xt_sb = _st1.enter_context(nc.sbuf_tensor("xt_sb", [128, KC * 1024], BF16))
    cosk_sb = _st1.enter_context(nc.sbuf_tensor("cosk_sb", [128, T], F32))
    sink_sb = _st1.enter_context(nc.sbuf_tensor("sink_sb", [128, T], F32))
    if True:

        _st2 = ExitStack()
        wk_sb = _st2.enter_context(nc.sbuf_tensor("wk_sb", [128, KC * NKV * 128], BF16))
        kraw0 = _st2.enter_context(nc.sbuf_tensor("kraw0", [128, 512], F32))
        kraw1 = _st2.enter_context(nc.sbuf_tensor("kraw1", [128, 512], F32))
        ktmp1 = _st2.enter_context(nc.sbuf_tensor("ktmp1", [128, 512], F32))
        ktmp2 = _st2.enter_context(nc.sbuf_tensor("ktmp2", [128, 512], F32))
        kbf0 = _st2.enter_context(nc.sbuf_tensor("kbf0", [128, 512], BF16))
        kbf1 = _st2.enter_context(nc.sbuf_tensor("kbf1", [128, 512], BF16))
        kps0 = _st2.enter_context(nc.psum_tensor("kps0", [128, 512], F32))
        kps1 = _st2.enter_context(nc.psum_tensor("kps1", [128, 512], F32))
        krt0 = _st2.enter_context(nc.psum_tensor("krt0", [128, 512], F32))
        krt1 = _st2.enter_context(nc.psum_tensor("krt1", [128, 512], F32))
        if True:
            kraw = (kraw0, kraw1)
            kbf = (kbf0, kbf1)
            kps = (kps0, kps1)
            krt = (krt0, krt1)
            p = Ph(nc, "kproj")
            p.op("sync", "dma_start", cosk_sb[:, :], cosk[:, :], inc=S["d_const"], inck=16)
            p.op("sync", "dma_start", sink_sb[:, :], sink[:, :], inc=S["d_const"], inck=16)
            cos_base = S["d_const"].v
            p.op("gpsimd", "dma_start", wk_sb[:, :], wk_img[:, :],
                 waits=[(S["d_cast"], 32)], inc=S["d_wk"], inck=16)

            units = [(half, h, tc) for half in range(2) for h in range(NKV)
                     for tc in range(2)]
            for half in range(2):
                for k in range(KC):
                    w = []
                    if half == 1 and k == 0:
                        w.append((S["t_kmm"], 16))
                    if half == 0 and k == 0:
                        w.append((S["d_cast"], 32))
                    p.op("sync", "dma_start", xt_sb[:, k * 1024:(k + 1) * 1024],
                         x_bf[half * 1024:(half + 1) * 1024, k * 128:(k + 1) * 128],
                         transpose=True, waits=w, inc=S["d_xt"], inck=16)
            for u, (half, h, tc) in enumerate(units):
                gpos = half * 1024 + tc * 512
                w = [(S["s_kev"], u - 1)]
                if u == 0 or (half == 1 and h == 0 and tc == 0):
                    w.append((S["d_xt"], 16 * 32 * (half + 1)))
                    w.append((S["d_wk"], 16))
                for k in range(KC):
                    p.op("tensor", "matmul", kps[u % 2][:, :],
                         wk_sb[:, k * 1024 + h * 128: k * 1024 + (h + 1) * 128],
                         xt_sb[:, k * 1024 + tc * 512: k * 1024 + tc * 512 + 512],
                         start=(k == 0), stop=(k == KC - 1),
                         waits=(w if k == 0 else ()),
                         inc=(S["t_kmm"] if k == KC - 1 else None))
                if u >= 1:
                    p.op("tensor", "matmul", krt[(u - 1) % 2][:, :], rotT[:, :],
                         kraw[(u - 1) % 2][:, :], start=True, stop=True,
                         waits=[(S["s_kev"], u), (S["v_krope"], u - 2)],
                         inc=S["t_krot"])
                p.op("scalar", "copy", kraw[u % 2][:, :], kps[u % 2][:, :],
                     waits=[(S["t_kmm"], u + 1), (S["t_krot"], u - 1),
                            (S["v_krope"], u - 1)], inc=S["s_kev"])
                if u >= 1:
                    v = u - 1
                    halfv, hv, tcv = units[v]
                    gposv = halfv * 1024 + tcv * 512
                    p.op("vector", "tensor_tensor", ktmp1[:, :], kraw[v % 2][:, :],
                         cosk_sb[:, gposv:gposv + 512], ALU.mult,
                         waits=[(S["s_kev"], v + 1), (S["t_krot"], v + 1),
                                (S["d_const"], cos_base)])
                    p.op("vector", "tensor_tensor", ktmp2[:, :], krt[v % 2][:, :],
                         sink_sb[:, gposv:gposv + 512], ALU.mult)
                    p.op("vector", "tensor_tensor", kbf[v % 2][:, :], ktmp1[:, :],
                         ktmp2[:, :], ALU.add,
                         waits=[(S["s_kst"][v % 2], S["s_kst"][v % 2].v)],
                         inc=S["v_krope"])
                    p.op("scalar", "dma_start",
                         k_scr[hv * 128:(hv + 1) * 128, gposv:gposv + 512],
                         kbf[v % 2][:, :],
                         waits=[(S["v_krope"], v + 1)], inc=S["s_kst"][v % 2], inck=16)
            v = len(units) - 1
            halfv, hv, tcv = units[v]
            gposv = halfv * 1024 + tcv * 512
            p.op("tensor", "matmul", krt[v % 2][:, :], rotT[:, :],
                 kraw[v % 2][:, :], start=True, stop=True,
                 waits=[(S["s_kev"], v + 1), (S["v_krope"], v - 1)],
                 inc=S["t_krot"])
            p.op("vector", "tensor_tensor", ktmp1[:, :], kraw[v % 2][:, :],
                 cosk_sb[:, gposv:gposv + 512], ALU.mult,
                 waits=[(S["t_krot"], v + 1)])
            p.op("vector", "tensor_tensor", ktmp2[:, :], krt[v % 2][:, :],
                 sink_sb[:, gposv:gposv + 512], ALU.mult)
            p.op("vector", "tensor_tensor", kbf[v % 2][:, :], ktmp1[:, :],
                 ktmp2[:, :], ALU.add, inc=S["v_krope"])
            p.op("scalar", "dma_start",
                 k_scr[hv * 128:(hv + 1) * 128, gposv:gposv + 512],
                 kbf[v % 2][:, :], waits=[(S["v_krope"], v + 1)],
                 inc=S["s_kst"][v % 2], inck=16)
            p.wait("scalar", S["s_kst"][0], S["s_kst"][0].v)
            p.wait("scalar", S["s_kst"][1], S["s_kst"][1].v)
            p.wait("sync", S["d_xt"], S["d_xt"].v)
            p.wait("gpsimd", S["d_wk"], S["d_wk"].v)
            p.run()

        _st2.close()
        _st3 = ExitStack()
        wv_sb = _st3.enter_context(nc.sbuf_tensor("wv_sb", [128, KC * NKV * 128], BF16))
        vst0 = _st3.enter_context(nc.sbuf_tensor("vst0", [128, 1024], BF16))
        vst1 = _st3.enter_context(nc.sbuf_tensor("vst1", [128, 1024], BF16))
        vps0 = _st3.enter_context(nc.psum_tensor("vps0", [128, 512], F32))
        vps1 = _st3.enter_context(nc.psum_tensor("vps1", [128, 512], F32))
        if True:
            vst = (vst0, vst1)
            vps = (vps0, vps1)
            p = Ph(nc, "vproj")
            p.op("gpsimd", "dma_start", wv_sb[:, :], wv_img[:, :],
                 inc=S["d_wv"], inck=16)
            xt_base = S["d_xt"].v
            for half in range(2):
                for k in range(KC):
                    w = []
                    if half == 1 and k == 0:
                        w.append((S["t_vmm"], 16))
                    p.op("sync", "dma_start", xt_sb[:, k * 1024:(k + 1) * 1024],
                         x_bf[half * 1024:(half + 1) * 1024, k * 128:(k + 1) * 128],
                         transpose=True, waits=w, inc=S["d_xt"], inck=16)
            units = [(half, tl, hg) for half in range(2) for tl in range(8)
                     for hg in range(2)]
            for u, (half, tl, hg) in enumerate(units):
                ttg = half * 8 + tl
                w = [(S["s_vev"], u - 1)]
                if tl == 0 and hg == 0:
                    w.append((S["d_xt"], xt_base + 16 * 32 * (half + 1)))
                    w.append((S["d_wv"], 16))
                for k in range(KC):
                    p.op("tensor", "matmul", vps[u % 2][:, :],
                         xt_sb[:, k * 1024 + tl * 128: k * 1024 + (tl + 1) * 128],
                         wv_sb[:, k * 1024 + hg * 512: k * 1024 + hg * 512 + 512],
                         start=(k == 0), stop=(k == KC - 1),
                         waits=(w if k == 0 else ()),
                         inc=(S["t_vmm"] if k == KC - 1 else None))
                p.op("scalar", "copy", vst[ttg % 2][:, hg * 512:hg * 512 + 512],
                     vps[u % 2][:, :],
                     waits=[(S["t_vmm"], u + 1),
                            (S["s_vst"][ttg % 2], S["s_vst"][ttg % 2].v)],
                     inc=S["s_vev"])
                if hg == 1:
                    p.op("scalar", "dma_start",
                         v_scr[ttg * 128:(ttg + 1) * 128, :], vst[ttg % 2][:, :],
                         inc=S["s_vst"][ttg % 2], inck=16)
            p.wait("scalar", S["s_vst"][0], S["s_vst"][0].v)
            p.wait("scalar", S["s_vst"][1], S["s_vst"][1].v)
            p.wait("sync", S["d_xt"], S["d_xt"].v)
            p.wait("gpsimd", S["d_wv"], S["d_wv"].v)
            p.run()

        _st3.close()
    _st1.close()
    _st4 = ExitStack()
    xTown = _st4.enter_context(nc.sbuf_tensor("xTown", [128, KC * 1024], BF16))
    cosq_sb = _st4.enter_context(nc.sbuf_tensor("cosq_sb", [128, NTOK], F32))
    sinq_sb = _st4.enter_context(nc.sbuf_tensor("sinq_sb", [128, NTOK], F32))
    wq_sb0 = _st4.enter_context(nc.sbuf_tensor("wq_sb0", [128, D], BF16))
    wq_sb1 = _st4.enter_context(nc.sbuf_tensor("wq_sb1", [128, D], BF16))
    qraw0 = _st4.enter_context(nc.sbuf_tensor("qraw0", [128, 512], F32))
    qraw1 = _st4.enter_context(nc.sbuf_tensor("qraw1", [128, 512], F32))
    qtmp1 = _st4.enter_context(nc.sbuf_tensor("qtmp1", [128, 512], F32))
    qtmp2 = _st4.enter_context(nc.sbuf_tensor("qtmp2", [128, 512], F32))
    qbf0 = _st4.enter_context(nc.sbuf_tensor("qbf0", [128, 512], BF16))
    qbf1 = _st4.enter_context(nc.sbuf_tensor("qbf1", [128, 512], BF16))
    qps0 = _st4.enter_context(nc.psum_tensor("qps0", [128, 512], F32))
    qps1 = _st4.enter_context(nc.psum_tensor("qps1", [128, 512], F32))
    qrt0 = _st4.enter_context(nc.psum_tensor("qrt0", [128, 512], F32))
    qrt1 = _st4.enter_context(nc.psum_tensor("qrt1", [128, 512], F32))
    if True:
        wq_sb = (wq_sb0, wq_sb1)
        qraw = (qraw0, qraw1)
        qbf = (qbf0, qbf1)
        qps = (qps0, qps1)
        qrt = (qrt0, qrt1)
        p = Ph(nc, "qproj")
        p.op("sync", "dma_start", cosq_sb[:, :], cosq[:, :], inc=S["d_const"], inck=16)
        p.op("sync", "dma_start", sinq_sb[:, :], sinq[:, :], inc=S["d_const"], inck=16)
        for k in range(KC):
            p.op("sync", "dma_start", xTown[:, k * 1024:(k + 1) * 1024],
                 xo_bf[:, k * 128:(k + 1) * 128], transpose=True,
                 waits=([(S["d_cast"], 32)] if k == 0 else ()),
                 inc=S["d_xo"], inck=16)
        wqv = []
        for head in range(NQH):
            p.op("gpsimd", "dma_start", wq_sb[head % 2][:, :],
                 wq_t[head * 128:(head + 1) * 128, :],
                 waits=[(S["t_qmm"], 2 * (head - 1))],
                 inc=S["d_wq"][head % 2], inck=16)
            wqv.append(S["d_wq"][head % 2].v)
        cq_base = S["d_const"].v

        def emit_qrope(v):
            headv, tgv = divmod(v, 2)
            p.op("vector", "tensor_tensor", qtmp1[:, :], qraw[v % 2][:, :],
                 cosq_sb[:, tgv * 512:tgv * 512 + 512], ALU.mult,
                 waits=[(S["t_qrot"], v + 1), (S["d_const"], cq_base)])
            p.op("vector", "tensor_tensor", qtmp2[:, :], qrt[v % 2][:, :],
                 sinq_sb[:, tgv * 512:tgv * 512 + 512], ALU.mult)
            p.op("vector", "tensor_tensor", qbf[v % 2][:, :], qtmp1[:, :],
                 qtmp2[:, :], ALU.add,
                 waits=[(S["s_qst"][v % 2], S["s_qst"][v % 2].v)],
                 inc=S["v_qrope"])
            for j in range(4):
                slot = tgv * 4 + j
                p.op("scalar", "dma_start",
                     q_scr[slot * 128:(slot + 1) * 128,
                           headv * 128:(headv + 1) * 128],
                     qbf[v % 2][:, j * 128:(j + 1) * 128],
                     waits=([(S["v_qrope"], v + 1)] if j == 0 else ()),
                     inc=S["s_qst"][v % 2], inck=16)

        for u in range(NQH * 2):
            head, tg = divmod(u, 2)
            w = [(S["s_qev"], u - 1)]
            if tg == 0:
                w.append((S["d_wq"][head % 2], wqv[head]))
            if u == 0:
                w.append((S["d_xo"], 16 * KC))
            for k in range(KC):
                p.op("tensor", "matmul", qps[u % 2][:, :],
                     wq_sb[head % 2][:, k * 128:(k + 1) * 128],
                     xTown[:, k * 1024 + tg * 512: k * 1024 + tg * 512 + 512],
                     start=(k == 0), stop=(k == KC - 1),
                     waits=(w if k == 0 else ()),
                     inc=(S["t_qmm"] if k == KC - 1 else None))
            if u >= 1:
                p.op("tensor", "matmul", qrt[(u - 1) % 2][:, :], rotT[:, :],
                     qraw[(u - 1) % 2][:, :], start=True, stop=True,
                     waits=[(S["s_qev"], u), (S["v_qrope"], u - 2)],
                     inc=S["t_qrot"])
            p.op("scalar", "copy", qraw[u % 2][:, :], qps[u % 2][:, :],
                 waits=[(S["t_qmm"], u + 1), (S["t_qrot"], u - 1),
                        (S["v_qrope"], u - 1)], inc=S["s_qev"])
            if u >= 1:
                emit_qrope(u - 1)
        v = NQH * 2 - 1
        p.op("tensor", "matmul", qrt[v % 2][:, :], rotT[:, :], qraw[v % 2][:, :],
             start=True, stop=True,
             waits=[(S["s_qev"], v + 1), (S["v_qrope"], v - 1)], inc=S["t_qrot"])
        emit_qrope(v)
        p.wait("scalar", S["s_qst"][0], S["s_qst"][0].v)
        p.wait("scalar", S["s_qst"][1], S["s_qst"][1].v)
        p.wait("sync", S["d_xo"], S["d_xo"].v)
        p.wait("gpsimd", S["d_wq"][0], S["d_wq"][0].v)
        p.wait("gpsimd", S["d_wq"][1], S["d_wq"][1].v)
        p.run()

    _st4.close()
    _st5 = ExitStack()
    ksb0 = _st5.enter_context(nc.sbuf_tensor("ksb0", [128, NKV * 2048], BF16))
    ksb1 = _st5.enter_context(nc.sbuf_tensor("ksb1", [128, NKV * 2048], BF16))
    vsb0 = _st5.enter_context(nc.sbuf_tensor("vsb0", [128, 16 * 1024], BF16))
    vsb1 = _st5.enter_context(nc.sbuf_tensor("vsb1", [128, 16 * 1024], BF16))
    qsb0 = _st5.enter_context(nc.sbuf_tensor("qsb0", [128, NQH * 128], BF16))
    qsb1 = _st5.enter_context(nc.sbuf_tensor("qsb1", [128, NQH * 128], BF16))
    Psb0 = _st5.enter_context(nc.sbuf_tensor("Psb0", [128, 2048], BF16))
    Psb1 = _st5.enter_context(nc.sbuf_tensor("Psb1", [128, 2048], BF16))
    Psb2 = _st5.enter_context(nc.sbuf_tensor("Psb2", [128, 2048], BF16))
    Ssb0 = _st5.enter_context(nc.sbuf_tensor("Ssb0", [128, 2048], F32))
    Ssb1 = _st5.enter_context(nc.sbuf_tensor("Ssb1", [128, 2048], F32))
    Ssb2 = _st5.enter_context(nc.sbuf_tensor("Ssb2", [128, 2048], F32))
    PTsb0 = _st5.enter_context(nc.sbuf_tensor("PTsb0", [128, 512], BF16))
    PTsb1 = _st5.enter_context(nc.sbuf_tensor("PTsb1", [128, 512], BF16))
    PTsb2 = _st5.enter_context(nc.sbuf_tensor("PTsb2", [128, 512], BF16))
    PTsb3 = _st5.enter_context(nc.sbuf_tensor("PTsb3", [128, 512], BF16))
    negm4 = _st5.enter_context(nc.sbuf_tensor("negm4", [128, 4], F32))
    lsum4 = _st5.enter_context(nc.sbuf_tensor("lsum4", [128, 4], F32))
    rinv0 = _st5.enter_context(nc.sbuf_tensor("rinv0", [128, 4], F32))
    rinv1 = _st5.enter_context(nc.sbuf_tensor("rinv1", [128, 4], F32))
    aoq0 = _st5.enter_context(nc.sbuf_tensor("aoq0", [128, 512], BF16))
    aoq1 = _st5.enter_context(nc.sbuf_tensor("aoq1", [128, 512], BF16))
    Sps = _st5.enter_context(nc.psum_tensor("Sps", [128, 1024], F32))
    PTps0 = _st5.enter_context(nc.psum_tensor("PTps0", [128, 1024], BF16))
    PTps1 = _st5.enter_context(nc.psum_tensor("PTps1", [128, 1024], BF16))
    PTps2 = _st5.enter_context(nc.psum_tensor("PTps2", [128, 1024], BF16))
    PTps3 = _st5.enter_context(nc.psum_tensor("PTps3", [128, 1024], BF16))
    AOps0 = _st5.enter_context(nc.psum_tensor("AOps0", [128, 512], F32))
    AOps1 = _st5.enter_context(nc.psum_tensor("AOps1", [128, 512], F32))
    if True:
        ksb = (ksb0, ksb1)
        vsb = (vsb0, vsb1)
        qsb = (qsb0, qsb1)
        Psb = (Psb0, Psb1, Psb2)
        Ssb = (Ssb0, Ssb1, Ssb2)
        PTsb = (PTsb0, PTsb1, PTsb2, PTsb3)
        rinv = (rinv0, rinv1)
        aoq = (aoq0, aoq1)
        PTps = (PTps0, PTps1, PTps2, PTps3)
        AOps = (AOps0, AOps1)
        p = Ph(nc, "attn", no_gpsimd_drain=True)
        p.op("gpsimd", "dma_start", wo_bf[:, :], woT[:, :], inc=S["d_bwo"], inck=16)
        smm_end = [0] * NO
        av_end = [0] * NO
        grp_glob = 0
        pt_end_unit = {}
        kst_done = [(S["s_kst"][i], S["s_kst"][i].v) for i in range(2)]
        vst_done = [(S["s_vst"][i], S["s_vst"][i].v) for i in range(2)]
        qst_done = [(S["s_qst"][i], S["s_qst"][i].v) for i in range(2)]

        def unit_groups(o):
            nj = 2 * (o + 1)
            return [(gs, min(4, nj - gs * 4)) for gs in range((nj + 3) // 4)]

        unit_base = {}

        def emit_pt(v):
            nonlocal grp_glob
            o2 = v // 32
            groups = unit_groups(o2)
            base = grp_glob
            unit_base[v] = base
            for gi, (gs, width) in enumerate(groups):
                gg = base + gi
                for jj in range(width):
                    w = []
                    if jj == 0:
                        w.append((S["s_exp"], v + 1))
                        w.append((S["s_pte"], gg - 3))
                    p.op("tensor", "transpose",
                         PTps[gg % 4][:, jj * 128:(jj + 1) * 128],
                         Psb[v % 3][:, (gs * 4 + jj) * 128:(gs * 4 + jj + 1) * 128],
                         idb[:, :], waits=w,
                         inc=(S["t_pt"] if jj == width - 1 else None))
            grp_glob = base + len(groups)
            pt_end_unit[v] = S["t_pt"].v

        def emit_ptevac(v):
            o2, rem = divmod(v, 32)
            h2, g2 = divmod(rem, 4)
            H = o2 * 8 + h2
            base = unit_base[v]
            for gi, (gs, width) in enumerate(unit_groups(o2)):
                gg = base + gi
                p.op("vector", "tensor_copy", PTsb[gg % 4][:, 0:width * 128],
                     PTps[gg % 4][:, 0:width * 128],
                     waits=[(S["t_pt"], gg + 1), (S["t_av"], gg - 3)],
                     inc=S["s_pte"])
            p.op("vector", "reciprocal", rinv[H % 2][:, g2:g2 + 1],
                 lsum4[:, v % 4:v % 4 + 1],
                 waits=[(S["s_exp"], v + 1)], inc=S["v_rcp"])
            p.op("vector", "drain")

        def emit_av(v):
            o2, rem = divmod(v, 32)
            h2, g2 = divmod(rem, 4)
            H = o2 * 8 + h2
            groups = unit_groups(o2)
            ng = len(groups)
            base = unit_base[v]
            for gi, (gs, width) in enumerate(groups):
                gg = base + gi
                for jj in range(width):
                    j = gs * 4 + jj
                    w = []
                    if jj == 0:
                        w.append((S["s_pte"], gg + 1))
                    if gi == 0 and jj == 0 and g2 == 0:
                        w.append((S["v_aoe"], H - 1))
                    p.op("tensor", "matmul",
                         AOps[H % 2][:, g2 * 128:(g2 + 1) * 128],
                         PTsb[gg % 4][:, jj * 128:(jj + 1) * 128],
                         vsb[o2 % 2][:, j * 1024 + h2 * 128: j * 1024 + (h2 + 1) * 128],
                         start=(gi == 0 and jj == 0),
                         stop=(gi == ng - 1 and jj == width - 1),
                         waits=w,
                         inc=(S["t_av"] if jj == width - 1 else None))
            if g2 == 3:
                for gg2 in range(4):
                    w = []
                    if gg2 == 0:
                        w = [(S["t_av"], S["t_av"].v),
                             (S["s_aod"][H % 2], S["s_aod"][H % 2].v)]
                    p.op("vector", "tensor_scalar",
                         aoq[H % 2][:, gg2 * 128:(gg2 + 1) * 128],
                         AOps[H % 2][:, gg2 * 128:(gg2 + 1) * 128],
                         rinv[H % 2][:, gg2:gg2 + 1], None, ALU.mult,
                         waits=w, inc=(S["v_aoe"] if gg2 == 3 else None))
                p.op("scalar", "dma_start",
                     ao_scr[o2 * 128:(o2 + 1) * 128, h2 * 512:(h2 + 1) * 512],
                     aoq[H % 2][:, :], waits=[(S["v_aoe"], H + 1)],
                     inc=S["s_aod"][H % 2], inck=16)

        gchunk = [0]

        def emit_exp(v):
            o2 = v // 32
            w = [(S["v_ngm"], v + 1), (S["v_rcp"], v - 3)]
            if v >= 3:
                w.append((S["t_pt"], pt_end_unit[v - 3]))
            p.op("scalar", "activation", Psb[v % 3][:, 0:LP[o2]],
                 Ssb[v % 3][:, 0:LP[o2]], AF.Exp,
                 bias=negm4[:, v % 4:v % 4 + 1], scale=1.0,
                 accum_out=lsum4[:, v % 4:v % 4 + 1], waits=w, inc=S["s_exp"])

        for o in range(NO):
            for hh in range(NKV):
                w = []
                if hh == 0:
                    w = [(S["t_smm"], smm_end[o - 2] if o >= 2 else 0)]
                    if o == 0:
                        w += kst_done
                    w = [(s, v) for (s, v) in w if v > 0]
                p.op("sync", "dma_start",
                     ksb[o % 2][:, hh * 2048: hh * 2048 + LP[o]],
                     k_scr[hh * 128:(hh + 1) * 128, 0:LP[o]],
                     waits=w, inc=S["d_k"][o % 2], inck=16)
            for j in range(2 * (o + 1)):
                w = []
                if j == 0:
                    w = [(S["t_av"], av_end[o - 2] if o >= 2 else 0)]
                    if o == 0:
                        w += vst_done
                    w = [(s, v) for (s, v) in w if v > 0]
                p.op("sync", "dma_start", vsb[o % 2][:, j * 1024:(j + 1) * 1024],
                     v_scr[j * 128:(j + 1) * 128, :], waits=w,
                     inc=S["d_v"][o % 2], inck=16)
            qw = [(S["t_smm"], smm_end[o - 2] if o >= 2 else 0)]
            if o == 0:
                qw += qst_done
            p.op("sync", "dma_start", qsb[o % 2][:, :],
                 q_scr[o * 128:(o + 1) * 128, :],
                 waits=[(s, v) for (s, v) in qw if v > 0],
                 inc=S["d_q"][o % 2], inck=16)
            k_ck = (S["d_k"][o % 2], S["d_k"][o % 2].v)
            v_ck = (S["d_v"][o % 2], S["d_v"][o % 2].v)
            q_ck = (S["d_q"][o % 2], S["d_q"][o % 2].v)
            ns = (LP[o] + 511) // 512
            for h in range(NKV):
                for g in range(4):
                    u = o * 32 + h * 4 + g
                    if u > o * 32:
                        emit_exp(u - 1)
                    for c in range(ns):
                        cw = min(512, LP[o] - c * 512)
                        gc = gchunk[0]
                        w = [(S["s_scp"], gc - 1)]
                        if c == 0 and h == 0 and g == 0:
                            w += [k_ck, v_ck, q_ck]
                        p.op("tensor", "matmul",
                             Sps[:, (gc % 2) * 512:(gc % 2) * 512 + cw],
                             qsb[o % 2][:, (h * 4 + g) * 128:(h * 4 + g + 1) * 128],
                             ksb[o % 2][:, h * 2048 + c * 512: h * 2048 + c * 512 + cw],
                             start=True, stop=True, waits=w, inc=S["t_smm"])
                        w2 = [(S["t_smm"], S["t_smm"].v)]
                        if c == 0:
                            w2.append((S["s_exp"], u - 2))
                        p.op("scalar", "copy",
                             Ssb[u % 3][:, c * 512:c * 512 + cw],
                             Sps[:, (gc % 2) * 512:(gc % 2) * 512 + cw],
                             waits=w2, inc=S["s_scp"])
                        gchunk[0] += 1
                    p.op("vector", "tensor_tensor",
                         Ssb[u % 3][:, LP[o] - 256:LP[o]],
                         Ssb[u % 3][:, LP[o] - 256:LP[o]],
                         cmk[:, o * 512 + 256:(o + 1) * 512], ALU.add,
                         waits=[(S["s_scp"], S["s_scp"].v),
                                (S["s_exp"], u - 3)])
                    p.op("vector", "reduce_max", negm4[:, u % 4:u % 4 + 1],
                         Ssb[u % 3][:, 0:LP[o]], axis=AX.X, negate=True,
                         inc=S["v_ngm"])
                    if u > o * 32 + 1:
                        emit_pt(u - 2)
                        emit_ptevac(u - 2)
                    if u > o * 32 + 2:
                        emit_av(u - 3)
            L = o * 32 + 31
            emit_exp(L)
            emit_pt(L - 1)
            emit_ptevac(L - 1)
            emit_av(L - 2)
            emit_pt(L)
            emit_ptevac(L)
            emit_av(L - 1)
            emit_av(L)
            smm_end[o] = S["t_smm"].v
            av_end[o] = S["t_av"].v
        p.wait("scalar", S["s_aod"][0], S["s_aod"][0].v)
        p.wait("scalar", S["s_aod"][1], S["s_aod"][1].v)
        for i in range(2):
            p.wait("sync", S["d_k"][i], S["d_k"][i].v)
            p.wait("sync", S["d_v"][i], S["d_v"][i].v)
            p.wait("sync", S["d_q"][i], S["d_q"][i].v)
        p.run()

    _st5.close()
    with nc.sbuf_tensor("r1sb", [128, NO * D], BF16) as r1sb:

        _st6 = ExitStack()
        aoc0 = _st6.enter_context(nc.sbuf_tensor("aoc0", [128, 1024], BF16))
        aoc1 = _st6.enter_context(nc.sbuf_tensor("aoc1", [128, 1024], BF16))
        aoc2 = _st6.enter_context(nc.sbuf_tensor("aoc2", [128, 1024], BF16))
        woc0 = _st6.enter_context(nc.sbuf_tensor("woc0", [128, 512], BF16))
        woc1 = _st6.enter_context(nc.sbuf_tensor("woc1", [128, 512], BF16))
        woc2 = _st6.enter_context(nc.sbuf_tensor("woc2", [128, 512], BF16))
        xr0 = _st6.enter_context(nc.sbuf_tensor("xr0", [128, 512], F32))
        xr1 = _st6.enter_context(nc.sbuf_tensor("xr1", [128, 512], F32))
        xr2 = _st6.enter_context(nc.sbuf_tensor("xr2", [128, 512], F32))
        xr3 = _st6.enter_context(nc.sbuf_tensor("xr3", [128, 512], F32))
        tw0 = _st6.enter_context(nc.sbuf_tensor("tw0", [128, 512], F32))
        tw1 = _st6.enter_context(nc.sbuf_tensor("tw1", [128, 512], F32))
        sqd0 = _st6.enter_context(nc.sbuf_tensor("sqd0", [128, 512], BF16))
        sqd1 = _st6.enter_context(nc.sbuf_tensor("sqd1", [128, 512], BF16))
        ws0 = _st6.enter_context(nc.sbuf_tensor("ws0", [128, 2], F32))
        ws1 = _st6.enter_context(nc.sbuf_tensor("ws1", [128, 2], F32))
        sacc = _st6.enter_context(nc.sbuf_tensor("sacc", [128, 8], F32))
        qacc = _st6.enter_context(nc.sbuf_tensor("qacc", [128, 8], F32))
        mn = _st6.enter_context(nc.sbuf_tensor("mn", [128, 1], F32))
        mnn = _st6.enter_context(nc.sbuf_tensor("mnn", [128, 1], F32))
        vr = _st6.enter_context(nc.sbuf_tensor("vr", [128, 1], F32))
        sd = _st6.enter_context(nc.sbuf_tensor("sd", [128, 1], F32))
        rs = _st6.enter_context(nc.sbuf_tensor("rs", [128, 1], F32))
        wp0 = _st6.enter_context(nc.psum_tensor("wp0", [128, 512], F32))
        wp1 = _st6.enter_context(nc.psum_tensor("wp1", [128, 512], F32))
        wp2 = _st6.enter_context(nc.psum_tensor("wp2", [128, 512], F32))
        wp3 = _st6.enter_context(nc.psum_tensor("wp3", [128, 512], F32))
        wp4 = _st6.enter_context(nc.psum_tensor("wp4", [128, 512], F32))
        wp5 = _st6.enter_context(nc.psum_tensor("wp5", [128, 512], F32))
        wp6 = _st6.enter_context(nc.psum_tensor("wp6", [128, 512], F32))
        wp7 = _st6.enter_context(nc.psum_tensor("wp7", [128, 512], F32))
        if True:
            aoc = (aoc0, aoc1, aoc2)
            woc = (woc0, woc1, woc2)
            xr = (xr0, xr1, xr2, xr3)
            tw = (tw0, tw1)
            sqd = (sqd0, sqd1)
            ws = (ws0, ws1)
            wp = (wp0, wp1, wp2, wp3, wp4, wp5, wp6, wp7)
            p = Ph(nc, "wo")
            p.op("vector", "memset", sacc[:, :], 0.0)
            p.op("vector", "memset", qacc[:, :], 0.0)
            aod_done = ([(S["s_aod"][0], S["s_aod"][0].v)] +
                        [(S["s_aod"][1], S["s_aod"][1].v)])
            for n in range(NO):
                for k in range(KC):
                    c = n * KC + k
                    p.op("sync", "dma_start", aoc[c % 3][:, :],
                         ao_scr[:, k * 128:(k + 1) * 128], transpose=True,
                         waits=[(S["t_wmm"], c - 2)] +
                               (aod_done if c == 0 else []),
                         inc=S["d_ao"][c % 3], inck=16)
                    aov = (S["d_ao"][c % 3], S["d_ao"][c % 3].v)
                    p.op("sync", "dma_start", woc[c % 3][:, :],
                         wo_bf[k * 128:(k + 1) * 128, n * 512:(n + 1) * 512],
                         waits=[(S["t_wmm"], c - 2)] +
                               ([(S["d_bwo"], 16)] if c == 0 else []),
                         inc=S["d_wo"][c % 3], inck=16)
                    wov = (S["d_wo"][c % 3], S["d_wo"][c % 3].v)
                    for t in range(NO):
                        w = []
                        if t == 0:
                            w = [aov, wov]
                        if k == 0:
                            w.append((S["v_wadd"], (n - 1) * 8 + t + 1 if n >= 1 else 0))
                        p.op("tensor", "matmul", wp[t][:, :],
                             aoc[c % 3][:, t * 128:(t + 1) * 128], woc[c % 3][:, :],
                             start=(k == 0), stop=(k == KC - 1), waits=w,
                             inc=(S["t_wmm"] if t == NO - 1 else None))
                for t in range(NO):
                    e = n * 8 + t
                    p.op("scalar", "dma_start", xr[e % 4][:, :],
                         x_own[t * 128:(t + 1) * 128, n * 512:(n + 1) * 512],
                         waits=[(S["v_wadd"], e - 3)], inc=S["d_xr"][e % 4], inck=16)
                    xrv = (S["d_xr"][e % 4], S["d_xr"][e % 4].v)
                    p.op("vector", "tensor_tensor", tw[e % 2][:, :], wp[t][:, :],
                         xr[e % 4][:, :], ALU.add,
                         waits=[(S["t_wmm"], (n + 1) * KC), xrv,
                                (S["s_wst"], e - 1)], inc=S["v_wadd"])
                    if e >= 1:
                        p.op("vector", "tensor_tensor", sacc[:, (e - 1) % 8:(e - 1) % 8 + 1],
                             sacc[:, (e - 1) % 8:(e - 1) % 8 + 1], ws[(e - 1) % 2][:, 0:1],
                             ALU.add, waits=[(S["s_wst"], e)])
                        p.op("vector", "tensor_tensor", qacc[:, (e - 1) % 8:(e - 1) % 8 + 1],
                             qacc[:, (e - 1) % 8:(e - 1) % 8 + 1], ws[(e - 1) % 2][:, 1:2],
                             ALU.add, inc=S["v_wsta"])
                    p.op("scalar", "activation", r1sb[:, t * D + n * 512: t * D + (n + 1) * 512],
                         tw[e % 2][:, :], AF.Copy, bias=0.0, scale=1.0,
                         accum_out=ws[e % 2][:, 0:1],
                         waits=[(S["v_wadd"], e + 1), (S["v_wsta"], e - 1)])
                    p.op("scalar", "activation", sqd[e % 2][:, :], tw[e % 2][:, :],
                         AF.Square, bias=0.0, scale=1.0,
                         accum_out=ws[e % 2][:, 1:2], inc=S["s_wst"])
            e = NO * 8 - 1
            p.op("vector", "tensor_tensor", sacc[:, e % 8:e % 8 + 1],
                 sacc[:, e % 8:e % 8 + 1], ws[e % 2][:, 0:1], ALU.add,
                 waits=[(S["s_wst"], e + 1)])
            p.op("vector", "tensor_tensor", qacc[:, e % 8:e % 8 + 1],
                 qacc[:, e % 8:e % 8 + 1], ws[e % 2][:, 1:2], ALU.add,
                 inc=S["v_wsta"])
            for t in range(NO):
                p.op("vector", "tensor_scalar", mn[:, :], sacc[:, t:t + 1],
                     1.0 / D, None, ALU.mult,
                     waits=([(S["v_wsta"], NO * 8)] if t == 0 else
                            [(S["s_sqrt"], t)]))
                p.op("vector", "tensor_scalar", vr[:, :], qacc[:, t:t + 1],
                     1.0 / D, EPS, ALU.mult, ALU.add)
                p.op("vector", "tensor_tensor", sd[:, :], mn[:, :], mn[:, :],
                     ALU.mult)
                p.op("vector", "tensor_scalar", mnn[:, :], mn[:, :], -1.0, None,
                     ALU.mult)
                p.op("vector", "tensor_tensor", vr[:, :], vr[:, :], sd[:, :],
                     ALU.subtract, inc=S["v_var"])
                p.op("scalar", "activation", sd[:, :], vr[:, :], AF.Sqrt,
                     bias=0.0, scale=1.0, waits=[(S["v_var"], t + 1)],
                     inc=S["s_sqrt"])
                p.op("vector", "reciprocal", rs[:, :], sd[:, :],
                     waits=[(S["s_sqrt"], t + 1)])
                p.op("vector", "drain")
                p.op("vector", "tensor_scalar", r1sb[:, t * D:(t + 1) * D],
                     r1sb[:, t * D:(t + 1) * D], mnn[:, :], rs[:, :],
                     ALU.add, ALU.mult, inc=S["v_ln"])
                p.op("scalar", "dma_start", x1_scr[t * 128:(t + 1) * 128, :],
                     r1sb[:, t * D:(t + 1) * D], waits=[(S["v_ln"], t + 1)],
                     inc=S["s_x1"], inck=16)
            p.wait("scalar", S["s_x1"], S["s_x1"].v)
            for i in range(4):
                p.wait("scalar", S["d_xr"][i], S["d_xr"][i].v)
            for i in range(3):
                p.wait("sync", S["d_ao"][i], S["d_ao"][i].v)
                p.wait("sync", S["d_wo"][i], S["d_wo"][i].v)
            p.run()

        _st6.close()
        with nc.sbuf_tensor("x1t_sb", [128, KC * 1024], BF16) as x1t_sb:
            _st7 = ExitStack()
            c1b = _st7.enter_context(nc.sbuf_tensor("c1b", [128, KC * RANK], BF16))
            c2sb = _st7.enter_context(nc.sbuf_tensor("c2sb", [RANK, ITR], BF16))
            r16f = _st7.enter_context(nc.sbuf_tensor("r16f", [128, 8 * RANK], F32))
            r16t_sb = _st7.enter_context(nc.sbuf_tensor("r16t_sb", [RANK, 1024], BF16))
            tmpidx0 = _st7.enter_context(nc.sbuf_tensor("tmpidx0", [128, 256], F32))
            tmpidx1 = _st7.enter_context(nc.sbuf_tensor("tmpidx1", [128, 256], F32))
            bmx = _st7.enter_context(nc.sbuf_tensor("bmx", [128, 2], F32))
            r16ps = _st7.enter_context(nc.psum_tensor("r16ps", [128, 512], F32))
            ips0 = _st7.enter_context(nc.psum_tensor("ips0", [128, 512], F32))
            ips1 = _st7.enter_context(nc.psum_tensor("ips1", [128, 512], F32))
            r16tps = _st7.enter_context(nc.psum_tensor("r16tps", [RANK, 1024], F32))
            if True:
                ips = (ips0, ips1)
                p = Ph(nc, "router")
                x1_done = S["s_x1"].v
                for k in range(KC):
                    p.op("sync", "dma_start", x1t_sb[:, k * 1024:(k + 1) * 1024],
                         x1_scr[:, k * 128:(k + 1) * 128], transpose=True,
                         waits=([(S["s_x1"], x1_done)] if k == 0 else ()),
                         inc=S["d_x1t"], inck=16)
                p.op("sync", "dma_start", c1b[:, :], c1_bfs[:, :],
                     waits=[(S["d_castc"], 32)], inc=S["d_c1"], inck=16)
                p.op("sync", "dma_start", c2sb[:, :], c2_bfs[:, :],
                     inc=S["d_c2"], inck=16)
                for t in range(NO):
                    for k in range(KC):
                        w = []
                        if k == 0:
                            w = [(S["s_r16"], t), (S["d_x1t"], 16 * KC),
                                 (S["d_c1"], 16)]
                        p.op("tensor", "matmul", r16ps[:, 0:RANK],
                             x1t_sb[:, k * 1024 + t * 128: k * 1024 + (t + 1) * 128],
                             c1b[:, k * RANK:(k + 1) * RANK],
                             start=(k == 0), stop=(k == KC - 1), waits=w,
                             inc=(S["t_r16"] if k == KC - 1 else None))
                    p.op("scalar", "copy", r16f[:, t * RANK:(t + 1) * RANK],
                         r16ps[:, 0:RANK], waits=[(S["t_r16"], t + 1)],
                         inc=S["s_r16"])
                for t in range(NO):
                    p.op("tensor", "transpose", r16tps[:, t * 128:(t + 1) * 128],
                         r16f[:, t * RANK:(t + 1) * RANK], idf[:, :],
                         waits=([(S["s_r16"], NO)] if t == 0 else ()),
                         inc=(S["t_r16t"] if t == NO - 1 else None))
                p.op("scalar", "copy", r16t_sb[:, :], r16tps[:, :],
                     waits=[(S["t_r16t"], 1)], inc=S["s_r16t"])
                for r in range(NRS):
                    for t in range(NO):
                        u = r * 8 + t
                        p.op("tensor", "matmul", ips[u % 2][:, :],
                             r16t_sb[:, t * 128:(t + 1) * 128],
                             c2sb[:, r * 512:(r + 1) * 512],
                             start=True, stop=True,
                             waits=[(S["v_idx"], u - 1), (S["s_r16t"], 1),
                                    (S["d_c2"], 16)],
                             inc=S["t_imm"])
                        tmpidx = (tmpidx0, tmpidx1)
                        for b in range(2):
                            p.op("vector", "reduce_max", bmx[:, b:b + 1],
                                 ips[u % 2][:, b * 256:(b + 1) * 256], axis=AX.X,
                                 waits=([(S["t_imm"], u + 1)] if b == 0 else ()))
                        for b in range(2):
                            p.op("vector", "scalar_tensor_tensor", tmpidx[b][:, :],
                                 ips[u % 2][:, b * 256:(b + 1) * 256], bmx[:, b:b + 1],
                                 iota[:, :], ALU.is_ge, ALU.mult)
                        for b in range(2):
                            p.op("vector", "reduce_sum",
                                 idxs[:, t * 56 + r * 2 + b: t * 56 + r * 2 + b + 1],
                                 tmpidx[b][:, :], axis=AX.X,
                                 inc=(S["v_idx"] if b == 1 else None))
                p.wait("sync", S["d_x1t"], S["d_x1t"].v)
                p.wait("sync", S["d_c2"], S["d_c2"].v)
                p.wait("sync", S["d_c1"], S["d_c1"].v)
                p.run()

            _st7.close()
            _st8 = ExitStack()
            w1c0 = _st8.enter_context(nc.sbuf_tensor("w1c0", [128, 512], BF16))
            w1c1 = _st8.enter_context(nc.sbuf_tensor("w1c1", [128, 512], BF16))
            w1c2 = _st8.enter_context(nc.sbuf_tensor("w1c2", [128, 512], BF16))
            w1c3 = _st8.enter_context(nc.sbuf_tensor("w1c3", [128, 512], BF16))
            w1c4 = _st8.enter_context(nc.sbuf_tensor("w1c4", [128, 512], BF16))
            w1c5 = _st8.enter_context(nc.sbuf_tensor("w1c5", [128, 512], BF16))
            zr0 = _st8.enter_context(nc.sbuf_tensor("zr0", [128, 512], BF16))
            zr1 = _st8.enter_context(nc.sbuf_tensor("zr1", [128, 512], BF16))
            ssb0 = _st8.enter_context(nc.sbuf_tensor("ssb0", [128, 512], BF16))
            ssb1 = _st8.enter_context(nc.sbuf_tensor("ssb1", [128, 512], BF16))
            zp0 = _st8.enter_context(nc.psum_tensor("zp0", [128, 512], F32))
            zp1 = _st8.enter_context(nc.psum_tensor("zp1", [128, 512], F32))
            zp2 = _st8.enter_context(nc.psum_tensor("zp2", [128, 512], F32))
            zp3 = _st8.enter_context(nc.psum_tensor("zp3", [128, 512], F32))
            zp4 = _st8.enter_context(nc.psum_tensor("zp4", [128, 512], F32))
            zp5 = _st8.enter_context(nc.psum_tensor("zp5", [128, 512], F32))
            zp6 = _st8.enter_context(nc.psum_tensor("zp6", [128, 512], F32))
            zp7 = _st8.enter_context(nc.psum_tensor("zp7", [128, 512], F32))
            if True:
                w1c = (w1c0, w1c1, w1c2, w1c3, w1c4, w1c5)
                zr = (zr0, zr1)
                ssb = (ssb0, ssb1)
                zp = (zp0, zp1, zp2, zp3, zp4, zp5, zp6, zp7)
                p = Ph(nc, "zmm")
                for r in range(NRS):
                    for k in range(KC):
                        c = r * KC + k
                        p.op("gpsimd", "dma_start", w1c[c % 6][:, :],
                             w1T[k * 128:(k + 1) * 128, r * 512:(r + 1) * 512],
                             waits=[(S["t_zmm"], c - 5)],
                             inc=S["d_w1"][c % 6], inck=16)
                        w1v = (S["d_w1"][c % 6], S["d_w1"][c % 6].v)
                        for t in range(NO):
                            w = []
                            if t == 0:
                                w.append(w1v)
                            if k == 0:
                                w.append((S["s_zrl"], (r - 1) * 8 + t + 1 if r >= 1 else 0))
                            p.op("tensor", "matmul", zp[t][:, :],
                                 x1t_sb[:, k * 1024 + t * 128: k * 1024 + (t + 1) * 128],
                                 w1c[c % 6][:, :],
                                 start=(k == 0), stop=(k == KC - 1), waits=w,
                                 inc=(S["t_zmm"] if t == NO - 1 else None))
                    for t in range(NO):
                        e = r * 8 + t
                        p.op("scalar", "activation", zr[e % 2][:, :], zp[t][:, :],
                             AF.Relu, bias=0.0, scale=1.0,
                             waits=[(S["t_zmm"], r * KC + KC), (S["v_zmk"], e - 1)],
                             inc=S["s_zrl"])
                        for b in range(2):
                            p.op("vector", "scalar_tensor_tensor",
                                 ssb[e % 2][:, b * 256:(b + 1) * 256], iota[:, :],
                                 idxs[:, t * 56 + r * 2 + b: t * 56 + r * 2 + b + 1],
                                 zr[e % 2][:, b * 256:(b + 1) * 256],
                                 ALU.is_equal, ALU.mult,
                                 waits=([(S["s_zrl"], e + 1),
                                         (S["d_ss"][e % 2], S["d_ss"][e % 2].v)]
                                        if b == 0 else ()),
                                 inc=(S["v_zmk"] if b == 1 else None))
                        p.op("scalar", "dma_start",
                             s_scr[t * 128:(t + 1) * 128, r * 512:(r + 1) * 512],
                             ssb[e % 2][:, :], waits=[(S["v_zmk"], e + 1)],
                             inc=S["d_ss"][e % 2], inck=16)
                p.wait("scalar", S["d_ss"][0], S["d_ss"][0].v)
                p.wait("scalar", S["d_ss"][1], S["d_ss"][1].v)
                for i in range(6):
                    p.wait("gpsimd", S["d_w1"][i], S["d_w1"][i].v)
                p.run()

            _st8.close()
        _st9 = ExitStack()
        w2c0 = _st9.enter_context(nc.sbuf_tensor("w2c0", [128, 512], BF16))
        w2c1 = _st9.enter_context(nc.sbuf_tensor("w2c1", [128, 512], BF16))
        w2c2 = _st9.enter_context(nc.sbuf_tensor("w2c2", [128, 512], BF16))
        w2c3 = _st9.enter_context(nc.sbuf_tensor("w2c3", [128, 512], BF16))
        w2c4 = _st9.enter_context(nc.sbuf_tensor("w2c4", [128, 512], BF16))
        w2c5 = _st9.enter_context(nc.sbuf_tensor("w2c5", [128, 512], BF16))
        stc0 = _st9.enter_context(nc.sbuf_tensor("stc0", [128, 1024], BF16))
        stc1 = _st9.enter_context(nc.sbuf_tensor("stc1", [128, 1024], BF16))
        stc2 = _st9.enter_context(nc.sbuf_tensor("stc2", [128, 1024], BF16))
        stc3 = _st9.enter_context(nc.sbuf_tensor("stc3", [128, 1024], BF16))
        stc4 = _st9.enter_context(nc.sbuf_tensor("stc4", [128, 1024], BF16))
        stc5 = _st9.enter_context(nc.sbuf_tensor("stc5", [128, 1024], BF16))
        to0 = _st9.enter_context(nc.sbuf_tensor("to0", [128, 512], F32))
        to1 = _st9.enter_context(nc.sbuf_tensor("to1", [128, 512], F32))
        sqe0 = _st9.enter_context(nc.sbuf_tensor("sqe0", [128, 512], BF16))
        sqe1 = _st9.enter_context(nc.sbuf_tensor("sqe1", [128, 512], BF16))
        os0 = _st9.enter_context(nc.sbuf_tensor("os0", [128, 2], F32))
        os1 = _st9.enter_context(nc.sbuf_tensor("os1", [128, 2], F32))
        sac2 = _st9.enter_context(nc.sbuf_tensor("sac2", [128, 8], F32))
        qac2 = _st9.enter_context(nc.sbuf_tensor("qac2", [128, 8], F32))
        mn2 = _st9.enter_context(nc.sbuf_tensor("mn2", [128, 1], F32))
        mnn2 = _st9.enter_context(nc.sbuf_tensor("mnn2", [128, 1], F32))
        vr2 = _st9.enter_context(nc.sbuf_tensor("vr2", [128, 1], F32))
        sd2 = _st9.enter_context(nc.sbuf_tensor("sd2", [128, 1], F32))
        rs2 = _st9.enter_context(nc.sbuf_tensor("rs2", [128, 1], F32))
        outsb0 = _st9.enter_context(nc.sbuf_tensor("outsb0", [128, D], F32))
        outsb1 = _st9.enter_context(nc.sbuf_tensor("outsb1", [128, D], F32))
        op0 = _st9.enter_context(nc.psum_tensor("op0", [128, 512], F32))
        op1 = _st9.enter_context(nc.psum_tensor("op1", [128, 512], F32))
        op2 = _st9.enter_context(nc.psum_tensor("op2", [128, 512], F32))
        op3 = _st9.enter_context(nc.psum_tensor("op3", [128, 512], F32))
        op4 = _st9.enter_context(nc.psum_tensor("op4", [128, 512], F32))
        op5 = _st9.enter_context(nc.psum_tensor("op5", [128, 512], F32))
        op6 = _st9.enter_context(nc.psum_tensor("op6", [128, 512], F32))
        op7 = _st9.enter_context(nc.psum_tensor("op7", [128, 512], F32))
        if True:
            w2c = (w2c0, w2c1, w2c2, w2c3, w2c4, w2c5)
            stc = (stc0, stc1, stc2, stc3, stc4, stc5)
            to = (to0, to1)
            sqe = (sqe0, sqe1)
            osb = (os0, os1)
            outsb = (outsb0, outsb1)
            opb = (op0, op1, op2, op3, op4, op5, op6, op7)
            p = Ph(nc, "outmm")
            p.op("vector", "memset", sac2[:, :], 0.0)
            p.op("vector", "memset", qac2[:, :], 0.0)
            ss_done = [(S["d_ss"][i], S["d_ss"][i].v) for i in range(2)]
            for n in range(NO):
                for k in range(NKO):
                    c = n * NKO + k
                    p.op("gpsimd", "dma_start", w2c[c % 6][:, :],
                         w2t_in[k * 128:(k + 1) * 128, n * 512:(n + 1) * 512],
                         waits=[(S["t_omm"], c - 5)],
                         inc=S["d_w2"][c % 6], inck=16)
                    w2v = (S["d_w2"][c % 6], S["d_w2"][c % 6].v)
                    p.op("sync", "dma_start", stc[c % 6][:, :],
                         s_scr[:, k * 128:(k + 1) * 128], transpose=True,
                         waits=[(S["t_omm"], c - 5)] +
                               (ss_done if c == 0 else []),
                         inc=S["d_st"][c % 6], inck=16)
                    stv = (S["d_st"][c % 6], S["d_st"][c % 6].v)
                    for t in range(NO):
                        w = []
                        if t == 0:
                            w = [w2v, stv]
                        if k == 0:
                            w.append((S["v_oad"], (n - 1) * 8 + t + 1 if n >= 1 else 0))
                        p.op("tensor", "matmul", opb[t][:, :],
                             stc[c % 6][:, t * 128:(t + 1) * 128], w2c[c % 6][:, :],
                             start=(k == 0), stop=(k == NKO - 1), waits=w,
                             inc=(S["t_omm"] if t == NO - 1 else None))
                for t in range(NO):
                    e = n * 8 + t
                    p.op("vector", "tensor_tensor", to[e % 2][:, :], opb[t][:, :],
                         r1sb[:, t * D + n * 512: t * D + (n + 1) * 512], ALU.add,
                         waits=[(S["t_omm"], (n + 1) * NKO), (S["s_ost"], e - 1)],
                         inc=S["v_oad"])
                    if e >= 1:
                        p.op("vector", "tensor_tensor",
                             sac2[:, (e - 1) % 8:(e - 1) % 8 + 1],
                             sac2[:, (e - 1) % 8:(e - 1) % 8 + 1],
                             osb[(e - 1) % 2][:, 0:1], ALU.add,
                             waits=[(S["s_ost"], e)])
                        p.op("vector", "tensor_tensor",
                             qac2[:, (e - 1) % 8:(e - 1) % 8 + 1],
                             qac2[:, (e - 1) % 8:(e - 1) % 8 + 1],
                             osb[(e - 1) % 2][:, 1:2], ALU.add, inc=S["v_osta"])
                    p.op("scalar", "activation",
                         r1sb[:, t * D + n * 512: t * D + (n + 1) * 512],
                         to[e % 2][:, :], AF.Copy, bias=0.0, scale=1.0,
                         accum_out=osb[e % 2][:, 0:1],
                         waits=[(S["v_oad"], e + 1), (S["v_osta"], e - 1)])
                    p.op("scalar", "activation", sqe[e % 2][:, :], to[e % 2][:, :],
                         AF.Square, bias=0.0, scale=1.0,
                         accum_out=osb[e % 2][:, 1:2], inc=S["s_ost"])
            e = NO * 8 - 1
            p.op("vector", "tensor_tensor", sac2[:, e % 8:e % 8 + 1],
                 sac2[:, e % 8:e % 8 + 1], osb[e % 2][:, 0:1], ALU.add,
                 waits=[(S["s_ost"], e + 1)])
            p.op("vector", "tensor_tensor", qac2[:, e % 8:e % 8 + 1],
                 qac2[:, e % 8:e % 8 + 1], osb[e % 2][:, 1:2], ALU.add,
                 inc=S["v_osta"])
            for t in range(NO):
                p.op("vector", "tensor_scalar", mn2[:, :], sac2[:, t:t + 1],
                     1.0 / D, None, ALU.mult,
                     waits=([(S["v_osta"], NO * 8)] if t == 0 else
                            [(S["s_sqrt2"], t)]))
                p.op("vector", "tensor_scalar", vr2[:, :], qac2[:, t:t + 1],
                     1.0 / D, EPS, ALU.mult, ALU.add)
                p.op("vector", "tensor_tensor", sd2[:, :], mn2[:, :], mn2[:, :],
                     ALU.mult)
                p.op("vector", "tensor_scalar", mnn2[:, :], mn2[:, :], -1.0, None,
                     ALU.mult)
                p.op("vector", "tensor_tensor", vr2[:, :], vr2[:, :], sd2[:, :],
                     ALU.subtract, inc=S["v_var2"])
                p.op("scalar", "activation", sd2[:, :], vr2[:, :], AF.Sqrt,
                     bias=0.0, scale=1.0, waits=[(S["v_var2"], t + 1)],
                     inc=S["s_sqrt2"])
                p.op("vector", "reciprocal", rs2[:, :], sd2[:, :],
                     waits=[(S["s_sqrt2"], t + 1)])
                p.op("vector", "drain")
                p.op("vector", "tensor_scalar", outsb[t % 2][:, :],
                     r1sb[:, t * D:(t + 1) * D], mnn2[:, :], rs2[:, :],
                     ALU.add, ALU.mult,
                     waits=[(S["d_out"][t % 2], S["d_out"][t % 2].v)],
                     inc=S["v_ln2"])
                p.op("scalar", "dma_start", out[t * 128:(t + 1) * 128, :],
                     outsb[t % 2][:, :], waits=[(S["v_ln2"], t + 1)],
                     inc=S["d_out"][t % 2], inck=16)
            for i in range(6):
                p.wait("gpsimd", S["d_w2"][i], S["d_w2"][i].v)
                p.wait("sync", S["d_st"][i], S["d_st"][i].v)
            p.wait("scalar", S["d_out"][0], S["d_out"][0].v)
            p.wait("scalar", S["d_out"][1], S["d_out"][1].v)
            p.wait("gpsimd", S["d_bwo"], S["d_bwo"].v)
            p.wait("gpsimd", S["d_bw1"], S["d_bw1"].v)
            p.wait("gpsimd", S["d_bw2"], S["d_bw2"].v)
            p.run()

        _st9.close()
    es.close()
    return nc


def _host_tables():
    rope_theta, rope_factor, high_ff, low_ff, orig_max = 500000.0, 8.0, 4.0, 1.0, 8192
    inv = 1.0 / rope_theta ** (np.arange(0, HD, 2, dtype=np.float32) / HD)
    wavelen = 2.0 * np.pi / inv
    low_wl = orig_max / low_ff
    high_wl = orig_max / high_ff
    inv_scaled = np.where(wavelen > low_wl, inv / rope_factor, inv)
    smooth = (orig_max / wavelen - low_ff) / (high_ff - low_ff)
    smoothed = (1.0 - smooth) / rope_factor * inv + smooth * inv
    is_medium = (wavelen <= low_wl) & (wavelen >= high_wl)
    inv_freq = np.where(is_medium, smoothed, inv_scaled).astype(np.float32)
    pos = np.arange(T, dtype=np.float32)
    freqs = pos[:, None] * inv_freq[None, :]
    emb = np.concatenate([freqs, freqs], axis=-1)
    cos_t = np.ascontiguousarray(np.cos(emb).T.astype(np.float32))
    sin_t = np.ascontiguousarray(np.sin(emb).T.astype(np.float32))
    return cos_t, sin_t


def kernel(L0_attn_x, L0_attn_Wq, L0_attn_Wk, L0_attn_Wv, L0_attn_Wo,
           L0_ffn_c1_w, L0_ffn_c2_w, L0_ffn_w1, L0_ffn_w2t):
    x = np.ascontiguousarray(np.asarray(L0_attn_x, dtype=np.float32))
    Wq = np.asarray(L0_attn_Wq, dtype=np.float32)
    Wk = np.asarray(L0_attn_Wk, dtype=np.float32)
    Wv = np.asarray(L0_attn_Wv, dtype=np.float32)
    Wo = np.asarray(L0_attn_Wo, dtype=np.float32)
    c1 = np.asarray(L0_ffn_c1_w, dtype=np.float32)
    c2 = np.asarray(L0_ffn_c2_w, dtype=np.float32)
    w1 = np.asarray(L0_ffn_w1, dtype=np.float32)
    w2t = np.ascontiguousarray(np.asarray(L0_ffn_w2t, dtype=np.float32))

    wqT = Wq.T
    wq_t = np.ascontiguousarray(
        wqT.reshape(KC, 128, NQH, 128).transpose(2, 1, 0, 3).reshape(NQH * 128, D))
    wk_img = np.ascontiguousarray(
        Wk.T.reshape(KC, 128, NKV, 128).transpose(1, 0, 2, 3).reshape(128, KC * NKV * 128))
    wv_img = np.ascontiguousarray(
        Wv.T.reshape(KC, 128, NKV, 128).transpose(1, 0, 2, 3).reshape(128, KC * NKV * 128))
    woT = np.ascontiguousarray(Wo.T)
    w1T = np.ascontiguousarray(w1.T)
    c1_img = np.ascontiguousarray(
        c1.T.reshape(KC, 128, RANK).transpose(1, 0, 2).reshape(128, KC * RANK))
    c2T = np.ascontiguousarray(c2.T)

    cos_t, sin_t = _host_tables()
    scale = 1.0 / np.sqrt(np.float32(HD))

    idb = np.eye(128, dtype=ml_dtypes.bfloat16)
    idf = np.eye(128, dtype=np.float32)
    R = np.zeros((128, 128), dtype=np.float32)
    R[np.arange(64), np.arange(64) + 64] = -1.0
    R[np.arange(64) + 64, np.arange(64)] = 1.0
    rotT = np.ascontiguousarray(R.T)
    iota256 = np.tile(np.arange(256, dtype=np.float32), (128, 1))

    halves = []
    for hh in range(2):
        pos_own = np.concatenate([np.arange(t0 * 128, (t0 + 1) * 128)
                                  for t0 in QT[hh]])
        cosq = np.ascontiguousarray(cos_t[:, pos_own] * scale)
        sinq = np.ascontiguousarray(sin_t[:, pos_own] * scale)
        cm = np.zeros((128, NO * 512), dtype=np.float32)
        for o in range(NO):
            i = QT[hh][o]
            lp = LP[o]
            qpos = i * 128 + np.arange(128)[:, None]
            kpos = lp - 256 + np.arange(256)[None, :]
            cm[:, o * 512 + 256:(o + 1) * 512] = np.where(kpos <= qpos, 0.0, NEG)
        halves.append((cosq, sinq, cm))

    nc = build_nc()
    in_maps = []
    for core in range(8):
        b, hh = core // 2, core % 2
        cosq, sinq, cm = halves[hh]
        xb = np.ascontiguousarray(x[b])
        x_own = np.ascontiguousarray(
            np.concatenate([x[b, t0 * 128:(t0 + 1) * 128] for t0 in QT[hh]], axis=0))
        in_maps.append({
            "xb": xb, "x_own": x_own, "wq_t": wq_t, "wk_img": wk_img,
            "wv_img": wv_img, "woT": woT, "w1T": w1T, "w2t_in": w2t,
            "c1_img": c1_img, "c2T": c2T, "cosk": cos_t, "sink": sin_t,
            "cosq": cosq, "sinq": sinq, "cmask": cm, "idb_in": idb,
            "idf_in": idf, "rotT_in": rotT, "iota_in": iota256,
        })
    res = run_bass_kernel_spmd(nc, in_maps, core_ids=list(range(8)))
    outp = np.empty((4, T, D), dtype=np.float32)
    for core in range(8):
        b, hh = core // 2, core % 2
        oc = res.results[core]["out"]
        for o in range(NO):
            t0 = QT[hh][o]
            outp[b, t0 * 128:(t0 + 1) * 128] = oc[o * 128:(o + 1) * 128]
    return outp


# version 13
# speedup vs baseline: 1.0673x; 1.0407x over previous
import numpy as np
import ml_dtypes
from contextlib import ExitStack

import concourse.bass as bass
import concourse.mybir as mybir
from concourse.bass_utils import run_bass_kernel_spmd

F32 = mybir.dt.float32
BF16 = mybir.dt.bfloat16
AF = mybir.ActivationFunctionType
ALU = mybir.AluOpType
AX = mybir.AxisListType

T, D, HD, NKV, NQH = 2048, 4096, 128, 8, 32
ITR, RANK = 14336, 16
KC = D // 128
NO = 8
NTOK = NO * 128
QT = ([0, 3, 4, 7, 8, 11, 12, 15], [1, 2, 5, 6, 9, 10, 13, 14])
LP = [256 * (o + 1) for o in range(NO)]
NEG = -1.0e30
EPS = 1e-5
NRS = ITR // 512
NKO = ITR // 128


PHASES = []


class Sem:
    def __init__(self, h, name=""):
        self.h = h
        self.name = name
        self.v = 0


class Ph:

    def __init__(self, nc, name, no_gpsimd_drain=False):
        self.nc = nc
        self.name = name
        self.no_gpsimd_drain = no_gpsimd_drain
        self.q = {k: [] for k in ("tensor", "vector", "scalar", "gpsimd", "sync")}
        self.meta = {k: [] for k in self.q}

    def op(self, eng, meth, *args, inc=None, inck=1, waits=(), **kw):
        wts = tuple((s.h, v) for (s, v) in waits if v > 0)

        def f(e, meth=meth, args=args, kw=kw, inc=inc, inck=inck, wts=wts):
            for (h, v) in wts:
                e.wait_ge(h, v)
            r = getattr(e, meth)(*args, **kw)
            if inc is not None:
                r.then_inc(inc.h, inck)

        self.q[eng].append(f)
        self.meta[eng].append((tuple((s.name, v) for (s, v) in waits if v > 0),
                               inc.name if inc is not None else None, inck, meth))
        if inc is not None:
            inc.v += inck

    def wait(self, eng, sem, v):
        if v > 0:
            self.q[eng].append(lambda e, h=sem.h, v=v: e.wait_ge(h, v))
            self.meta[eng].append((((sem.name, v),), None, 0, "wait"))

    def run(self):
        PHASES.append((self.name, self.meta))
        with self.nc.Block(name=self.name,
                           no_gpsimd_drain=self.no_gpsimd_drain) as blk:
            for en in ("sync", "gpsimd", "tensor", "vector", "scalar"):
                fns = self.q[en]
                if fns:
                    def body(e, fns=fns):
                        for fn in fns:
                            fn(e)
                    getattr(blk, en)(body)


def build_nc():
    nc = bass.Bass("TRN2", target_bir_lowering=False)

    xb = nc.dram_tensor("xb", [T, D], F32, kind="ExternalInput")
    x_own = nc.dram_tensor("x_own", [NTOK, D], F32, kind="ExternalInput")
    wq_t = nc.dram_tensor("wq_t", [NQH * 128, D], F32, kind="ExternalInput")
    wk_img = nc.dram_tensor("wk_img", [128, KC * NKV * 128], F32, kind="ExternalInput")
    wv_img = nc.dram_tensor("wv_img", [128, KC * NKV * 128], F32, kind="ExternalInput")
    woT = nc.dram_tensor("woT", [D, D], F32, kind="ExternalInput")
    w1T = nc.dram_tensor("w1T", [D, ITR], F32, kind="ExternalInput")
    w2t_in = nc.dram_tensor("w2t_in", [ITR, D], F32, kind="ExternalInput")
    c1_img = nc.dram_tensor("c1_img", [128, KC * RANK], F32, kind="ExternalInput")
    c2T = nc.dram_tensor("c2T", [RANK, ITR], F32, kind="ExternalInput")
    cosk = nc.dram_tensor("cosk", [128, T], F32, kind="ExternalInput")
    sink = nc.dram_tensor("sink", [128, T], F32, kind="ExternalInput")
    cosq = nc.dram_tensor("cosq", [128, NTOK], F32, kind="ExternalInput")
    sinq = nc.dram_tensor("sinq", [128, NTOK], F32, kind="ExternalInput")
    cmask = nc.dram_tensor("cmask", [128, NO * 512], F32, kind="ExternalInput")
    idb_in = nc.dram_tensor("idb_in", [128, 128], BF16, kind="ExternalInput")
    idf_in = nc.dram_tensor("idf_in", [128, 128], F32, kind="ExternalInput")
    rotT_in = nc.dram_tensor("rotT_in", [128, 128], F32, kind="ExternalInput")
    iota_in = nc.dram_tensor("iota_in", [128, 256], F32, kind="ExternalInput")
    out = nc.dram_tensor("out", [NTOK, D], F32, kind="ExternalOutput")
    x_bf = nc.dram_tensor("x_bf", [T, D], BF16)
    xo_bf = nc.dram_tensor("xo_bf", [NTOK, D], BF16)
    k_scr = nc.dram_tensor("k_scr", [NKV * 128, T], BF16)
    v_scr = nc.dram_tensor("v_scr", [T, NKV * 128], BF16)
    q_scr = nc.dram_tensor("q_scr", [NO * 128, NQH * 128], BF16)
    ao_scr = nc.dram_tensor("ao_scr", [NTOK, D], BF16)
    x1_scr = nc.dram_tensor("x1_scr", [NTOK, D], BF16)
    s_scr = nc.dram_tensor("s_scr", [NTOK, ITR], BF16)
    c1_bfs = nc.dram_tensor("c1_bfs", [128, KC * RANK], BF16)
    c2_bfs = nc.dram_tensor("c2_bfs", [RANK, ITR], BF16)
    wo_bf = nc.dram_tensor("wo_bf", [D, D], BF16)
    w1_bf = nc.dram_tensor("w1_bf", [D, ITR], BF16)
    w2_bf = nc.dram_tensor("w2_bf", [ITR, D], BF16)

    es = ExitStack()
    sem = {}
    for name in ("d_const d_cast d_castc d_wk d_xt d_wv d_xo t_kmm s_kev t_krot "
                 "v_krope s_kst0 s_kst1 t_vmm s_vev t_qmm s_qev t_qrot v_qrope "
                 "d_k0 d_k1 d_v0 d_v1 d_q0 d_q1 t_smm v_msk s_scp v_ngm s_exp "
                 "v_rcp t_pt s_pte t_av v_aoe "
                 "d_ao0 d_ao1 d_ao2 d_wo0 d_wo1 d_wo2 d_xr0 d_xr1 d_xr2 d_xr3 "
                 "t_wmm v_wadd s_wst v_wsta v_var s_sqrt v_ln s_x1 "
                 "d_x1t d_c1 d_c2 t_r16 s_r16 t_r16t s_r16t t_imm v_idx "
                 "d_w10 d_w11 d_w12 d_w13 d_w14 d_w15 t_zmm s_zrl v_zmk "
                 "d_bwo d_bw1 d_bw2 "
                 "t_omm v_oad s_ost v_osta v_var2 s_sqrt2 v_ln2"
                 ).split():
        sem[name] = Sem(es.enter_context(nc.semaphore(name)), name)
    for base, n in (("s_kst", 2), ("d_k", 2), ("d_v", 2), ("d_q", 2),
                    ("d_ao", 3), ("d_wo", 3), ("d_xr", 4), ("d_w1", 6)):
        sem[base] = [sem[f"{base}{i}"] for i in range(n)]
    for alias in ("s_vst", "s_qst", "s_aod", "d_ss", "d_out"):
        sem[alias] = sem["s_kst"]
    sem["d_st"] = sem["d_ao"] + sem["d_wo"]
    sem["d_w2"] = sem["d_w1"]
    sem["d_wq"] = sem["d_w1"][0:2]

    idb = es.enter_context(nc.sbuf_tensor("idb", [128, 128], BF16))
    idf = es.enter_context(nc.sbuf_tensor("idf", [128, 128], F32))
    rotT = es.enter_context(nc.sbuf_tensor("rotT", [128, 128], F32))
    iota = es.enter_context(nc.sbuf_tensor("iota", [128, 256], F32))
    cmk = es.enter_context(nc.sbuf_tensor("cmk", [128, NO * 512], F32))
    idxs = es.enter_context(nc.sbuf_tensor("idxs", [128, NO * 56], F32))

    p = Ph(nc, "init")
    S = sem
    for dst, src in ((idb, idb_in), (idf, idf_in), (rotT, rotT_in),
                     (iota, iota_in), (cmk, cmask)):
        p.op("sync", "dma_start", dst[:, :], src[:, :], inc=S["d_const"], inck=16)
    p.op("gpsimd", "dma_start", x_bf[:, :], xb[:, :], inc=S["d_cast"], inck=16)
    p.op("gpsimd", "dma_start", xo_bf[:, :], x_own[:, :], inc=S["d_cast"], inck=16)
    p.op("gpsimd", "dma_start", c1_bfs[:, :], c1_img[:, :], inc=S["d_castc"], inck=16)
    p.op("gpsimd", "dma_start", c2_bfs[:, :], c2T[:, :], inc=S["d_castc"], inck=16)
    p.wait("sync", S["d_const"], S["d_const"].v)
    p.wait("gpsimd", S["d_cast"], S["d_cast"].v)
    p.wait("gpsimd", S["d_castc"], S["d_castc"].v)
    p.run()

    _st1 = ExitStack()
    xt_sb = _st1.enter_context(nc.sbuf_tensor("xt_sb", [128, KC * 1024], BF16))
    cosk_sb = _st1.enter_context(nc.sbuf_tensor("cosk_sb", [128, T], F32))
    sink_sb = _st1.enter_context(nc.sbuf_tensor("sink_sb", [128, T], F32))
    if True:

        _st2 = ExitStack()
        wk_sb = _st2.enter_context(nc.sbuf_tensor("wk_sb", [128, KC * NKV * 128], BF16))
        kraw0 = _st2.enter_context(nc.sbuf_tensor("kraw0", [128, 512], F32))
        kraw1 = _st2.enter_context(nc.sbuf_tensor("kraw1", [128, 512], F32))
        ktmp1 = _st2.enter_context(nc.sbuf_tensor("ktmp1", [128, 512], F32))
        ktmp2 = _st2.enter_context(nc.sbuf_tensor("ktmp2", [128, 512], F32))
        kbf0 = _st2.enter_context(nc.sbuf_tensor("kbf0", [128, 512], BF16))
        kbf1 = _st2.enter_context(nc.sbuf_tensor("kbf1", [128, 512], BF16))
        kps0 = _st2.enter_context(nc.psum_tensor("kps0", [128, 512], F32))
        kps1 = _st2.enter_context(nc.psum_tensor("kps1", [128, 512], F32))
        krt0 = _st2.enter_context(nc.psum_tensor("krt0", [128, 512], F32))
        krt1 = _st2.enter_context(nc.psum_tensor("krt1", [128, 512], F32))
        if True:
            kraw = (kraw0, kraw1)
            kbf = (kbf0, kbf1)
            kps = (kps0, kps1)
            krt = (krt0, krt1)
            p = Ph(nc, "kproj")
            p.op("sync", "dma_start", cosk_sb[:, :], cosk[:, :], inc=S["d_const"], inck=16)
            p.op("sync", "dma_start", sink_sb[:, :], sink[:, :], inc=S["d_const"], inck=16)
            cos_base = S["d_const"].v
            p.op("gpsimd", "dma_start", wk_sb[:, :], wk_img[:, :],
                 waits=[(S["d_cast"], 32)], inc=S["d_wk"], inck=16)

            units = [(half, h, tc) for half in range(2) for h in range(NKV)
                     for tc in range(2)]
            for half in range(2):
                for k in range(KC):
                    w = []
                    if half == 1 and k == 0:
                        w.append((S["t_kmm"], 16))
                    if half == 0 and k == 0:
                        w.append((S["d_cast"], 32))
                    p.op("sync", "dma_start", xt_sb[:, k * 1024:(k + 1) * 1024],
                         x_bf[half * 1024:(half + 1) * 1024, k * 128:(k + 1) * 128],
                         transpose=True, waits=w, inc=S["d_xt"], inck=16)
            for u, (half, h, tc) in enumerate(units):
                gpos = half * 1024 + tc * 512
                w = [(S["s_kev"], u - 1)]
                if u == 0 or (half == 1 and h == 0 and tc == 0):
                    w.append((S["d_xt"], 16 * 32 * (half + 1)))
                    w.append((S["d_wk"], 16))
                for k in range(KC):
                    p.op("tensor", "matmul", kps[u % 2][:, :],
                         wk_sb[:, k * 1024 + h * 128: k * 1024 + (h + 1) * 128],
                         xt_sb[:, k * 1024 + tc * 512: k * 1024 + tc * 512 + 512],
                         start=(k == 0), stop=(k == KC - 1),
                         waits=(w if k == 0 else ()),
                         inc=(S["t_kmm"] if k == KC - 1 else None))
                if u >= 1:
                    p.op("tensor", "matmul", krt[(u - 1) % 2][:, :], rotT[:, :],
                         kraw[(u - 1) % 2][:, :], start=True, stop=True,
                         waits=[(S["s_kev"], u), (S["v_krope"], u - 2)],
                         inc=S["t_krot"])
                p.op("scalar", "copy", kraw[u % 2][:, :], kps[u % 2][:, :],
                     waits=[(S["t_kmm"], u + 1), (S["t_krot"], u - 1),
                            (S["v_krope"], u - 1)], inc=S["s_kev"])
                if u >= 1:
                    v = u - 1
                    halfv, hv, tcv = units[v]
                    gposv = halfv * 1024 + tcv * 512
                    p.op("vector", "tensor_tensor", ktmp1[:, :], kraw[v % 2][:, :],
                         cosk_sb[:, gposv:gposv + 512], ALU.mult,
                         waits=[(S["s_kev"], v + 1), (S["t_krot"], v + 1),
                                (S["d_const"], cos_base)])
                    p.op("vector", "tensor_tensor", ktmp2[:, :], krt[v % 2][:, :],
                         sink_sb[:, gposv:gposv + 512], ALU.mult)
                    p.op("vector", "tensor_tensor", kbf[v % 2][:, :], ktmp1[:, :],
                         ktmp2[:, :], ALU.add,
                         waits=[(S["s_kst"][v % 2], S["s_kst"][v % 2].v)],
                         inc=S["v_krope"])
                    p.op("scalar", "dma_start",
                         k_scr[hv * 128:(hv + 1) * 128, gposv:gposv + 512],
                         kbf[v % 2][:, :],
                         waits=[(S["v_krope"], v + 1)], inc=S["s_kst"][v % 2], inck=16)
            v = len(units) - 1
            halfv, hv, tcv = units[v]
            gposv = halfv * 1024 + tcv * 512
            p.op("tensor", "matmul", krt[v % 2][:, :], rotT[:, :],
                 kraw[v % 2][:, :], start=True, stop=True,
                 waits=[(S["s_kev"], v + 1), (S["v_krope"], v - 1)],
                 inc=S["t_krot"])
            p.op("vector", "tensor_tensor", ktmp1[:, :], kraw[v % 2][:, :],
                 cosk_sb[:, gposv:gposv + 512], ALU.mult,
                 waits=[(S["t_krot"], v + 1)])
            p.op("vector", "tensor_tensor", ktmp2[:, :], krt[v % 2][:, :],
                 sink_sb[:, gposv:gposv + 512], ALU.mult)
            p.op("vector", "tensor_tensor", kbf[v % 2][:, :], ktmp1[:, :],
                 ktmp2[:, :], ALU.add, inc=S["v_krope"])
            p.op("scalar", "dma_start",
                 k_scr[hv * 128:(hv + 1) * 128, gposv:gposv + 512],
                 kbf[v % 2][:, :], waits=[(S["v_krope"], v + 1)],
                 inc=S["s_kst"][v % 2], inck=16)
            p.wait("scalar", S["s_kst"][0], S["s_kst"][0].v)
            p.wait("scalar", S["s_kst"][1], S["s_kst"][1].v)
            p.wait("sync", S["d_xt"], S["d_xt"].v)
            p.wait("gpsimd", S["d_wk"], S["d_wk"].v)
            p.run()

        _st2.close()
        _st3 = ExitStack()
        wv_sb = _st3.enter_context(nc.sbuf_tensor("wv_sb", [128, KC * NKV * 128], BF16))
        vst0 = _st3.enter_context(nc.sbuf_tensor("vst0", [128, 1024], BF16))
        vst1 = _st3.enter_context(nc.sbuf_tensor("vst1", [128, 1024], BF16))
        vps0 = _st3.enter_context(nc.psum_tensor("vps0", [128, 512], F32))
        vps1 = _st3.enter_context(nc.psum_tensor("vps1", [128, 512], F32))
        if True:
            vst = (vst0, vst1)
            vps = (vps0, vps1)
            p = Ph(nc, "vproj")
            p.op("gpsimd", "dma_start", wv_sb[:, :], wv_img[:, :],
                 inc=S["d_wv"], inck=16)
            xt_base = S["d_xt"].v
            for half in range(2):
                for k in range(KC):
                    w = []
                    if half == 1 and k == 0:
                        w.append((S["t_vmm"], 16))
                    p.op("sync", "dma_start", xt_sb[:, k * 1024:(k + 1) * 1024],
                         x_bf[half * 1024:(half + 1) * 1024, k * 128:(k + 1) * 128],
                         transpose=True, waits=w, inc=S["d_xt"], inck=16)
            units = [(half, tl, hg) for half in range(2) for tl in range(8)
                     for hg in range(2)]
            for u, (half, tl, hg) in enumerate(units):
                ttg = half * 8 + tl
                w = [(S["s_vev"], u - 1)]
                if tl == 0 and hg == 0:
                    w.append((S["d_xt"], xt_base + 16 * 32 * (half + 1)))
                    w.append((S["d_wv"], 16))
                for k in range(KC):
                    p.op("tensor", "matmul", vps[u % 2][:, :],
                         xt_sb[:, k * 1024 + tl * 128: k * 1024 + (tl + 1) * 128],
                         wv_sb[:, k * 1024 + hg * 512: k * 1024 + hg * 512 + 512],
                         start=(k == 0), stop=(k == KC - 1),
                         waits=(w if k == 0 else ()),
                         inc=(S["t_vmm"] if k == KC - 1 else None))
                p.op("scalar", "copy", vst[ttg % 2][:, hg * 512:hg * 512 + 512],
                     vps[u % 2][:, :],
                     waits=[(S["t_vmm"], u + 1),
                            (S["s_vst"][ttg % 2], S["s_vst"][ttg % 2].v)],
                     inc=S["s_vev"])
                if hg == 1:
                    p.op("scalar", "dma_start",
                         v_scr[ttg * 128:(ttg + 1) * 128, :], vst[ttg % 2][:, :],
                         inc=S["s_vst"][ttg % 2], inck=16)
            p.wait("scalar", S["s_vst"][0], S["s_vst"][0].v)
            p.wait("scalar", S["s_vst"][1], S["s_vst"][1].v)
            p.wait("sync", S["d_xt"], S["d_xt"].v)
            p.wait("gpsimd", S["d_wv"], S["d_wv"].v)
            p.run()

        _st3.close()
    _st1.close()
    _st4 = ExitStack()
    xTown = _st4.enter_context(nc.sbuf_tensor("xTown", [128, KC * 1024], BF16))
    cosq_sb = _st4.enter_context(nc.sbuf_tensor("cosq_sb", [128, NTOK], F32))
    sinq_sb = _st4.enter_context(nc.sbuf_tensor("sinq_sb", [128, NTOK], F32))
    wq_sb0 = _st4.enter_context(nc.sbuf_tensor("wq_sb0", [128, D], BF16))
    wq_sb1 = _st4.enter_context(nc.sbuf_tensor("wq_sb1", [128, D], BF16))
    qraw0 = _st4.enter_context(nc.sbuf_tensor("qraw0", [128, 512], F32))
    qraw1 = _st4.enter_context(nc.sbuf_tensor("qraw1", [128, 512], F32))
    qtmp1 = _st4.enter_context(nc.sbuf_tensor("qtmp1", [128, 512], F32))
    qtmp2 = _st4.enter_context(nc.sbuf_tensor("qtmp2", [128, 512], F32))
    qbf0 = _st4.enter_context(nc.sbuf_tensor("qbf0", [128, 512], BF16))
    qbf1 = _st4.enter_context(nc.sbuf_tensor("qbf1", [128, 512], BF16))
    qps0 = _st4.enter_context(nc.psum_tensor("qps0", [128, 512], F32))
    qps1 = _st4.enter_context(nc.psum_tensor("qps1", [128, 512], F32))
    qrt0 = _st4.enter_context(nc.psum_tensor("qrt0", [128, 512], F32))
    qrt1 = _st4.enter_context(nc.psum_tensor("qrt1", [128, 512], F32))
    if True:
        wq_sb = (wq_sb0, wq_sb1)
        qraw = (qraw0, qraw1)
        qbf = (qbf0, qbf1)
        qps = (qps0, qps1)
        qrt = (qrt0, qrt1)
        p = Ph(nc, "qproj")
        p.op("sync", "dma_start", cosq_sb[:, :], cosq[:, :], inc=S["d_const"], inck=16)
        p.op("sync", "dma_start", sinq_sb[:, :], sinq[:, :], inc=S["d_const"], inck=16)
        for k in range(KC):
            p.op("sync", "dma_start", xTown[:, k * 1024:(k + 1) * 1024],
                 xo_bf[:, k * 128:(k + 1) * 128], transpose=True,
                 waits=([(S["d_cast"], 32)] if k == 0 else ()),
                 inc=S["d_xo"], inck=16)
        wqv = []
        for head in range(NQH):
            p.op("gpsimd", "dma_start", wq_sb[head % 2][:, :],
                 wq_t[head * 128:(head + 1) * 128, :],
                 waits=[(S["t_qmm"], 2 * (head - 1))],
                 inc=S["d_wq"][head % 2], inck=16)
            wqv.append(S["d_wq"][head % 2].v)
        cq_base = S["d_const"].v

        def emit_qrope(v):
            headv, tgv = divmod(v, 2)
            p.op("vector", "tensor_tensor", qtmp1[:, :], qraw[v % 2][:, :],
                 cosq_sb[:, tgv * 512:tgv * 512 + 512], ALU.mult,
                 waits=[(S["t_qrot"], v + 1), (S["d_const"], cq_base)])
            p.op("vector", "tensor_tensor", qtmp2[:, :], qrt[v % 2][:, :],
                 sinq_sb[:, tgv * 512:tgv * 512 + 512], ALU.mult)
            p.op("vector", "tensor_tensor", qbf[v % 2][:, :], qtmp1[:, :],
                 qtmp2[:, :], ALU.add,
                 waits=[(S["s_qst"][v % 2], S["s_qst"][v % 2].v)],
                 inc=S["v_qrope"])
            for j in range(4):
                slot = tgv * 4 + j
                p.op("scalar", "dma_start",
                     q_scr[slot * 128:(slot + 1) * 128,
                           headv * 128:(headv + 1) * 128],
                     qbf[v % 2][:, j * 128:(j + 1) * 128],
                     waits=([(S["v_qrope"], v + 1)] if j == 0 else ()),
                     inc=S["s_qst"][v % 2], inck=16)

        for u in range(NQH * 2):
            head, tg = divmod(u, 2)
            w = [(S["s_qev"], u - 1)]
            if tg == 0:
                w.append((S["d_wq"][head % 2], wqv[head]))
            if u == 0:
                w.append((S["d_xo"], 16 * KC))
            for k in range(KC):
                p.op("tensor", "matmul", qps[u % 2][:, :],
                     wq_sb[head % 2][:, k * 128:(k + 1) * 128],
                     xTown[:, k * 1024 + tg * 512: k * 1024 + tg * 512 + 512],
                     start=(k == 0), stop=(k == KC - 1),
                     waits=(w if k == 0 else ()),
                     inc=(S["t_qmm"] if k == KC - 1 else None))
            if u >= 1:
                p.op("tensor", "matmul", qrt[(u - 1) % 2][:, :], rotT[:, :],
                     qraw[(u - 1) % 2][:, :], start=True, stop=True,
                     waits=[(S["s_qev"], u), (S["v_qrope"], u - 2)],
                     inc=S["t_qrot"])
            p.op("scalar", "copy", qraw[u % 2][:, :], qps[u % 2][:, :],
                 waits=[(S["t_qmm"], u + 1), (S["t_qrot"], u - 1),
                        (S["v_qrope"], u - 1)], inc=S["s_qev"])
            if u >= 1:
                emit_qrope(u - 1)
        v = NQH * 2 - 1
        p.op("tensor", "matmul", qrt[v % 2][:, :], rotT[:, :], qraw[v % 2][:, :],
             start=True, stop=True,
             waits=[(S["s_qev"], v + 1), (S["v_qrope"], v - 1)], inc=S["t_qrot"])
        emit_qrope(v)
        p.wait("scalar", S["s_qst"][0], S["s_qst"][0].v)
        p.wait("scalar", S["s_qst"][1], S["s_qst"][1].v)
        p.wait("sync", S["d_xo"], S["d_xo"].v)
        p.wait("gpsimd", S["d_wq"][0], S["d_wq"][0].v)
        p.wait("gpsimd", S["d_wq"][1], S["d_wq"][1].v)
        p.run()

    _st4.close()
    _st5 = ExitStack()
    ksb0 = _st5.enter_context(nc.sbuf_tensor("ksb0", [128, NKV * 2048], BF16))
    ksb1 = _st5.enter_context(nc.sbuf_tensor("ksb1", [128, NKV * 2048], BF16))
    vsb0 = _st5.enter_context(nc.sbuf_tensor("vsb0", [128, 16 * 1024], BF16))
    vsb1 = _st5.enter_context(nc.sbuf_tensor("vsb1", [128, 16 * 1024], BF16))
    qsb0 = _st5.enter_context(nc.sbuf_tensor("qsb0", [128, NQH * 128], BF16))
    qsb1 = _st5.enter_context(nc.sbuf_tensor("qsb1", [128, NQH * 128], BF16))
    Psb0 = _st5.enter_context(nc.sbuf_tensor("Psb0", [128, 2048], BF16))
    Psb1 = _st5.enter_context(nc.sbuf_tensor("Psb1", [128, 2048], BF16))
    Psb2 = _st5.enter_context(nc.sbuf_tensor("Psb2", [128, 2048], BF16))
    Ssb0 = _st5.enter_context(nc.sbuf_tensor("Ssb0", [128, 2048], F32))
    Ssb1 = _st5.enter_context(nc.sbuf_tensor("Ssb1", [128, 2048], F32))
    Ssb2 = _st5.enter_context(nc.sbuf_tensor("Ssb2", [128, 2048], F32))
    PTsb0 = _st5.enter_context(nc.sbuf_tensor("PTsb0", [128, 512], BF16))
    PTsb1 = _st5.enter_context(nc.sbuf_tensor("PTsb1", [128, 512], BF16))
    PTsb2 = _st5.enter_context(nc.sbuf_tensor("PTsb2", [128, 512], BF16))
    PTsb3 = _st5.enter_context(nc.sbuf_tensor("PTsb3", [128, 512], BF16))
    negm4 = _st5.enter_context(nc.sbuf_tensor("negm4", [128, 4], F32))
    lsum4 = _st5.enter_context(nc.sbuf_tensor("lsum4", [128, 4], F32))
    rinv0 = _st5.enter_context(nc.sbuf_tensor("rinv0", [128, 4], F32))
    rinv1 = _st5.enter_context(nc.sbuf_tensor("rinv1", [128, 4], F32))
    aoq0 = _st5.enter_context(nc.sbuf_tensor("aoq0", [128, 512], BF16))
    aoq1 = _st5.enter_context(nc.sbuf_tensor("aoq1", [128, 512], BF16))
    Sps = _st5.enter_context(nc.psum_tensor("Sps", [128, 1024], F32))
    PTps0 = _st5.enter_context(nc.psum_tensor("PTps0", [128, 1024], BF16))
    PTps1 = _st5.enter_context(nc.psum_tensor("PTps1", [128, 1024], BF16))
    PTps2 = _st5.enter_context(nc.psum_tensor("PTps2", [128, 1024], BF16))
    PTps3 = _st5.enter_context(nc.psum_tensor("PTps3", [128, 1024], BF16))
    AOps0 = _st5.enter_context(nc.psum_tensor("AOps0", [128, 512], F32))
    AOps1 = _st5.enter_context(nc.psum_tensor("AOps1", [128, 512], F32))
    if True:
        ksb = (ksb0, ksb1)
        vsb = (vsb0, vsb1)
        qsb = (qsb0, qsb1)
        Psb = (Psb0, Psb1, Psb2)
        Ssb = (Ssb0, Ssb1, Ssb2)
        PTsb = (PTsb0, PTsb1, PTsb2, PTsb3)
        rinv = (rinv0, rinv1)
        aoq = (aoq0, aoq1)
        PTps = (PTps0, PTps1, PTps2, PTps3)
        AOps = (AOps0, AOps1)
        p = Ph(nc, "attn", no_gpsimd_drain=True)
        p.op("gpsimd", "dma_start", wo_bf[:, :], woT[:, :], inc=S["d_bwo"], inck=16)
        smm_end = [0] * NO
        av_end = [0] * NO
        grp_glob = 0
        pt_end_unit = {}
        kst_done = [(S["s_kst"][i], S["s_kst"][i].v) for i in range(2)]
        vst_done = [(S["s_vst"][i], S["s_vst"][i].v) for i in range(2)]
        qst_done = [(S["s_qst"][i], S["s_qst"][i].v) for i in range(2)]

        def unit_groups(o):
            nj = 2 * (o + 1)
            return [(gs, min(4, nj - gs * 4)) for gs in range((nj + 3) // 4)]

        unit_base = {}

        def emit_pt(v):
            nonlocal grp_glob
            o2 = v // 32
            groups = unit_groups(o2)
            base = grp_glob
            unit_base[v] = base
            for gi, (gs, width) in enumerate(groups):
                gg = base + gi
                for jj in range(width):
                    w = []
                    if jj == 0:
                        w.append((S["s_exp"], v + 1))
                        w.append((S["s_pte"], gg - 3))
                    p.op("tensor", "transpose",
                         PTps[gg % 4][:, jj * 128:(jj + 1) * 128],
                         Psb[v % 3][:, (gs * 4 + jj) * 128:(gs * 4 + jj + 1) * 128],
                         idb[:, :], waits=w,
                         inc=(S["t_pt"] if jj == width - 1 else None))
            grp_glob = base + len(groups)
            pt_end_unit[v] = S["t_pt"].v

        def emit_ptevac(v):
            o2, rem = divmod(v, 32)
            h2, g2 = divmod(rem, 4)
            H = o2 * 8 + h2
            base = unit_base[v]
            for gi, (gs, width) in enumerate(unit_groups(o2)):
                gg = base + gi
                p.op("vector", "tensor_copy", PTsb[gg % 4][:, 0:width * 128],
                     PTps[gg % 4][:, 0:width * 128],
                     waits=[(S["t_pt"], gg + 1), (S["t_av"], gg - 3)],
                     inc=S["s_pte"])
            p.op("vector", "reciprocal", rinv[H % 2][:, g2:g2 + 1],
                 lsum4[:, v % 4:v % 4 + 1],
                 waits=[(S["s_exp"], v + 1)], inc=S["v_rcp"])
            p.op("vector", "drain")

        def emit_av(v):
            o2, rem = divmod(v, 32)
            h2, g2 = divmod(rem, 4)
            H = o2 * 8 + h2
            groups = unit_groups(o2)
            ng = len(groups)
            base = unit_base[v]
            for gi, (gs, width) in enumerate(groups):
                gg = base + gi
                for jj in range(width):
                    j = gs * 4 + jj
                    w = []
                    if jj == 0:
                        w.append((S["s_pte"], gg + 1))
                    if gi == 0 and jj == 0 and g2 == 0:
                        w.append((S["v_aoe"], H - 1))
                    p.op("tensor", "matmul",
                         AOps[H % 2][:, g2 * 128:(g2 + 1) * 128],
                         PTsb[gg % 4][:, jj * 128:(jj + 1) * 128],
                         vsb[o2 % 2][:, j * 1024 + h2 * 128: j * 1024 + (h2 + 1) * 128],
                         start=(gi == 0 and jj == 0),
                         stop=(gi == ng - 1 and jj == width - 1),
                         waits=w,
                         inc=(S["t_av"] if jj == width - 1 else None))
            if g2 == 3:
                for gg2 in range(4):
                    w = []
                    if gg2 == 0:
                        w = [(S["t_av"], S["t_av"].v),
                             (S["s_aod"][H % 2], S["s_aod"][H % 2].v)]
                    p.op("vector", "tensor_scalar",
                         aoq[H % 2][:, gg2 * 128:(gg2 + 1) * 128],
                         AOps[H % 2][:, gg2 * 128:(gg2 + 1) * 128],
                         rinv[H % 2][:, gg2:gg2 + 1], None, ALU.mult,
                         waits=w, inc=(S["v_aoe"] if gg2 == 3 else None))
                p.op("scalar", "dma_start",
                     ao_scr[o2 * 128:(o2 + 1) * 128, h2 * 512:(h2 + 1) * 512],
                     aoq[H % 2][:, :], waits=[(S["v_aoe"], H + 1)],
                     inc=S["s_aod"][H % 2], inck=16)

        gchunk = [0]

        def emit_exp(v):
            o2 = v // 32
            w = [(S["v_ngm"], v + 1), (S["v_rcp"], v - 3)]
            if v >= 3:
                w.append((S["t_pt"], pt_end_unit[v - 3]))
            p.op("scalar", "activation", Psb[v % 3][:, 0:LP[o2]],
                 Ssb[v % 3][:, 0:LP[o2]], AF.Exp,
                 bias=negm4[:, v % 4:v % 4 + 1], scale=1.0,
                 accum_out=lsum4[:, v % 4:v % 4 + 1], waits=w, inc=S["s_exp"])

        for o in range(NO):
            for hh in range(NKV):
                w = []
                if hh == 0:
                    w = [(S["t_smm"], smm_end[o - 2] if o >= 2 else 0)]
                    if o == 0:
                        w += kst_done
                    w = [(s, v) for (s, v) in w if v > 0]
                p.op("sync", "dma_start",
                     ksb[o % 2][:, hh * 2048: hh * 2048 + LP[o]],
                     k_scr[hh * 128:(hh + 1) * 128, 0:LP[o]],
                     waits=w, inc=S["d_k"][o % 2], inck=16)
            for j in range(2 * (o + 1)):
                w = []
                if j == 0:
                    w = [(S["t_av"], av_end[o - 2] if o >= 2 else 0)]
                    if o == 0:
                        w += vst_done
                    w = [(s, v) for (s, v) in w if v > 0]
                p.op("sync", "dma_start", vsb[o % 2][:, j * 1024:(j + 1) * 1024],
                     v_scr[j * 128:(j + 1) * 128, :], waits=w,
                     inc=S["d_v"][o % 2], inck=16)
            qw = [(S["t_smm"], smm_end[o - 2] if o >= 2 else 0)]
            if o == 0:
                qw += qst_done
            p.op("sync", "dma_start", qsb[o % 2][:, :],
                 q_scr[o * 128:(o + 1) * 128, :],
                 waits=[(s, v) for (s, v) in qw if v > 0],
                 inc=S["d_q"][o % 2], inck=16)
            k_ck = (S["d_k"][o % 2], S["d_k"][o % 2].v)
            v_ck = (S["d_v"][o % 2], S["d_v"][o % 2].v)
            q_ck = (S["d_q"][o % 2], S["d_q"][o % 2].v)
            ns = (LP[o] + 511) // 512
            for h in range(NKV):
                for g in range(4):
                    u = o * 32 + h * 4 + g
                    if u > o * 32:
                        emit_exp(u - 1)
                    for c in range(ns):
                        cw = min(512, LP[o] - c * 512)
                        gc = gchunk[0]
                        w = [(S["s_scp"], gc - 1)]
                        if c == 0 and h == 0 and g == 0:
                            w += [k_ck, v_ck, q_ck]
                        p.op("tensor", "matmul",
                             Sps[:, (gc % 2) * 512:(gc % 2) * 512 + cw],
                             qsb[o % 2][:, (h * 4 + g) * 128:(h * 4 + g + 1) * 128],
                             ksb[o % 2][:, h * 2048 + c * 512: h * 2048 + c * 512 + cw],
                             start=True, stop=True, waits=w, inc=S["t_smm"])
                        w2 = [(S["t_smm"], S["t_smm"].v)]
                        if c == 0:
                            w2.append((S["s_exp"], u - 2))
                        p.op("scalar", "copy",
                             Ssb[u % 3][:, c * 512:c * 512 + cw],
                             Sps[:, (gc % 2) * 512:(gc % 2) * 512 + cw],
                             waits=w2, inc=S["s_scp"])
                        gchunk[0] += 1
                    p.op("vector", "tensor_tensor",
                         Ssb[u % 3][:, LP[o] - 256:LP[o]],
                         Ssb[u % 3][:, LP[o] - 256:LP[o]],
                         cmk[:, o * 512 + 256:(o + 1) * 512], ALU.add,
                         waits=[(S["s_scp"], S["s_scp"].v),
                                (S["s_exp"], u - 3)])
                    p.op("vector", "reduce_max", negm4[:, u % 4:u % 4 + 1],
                         Ssb[u % 3][:, 0:LP[o]], axis=AX.X, negate=True,
                         inc=S["v_ngm"])
                    if u > o * 32 + 1:
                        emit_pt(u - 2)
                        emit_ptevac(u - 2)
                    if u > o * 32 + 2:
                        emit_av(u - 3)
            L = o * 32 + 31
            emit_exp(L)
            emit_pt(L - 1)
            emit_ptevac(L - 1)
            emit_av(L - 2)
            emit_pt(L)
            emit_ptevac(L)
            emit_av(L - 1)
            emit_av(L)
            smm_end[o] = S["t_smm"].v
            av_end[o] = S["t_av"].v
        p.wait("scalar", S["s_aod"][0], S["s_aod"][0].v)
        p.wait("scalar", S["s_aod"][1], S["s_aod"][1].v)
        for i in range(2):
            p.wait("sync", S["d_k"][i], S["d_k"][i].v)
            p.wait("sync", S["d_v"][i], S["d_v"][i].v)
            p.wait("sync", S["d_q"][i], S["d_q"][i].v)
        p.run()

    _st5.close()
    with nc.sbuf_tensor("r1sb", [128, NO * D], BF16) as r1sb:

        _st6 = ExitStack()
        aoc0 = _st6.enter_context(nc.sbuf_tensor("aoc0", [128, 1024], BF16))
        aoc1 = _st6.enter_context(nc.sbuf_tensor("aoc1", [128, 1024], BF16))
        aoc2 = _st6.enter_context(nc.sbuf_tensor("aoc2", [128, 1024], BF16))
        woc0 = _st6.enter_context(nc.sbuf_tensor("woc0", [128, 512], BF16))
        woc1 = _st6.enter_context(nc.sbuf_tensor("woc1", [128, 512], BF16))
        woc2 = _st6.enter_context(nc.sbuf_tensor("woc2", [128, 512], BF16))
        xr0 = _st6.enter_context(nc.sbuf_tensor("xr0", [128, 512], F32))
        xr1 = _st6.enter_context(nc.sbuf_tensor("xr1", [128, 512], F32))
        xr2 = _st6.enter_context(nc.sbuf_tensor("xr2", [128, 512], F32))
        xr3 = _st6.enter_context(nc.sbuf_tensor("xr3", [128, 512], F32))
        tw0 = _st6.enter_context(nc.sbuf_tensor("tw0", [128, 512], F32))
        tw1 = _st6.enter_context(nc.sbuf_tensor("tw1", [128, 512], F32))
        sqd0 = _st6.enter_context(nc.sbuf_tensor("sqd0", [128, 512], BF16))
        sqd1 = _st6.enter_context(nc.sbuf_tensor("sqd1", [128, 512], BF16))
        ws0 = _st6.enter_context(nc.sbuf_tensor("ws0", [128, 2], F32))
        ws1 = _st6.enter_context(nc.sbuf_tensor("ws1", [128, 2], F32))
        sacc = _st6.enter_context(nc.sbuf_tensor("sacc", [128, 8], F32))
        qacc = _st6.enter_context(nc.sbuf_tensor("qacc", [128, 8], F32))
        mn = _st6.enter_context(nc.sbuf_tensor("mn", [128, 1], F32))
        mnn = _st6.enter_context(nc.sbuf_tensor("mnn", [128, 1], F32))
        vr = _st6.enter_context(nc.sbuf_tensor("vr", [128, 1], F32))
        sd = _st6.enter_context(nc.sbuf_tensor("sd", [128, 1], F32))
        rs = _st6.enter_context(nc.sbuf_tensor("rs", [128, 1], F32))
        wp0 = _st6.enter_context(nc.psum_tensor("wp0", [128, 512], F32))
        wp1 = _st6.enter_context(nc.psum_tensor("wp1", [128, 512], F32))
        wp2 = _st6.enter_context(nc.psum_tensor("wp2", [128, 512], F32))
        wp3 = _st6.enter_context(nc.psum_tensor("wp3", [128, 512], F32))
        wp4 = _st6.enter_context(nc.psum_tensor("wp4", [128, 512], F32))
        wp5 = _st6.enter_context(nc.psum_tensor("wp5", [128, 512], F32))
        wp6 = _st6.enter_context(nc.psum_tensor("wp6", [128, 512], F32))
        wp7 = _st6.enter_context(nc.psum_tensor("wp7", [128, 512], F32))
        if True:
            aoc = (aoc0, aoc1, aoc2)
            woc = (woc0, woc1, woc2)
            xr = (xr0, xr1, xr2, xr3)
            tw = (tw0, tw1)
            sqd = (sqd0, sqd1)
            ws = (ws0, ws1)
            wp = (wp0, wp1, wp2, wp3, wp4, wp5, wp6, wp7)
            p = Ph(nc, "wo")
            p.op("vector", "memset", sacc[:, :], 0.0)
            p.op("vector", "memset", qacc[:, :], 0.0)
            aod_done = ([(S["s_aod"][0], S["s_aod"][0].v)] +
                        [(S["s_aod"][1], S["s_aod"][1].v)])
            for n in range(NO):
                for k in range(KC):
                    c = n * KC + k
                    p.op("sync", "dma_start", aoc[c % 3][:, :],
                         ao_scr[:, k * 128:(k + 1) * 128], transpose=True,
                         waits=[(S["t_wmm"], c - 2)] +
                               (aod_done if c == 0 else []),
                         inc=S["d_ao"][c % 3], inck=16)
                    aov = (S["d_ao"][c % 3], S["d_ao"][c % 3].v)
                    p.op("sync", "dma_start", woc[c % 3][:, :],
                         wo_bf[k * 128:(k + 1) * 128, n * 512:(n + 1) * 512],
                         waits=[(S["t_wmm"], c - 2)] +
                               ([(S["d_bwo"], 16)] if c == 0 else []),
                         inc=S["d_wo"][c % 3], inck=16)
                    wov = (S["d_wo"][c % 3], S["d_wo"][c % 3].v)
                    for t in range(NO):
                        w = []
                        if t == 0:
                            w = [aov, wov]
                        if k == 0:
                            w.append((S["v_wadd"], (n - 1) * 8 + t + 1 if n >= 1 else 0))
                        p.op("tensor", "matmul", wp[t][:, :],
                             aoc[c % 3][:, t * 128:(t + 1) * 128], woc[c % 3][:, :],
                             start=(k == 0), stop=(k == KC - 1), waits=w,
                             inc=(S["t_wmm"] if t == NO - 1 else None))
                for t in range(NO):
                    e = n * 8 + t
                    p.op("scalar", "dma_start", xr[e % 4][:, :],
                         x_own[t * 128:(t + 1) * 128, n * 512:(n + 1) * 512],
                         waits=[(S["v_wadd"], e - 3)], inc=S["d_xr"][e % 4], inck=16)
                    xrv = (S["d_xr"][e % 4], S["d_xr"][e % 4].v)
                    p.op("vector", "tensor_tensor", tw[e % 2][:, :], wp[t][:, :],
                         xr[e % 4][:, :], ALU.add,
                         waits=[(S["t_wmm"], (n + 1) * KC), xrv,
                                (S["s_wst"], e - 1)], inc=S["v_wadd"])
                    if e >= 1:
                        p.op("vector", "tensor_tensor", sacc[:, (e - 1) % 8:(e - 1) % 8 + 1],
                             sacc[:, (e - 1) % 8:(e - 1) % 8 + 1], ws[(e - 1) % 2][:, 0:1],
                             ALU.add, waits=[(S["s_wst"], e)])
                        p.op("vector", "tensor_tensor", qacc[:, (e - 1) % 8:(e - 1) % 8 + 1],
                             qacc[:, (e - 1) % 8:(e - 1) % 8 + 1], ws[(e - 1) % 2][:, 1:2],
                             ALU.add, inc=S["v_wsta"])
                    p.op("scalar", "activation", r1sb[:, t * D + n * 512: t * D + (n + 1) * 512],
                         tw[e % 2][:, :], AF.Copy, bias=0.0, scale=1.0,
                         accum_out=ws[e % 2][:, 0:1],
                         waits=[(S["v_wadd"], e + 1), (S["v_wsta"], e - 1)])
                    p.op("scalar", "activation", sqd[e % 2][:, :], tw[e % 2][:, :],
                         AF.Square, bias=0.0, scale=1.0,
                         accum_out=ws[e % 2][:, 1:2], inc=S["s_wst"])
            e = NO * 8 - 1
            p.op("vector", "tensor_tensor", sacc[:, e % 8:e % 8 + 1],
                 sacc[:, e % 8:e % 8 + 1], ws[e % 2][:, 0:1], ALU.add,
                 waits=[(S["s_wst"], e + 1)])
            p.op("vector", "tensor_tensor", qacc[:, e % 8:e % 8 + 1],
                 qacc[:, e % 8:e % 8 + 1], ws[e % 2][:, 1:2], ALU.add,
                 inc=S["v_wsta"])
            for t in range(NO):
                p.op("vector", "tensor_scalar", mn[:, :], sacc[:, t:t + 1],
                     1.0 / D, None, ALU.mult,
                     waits=([(S["v_wsta"], NO * 8)] if t == 0 else
                            [(S["s_sqrt"], t)]))
                p.op("vector", "tensor_scalar", vr[:, :], qacc[:, t:t + 1],
                     1.0 / D, EPS, ALU.mult, ALU.add)
                p.op("vector", "tensor_tensor", sd[:, :], mn[:, :], mn[:, :],
                     ALU.mult)
                p.op("vector", "tensor_scalar", mnn[:, :], mn[:, :], -1.0, None,
                     ALU.mult)
                p.op("vector", "tensor_tensor", vr[:, :], vr[:, :], sd[:, :],
                     ALU.subtract, inc=S["v_var"])
                p.op("scalar", "activation", sd[:, :], vr[:, :], AF.Sqrt,
                     bias=0.0, scale=1.0, waits=[(S["v_var"], t + 1)],
                     inc=S["s_sqrt"])
                p.op("vector", "reciprocal", rs[:, :], sd[:, :],
                     waits=[(S["s_sqrt"], t + 1)])
                p.op("vector", "drain")
                p.op("vector", "tensor_scalar", r1sb[:, t * D:(t + 1) * D],
                     r1sb[:, t * D:(t + 1) * D], mnn[:, :], rs[:, :],
                     ALU.add, ALU.mult, inc=S["v_ln"])
                p.op("scalar", "dma_start", x1_scr[t * 128:(t + 1) * 128, :],
                     r1sb[:, t * D:(t + 1) * D], waits=[(S["v_ln"], t + 1)],
                     inc=S["s_x1"], inck=16)
            p.wait("scalar", S["s_x1"], S["s_x1"].v)
            for i in range(4):
                p.wait("scalar", S["d_xr"][i], S["d_xr"][i].v)
            for i in range(3):
                p.wait("sync", S["d_ao"][i], S["d_ao"][i].v)
                p.wait("sync", S["d_wo"][i], S["d_wo"][i].v)
            p.run()

        _st6.close()
        with nc.sbuf_tensor("x1t_sb", [128, KC * 1024], BF16) as x1t_sb:
            _st7 = ExitStack()
            c1b = _st7.enter_context(nc.sbuf_tensor("c1b", [128, KC * RANK], BF16))
            c2sb = _st7.enter_context(nc.sbuf_tensor("c2sb", [RANK, ITR], BF16))
            r16f = _st7.enter_context(nc.sbuf_tensor("r16f", [128, 8 * RANK], F32))
            r16t_sb = _st7.enter_context(nc.sbuf_tensor("r16t_sb", [RANK, 1024], BF16))
            tmpidx0 = _st7.enter_context(nc.sbuf_tensor("tmpidx0", [128, 256], F32))
            tmpidx1 = _st7.enter_context(nc.sbuf_tensor("tmpidx1", [128, 256], F32))
            bmx = _st7.enter_context(nc.sbuf_tensor("bmx", [128, 2], F32))
            r16ps = _st7.enter_context(nc.psum_tensor("r16ps", [128, 512], F32))
            ips0 = _st7.enter_context(nc.psum_tensor("ips0", [128, 512], F32))
            ips1 = _st7.enter_context(nc.psum_tensor("ips1", [128, 512], F32))
            r16tps = _st7.enter_context(nc.psum_tensor("r16tps", [RANK, 1024], F32))
            if True:
                ips = (ips0, ips1)
                p = Ph(nc, "router")
                x1_done = S["s_x1"].v
                for k in range(KC):
                    p.op("sync", "dma_start", x1t_sb[:, k * 1024:(k + 1) * 1024],
                         x1_scr[:, k * 128:(k + 1) * 128], transpose=True,
                         waits=([(S["s_x1"], x1_done)] if k == 0 else ()),
                         inc=S["d_x1t"], inck=16)
                p.op("sync", "dma_start", c1b[:, :], c1_bfs[:, :],
                     waits=[(S["d_castc"], 32)], inc=S["d_c1"], inck=16)
                p.op("sync", "dma_start", c2sb[:, :], c2_bfs[:, :],
                     inc=S["d_c2"], inck=16)
                for t in range(NO):
                    for k in range(KC):
                        w = []
                        if k == 0:
                            w = [(S["s_r16"], t), (S["d_x1t"], 16 * KC),
                                 (S["d_c1"], 16)]
                        p.op("tensor", "matmul", r16ps[:, 0:RANK],
                             x1t_sb[:, k * 1024 + t * 128: k * 1024 + (t + 1) * 128],
                             c1b[:, k * RANK:(k + 1) * RANK],
                             start=(k == 0), stop=(k == KC - 1), waits=w,
                             inc=(S["t_r16"] if k == KC - 1 else None))
                    p.op("scalar", "copy", r16f[:, t * RANK:(t + 1) * RANK],
                         r16ps[:, 0:RANK], waits=[(S["t_r16"], t + 1)],
                         inc=S["s_r16"])
                for t in range(NO):
                    p.op("tensor", "transpose", r16tps[:, t * 128:(t + 1) * 128],
                         r16f[:, t * RANK:(t + 1) * RANK], idf[:, :],
                         waits=([(S["s_r16"], NO)] if t == 0 else ()),
                         inc=(S["t_r16t"] if t == NO - 1 else None))
                p.op("scalar", "copy", r16t_sb[:, :], r16tps[:, :],
                     waits=[(S["t_r16t"], 1)], inc=S["s_r16t"])
                for r in range(NRS):
                    for t in range(NO):
                        u = r * 8 + t
                        p.op("tensor", "matmul", ips[u % 2][:, :],
                             r16t_sb[:, t * 128:(t + 1) * 128],
                             c2sb[:, r * 512:(r + 1) * 512],
                             start=True, stop=True,
                             waits=[(S["v_idx"], u - 1), (S["s_r16t"], 1),
                                    (S["d_c2"], 16)],
                             inc=S["t_imm"])
                        tmpidx = (tmpidx0, tmpidx1)
                        for b in range(2):
                            p.op("vector", "reduce_max", bmx[:, b:b + 1],
                                 ips[u % 2][:, b * 256:(b + 1) * 256], axis=AX.X,
                                 waits=([(S["t_imm"], u + 1)] if b == 0 else ()))
                        for b in range(2):
                            p.op("vector", "scalar_tensor_tensor", tmpidx[b][:, :],
                                 ips[u % 2][:, b * 256:(b + 1) * 256], bmx[:, b:b + 1],
                                 iota[:, :], ALU.is_ge, ALU.mult)
                        for b in range(2):
                            p.op("vector", "reduce_sum",
                                 idxs[:, t * 56 + r * 2 + b: t * 56 + r * 2 + b + 1],
                                 tmpidx[b][:, :], axis=AX.X,
                                 inc=(S["v_idx"] if b == 1 else None))
                p.wait("sync", S["d_x1t"], S["d_x1t"].v)
                p.wait("sync", S["d_c2"], S["d_c2"].v)
                p.wait("sync", S["d_c1"], S["d_c1"].v)
                p.run()

            _st7.close()
            _st8 = ExitStack()
            w1c0 = _st8.enter_context(nc.sbuf_tensor("w1c0", [128, 1024], BF16))
            w1c1 = _st8.enter_context(nc.sbuf_tensor("w1c1", [128, 1024], BF16))
            w1c2 = _st8.enter_context(nc.sbuf_tensor("w1c2", [128, 1024], BF16))
            w1c3 = _st8.enter_context(nc.sbuf_tensor("w1c3", [128, 1024], BF16))
            w1c4 = _st8.enter_context(nc.sbuf_tensor("w1c4", [128, 1024], BF16))
            w1c5 = _st8.enter_context(nc.sbuf_tensor("w1c5", [128, 1024], BF16))
            zr0 = _st8.enter_context(nc.sbuf_tensor("zr0", [128, 512], BF16))
            zr1 = _st8.enter_context(nc.sbuf_tensor("zr1", [128, 512], BF16))
            ssb0 = _st8.enter_context(nc.sbuf_tensor("ssb0", [128, 512], BF16))
            ssb1 = _st8.enter_context(nc.sbuf_tensor("ssb1", [128, 512], BF16))
            zp0 = _st8.enter_context(nc.psum_tensor("zp0", [128, 512], F32))
            zp1 = _st8.enter_context(nc.psum_tensor("zp1", [128, 512], F32))
            zp2 = _st8.enter_context(nc.psum_tensor("zp2", [128, 512], F32))
            zp3 = _st8.enter_context(nc.psum_tensor("zp3", [128, 512], F32))
            zp4 = _st8.enter_context(nc.psum_tensor("zp4", [128, 512], F32))
            zp5 = _st8.enter_context(nc.psum_tensor("zp5", [128, 512], F32))
            zp6 = _st8.enter_context(nc.psum_tensor("zp6", [128, 512], F32))
            zp7 = _st8.enter_context(nc.psum_tensor("zp7", [128, 512], F32))
            if True:
                w1c = (w1c0, w1c1, w1c2, w1c3, w1c4, w1c5)
                zr = (zr0, zr1)
                ssb = (ssb0, ssb1)
                zp = (zp0, zp1, zp2, zp3, zp4, zp5, zp6, zp7)
                p = Ph(nc, "zmm")
                for r in range(NRS):
                    for k in range(KC):
                        c = r * KC + k
                        if k % 2 == 0:
                            cp = c // 2
                            p.op("gpsimd", "dma_start", w1c[cp % 6][:, :],
                                 bass.AP(w1T, k * 128 * ITR + r * 512,
                                         [[ITR, 128], [128 * ITR, 2], [1, 512]]),
                                 waits=[(S["t_zmm"], 2 * cp - 10)],
                                 inc=S["d_w1"][cp % 6], inck=16)
                            w1v = (S["d_w1"][cp % 6], S["d_w1"][cp % 6].v)
                        cp = c // 2
                        half = (k % 2) * 512
                        for t in range(NO):
                            w = []
                            if t == 0 and k % 2 == 0:
                                w.append(w1v)
                            if k == 0:
                                w.append((S["s_zrl"], (r - 1) * 8 + t + 1 if r >= 1 else 0))
                            p.op("tensor", "matmul", zp[t][:, :],
                                 x1t_sb[:, k * 1024 + t * 128: k * 1024 + (t + 1) * 128],
                                 w1c[cp % 6][:, half:half + 512],
                                 start=(k == 0), stop=(k == KC - 1), waits=w,
                                 inc=(S["t_zmm"] if t == NO - 1 else None))
                    for t in range(NO):
                        e = r * 8 + t
                        p.op("scalar", "activation", zr[e % 2][:, :], zp[t][:, :],
                             AF.Relu, bias=0.0, scale=1.0,
                             waits=[(S["t_zmm"], r * KC + KC), (S["v_zmk"], e - 1)],
                             inc=S["s_zrl"])
                        for b in range(2):
                            p.op("vector", "scalar_tensor_tensor",
                                 ssb[e % 2][:, b * 256:(b + 1) * 256], iota[:, :],
                                 idxs[:, t * 56 + r * 2 + b: t * 56 + r * 2 + b + 1],
                                 zr[e % 2][:, b * 256:(b + 1) * 256],
                                 ALU.is_equal, ALU.mult,
                                 waits=([(S["s_zrl"], e + 1),
                                         (S["d_ss"][e % 2], S["d_ss"][e % 2].v)]
                                        if b == 0 else ()),
                                 inc=(S["v_zmk"] if b == 1 else None))
                        p.op("scalar", "dma_start",
                             s_scr[t * 128:(t + 1) * 128, r * 512:(r + 1) * 512],
                             ssb[e % 2][:, :], waits=[(S["v_zmk"], e + 1)],
                             inc=S["d_ss"][e % 2], inck=16)
                p.wait("scalar", S["d_ss"][0], S["d_ss"][0].v)
                p.wait("scalar", S["d_ss"][1], S["d_ss"][1].v)
                for i in range(6):
                    p.wait("gpsimd", S["d_w1"][i], S["d_w1"][i].v)
                p.run()

            _st8.close()
        _st9 = ExitStack()
        w2c0 = _st9.enter_context(nc.sbuf_tensor("w2c0", [128, 1024], BF16))
        w2c1 = _st9.enter_context(nc.sbuf_tensor("w2c1", [128, 1024], BF16))
        w2c2 = _st9.enter_context(nc.sbuf_tensor("w2c2", [128, 1024], BF16))
        w2c3 = _st9.enter_context(nc.sbuf_tensor("w2c3", [128, 1024], BF16))
        w2c4 = _st9.enter_context(nc.sbuf_tensor("w2c4", [128, 1024], BF16))
        w2c5 = _st9.enter_context(nc.sbuf_tensor("w2c5", [128, 1024], BF16))
        stc0 = _st9.enter_context(nc.sbuf_tensor("stc0", [128, 1024], BF16))
        stc1 = _st9.enter_context(nc.sbuf_tensor("stc1", [128, 1024], BF16))
        stc2 = _st9.enter_context(nc.sbuf_tensor("stc2", [128, 1024], BF16))
        stc3 = _st9.enter_context(nc.sbuf_tensor("stc3", [128, 1024], BF16))
        stc4 = _st9.enter_context(nc.sbuf_tensor("stc4", [128, 1024], BF16))
        stc5 = _st9.enter_context(nc.sbuf_tensor("stc5", [128, 1024], BF16))
        to0 = _st9.enter_context(nc.sbuf_tensor("to0", [128, 512], F32))
        to1 = _st9.enter_context(nc.sbuf_tensor("to1", [128, 512], F32))
        sqe0 = _st9.enter_context(nc.sbuf_tensor("sqe0", [128, 512], BF16))
        sqe1 = _st9.enter_context(nc.sbuf_tensor("sqe1", [128, 512], BF16))
        os0 = _st9.enter_context(nc.sbuf_tensor("os0", [128, 2], F32))
        os1 = _st9.enter_context(nc.sbuf_tensor("os1", [128, 2], F32))
        sac2 = _st9.enter_context(nc.sbuf_tensor("sac2", [128, 8], F32))
        qac2 = _st9.enter_context(nc.sbuf_tensor("qac2", [128, 8], F32))
        mn2 = _st9.enter_context(nc.sbuf_tensor("mn2", [128, 1], F32))
        mnn2 = _st9.enter_context(nc.sbuf_tensor("mnn2", [128, 1], F32))
        vr2 = _st9.enter_context(nc.sbuf_tensor("vr2", [128, 1], F32))
        sd2 = _st9.enter_context(nc.sbuf_tensor("sd2", [128, 1], F32))
        rs2 = _st9.enter_context(nc.sbuf_tensor("rs2", [128, 1], F32))
        outsb0 = _st9.enter_context(nc.sbuf_tensor("outsb0", [128, D], F32))
        outsb1 = _st9.enter_context(nc.sbuf_tensor("outsb1", [128, D], F32))
        op0 = _st9.enter_context(nc.psum_tensor("op0", [128, 512], F32))
        op1 = _st9.enter_context(nc.psum_tensor("op1", [128, 512], F32))
        op2 = _st9.enter_context(nc.psum_tensor("op2", [128, 512], F32))
        op3 = _st9.enter_context(nc.psum_tensor("op3", [128, 512], F32))
        op4 = _st9.enter_context(nc.psum_tensor("op4", [128, 512], F32))
        op5 = _st9.enter_context(nc.psum_tensor("op5", [128, 512], F32))
        op6 = _st9.enter_context(nc.psum_tensor("op6", [128, 512], F32))
        op7 = _st9.enter_context(nc.psum_tensor("op7", [128, 512], F32))
        if True:
            w2c = (w2c0, w2c1, w2c2, w2c3, w2c4, w2c5)
            stc = (stc0, stc1, stc2, stc3, stc4, stc5)
            to = (to0, to1)
            sqe = (sqe0, sqe1)
            osb = (os0, os1)
            outsb = (outsb0, outsb1)
            opb = (op0, op1, op2, op3, op4, op5, op6, op7)
            p = Ph(nc, "outmm")
            p.op("vector", "memset", sac2[:, :], 0.0)
            p.op("vector", "memset", qac2[:, :], 0.0)
            ss_done = [(S["d_ss"][i], S["d_ss"][i].v) for i in range(2)]
            for n in range(NO):
                for k in range(NKO):
                    c = n * NKO + k
                    if k % 2 == 0:
                        cp = c // 2
                        p.op("gpsimd", "dma_start", w2c[cp % 6][:, :],
                             bass.AP(w2t_in, k * 128 * D + n * 512,
                                     [[D, 128], [128 * D, 2], [1, 512]]),
                             waits=[(S["t_omm"], 2 * cp - 10)],
                             inc=S["d_w2"][cp % 6], inck=16)
                        w2v = (S["d_w2"][cp % 6], S["d_w2"][cp % 6].v)
                    cp = c // 2
                    half2 = (k % 2) * 512
                    p.op("sync", "dma_start", stc[c % 6][:, :],
                         s_scr[:, k * 128:(k + 1) * 128], transpose=True,
                         waits=[(S["t_omm"], c - 5)] +
                               (ss_done if c == 0 else []),
                         inc=S["d_st"][c % 6], inck=16)
                    stv = (S["d_st"][c % 6], S["d_st"][c % 6].v)
                    for t in range(NO):
                        w = []
                        if t == 0:
                            w = ([w2v] if k % 2 == 0 else []) + [stv]
                        if k == 0:
                            w.append((S["v_oad"], (n - 1) * 8 + t + 1 if n >= 1 else 0))
                        p.op("tensor", "matmul", opb[t][:, :],
                             stc[c % 6][:, t * 128:(t + 1) * 128],
                             w2c[cp % 6][:, half2:half2 + 512],
                             start=(k == 0), stop=(k == NKO - 1), waits=w,
                             inc=(S["t_omm"] if t == NO - 1 else None))
                for t in range(NO):
                    e = n * 8 + t
                    p.op("vector", "tensor_tensor", to[e % 2][:, :], opb[t][:, :],
                         r1sb[:, t * D + n * 512: t * D + (n + 1) * 512], ALU.add,
                         waits=[(S["t_omm"], (n + 1) * NKO), (S["s_ost"], e - 1)],
                         inc=S["v_oad"])
                    if e >= 1:
                        p.op("vector", "tensor_tensor",
                             sac2[:, (e - 1) % 8:(e - 1) % 8 + 1],
                             sac2[:, (e - 1) % 8:(e - 1) % 8 + 1],
                             osb[(e - 1) % 2][:, 0:1], ALU.add,
                             waits=[(S["s_ost"], e)])
                        p.op("vector", "tensor_tensor",
                             qac2[:, (e - 1) % 8:(e - 1) % 8 + 1],
                             qac2[:, (e - 1) % 8:(e - 1) % 8 + 1],
                             osb[(e - 1) % 2][:, 1:2], ALU.add, inc=S["v_osta"])
                    p.op("scalar", "activation",
                         r1sb[:, t * D + n * 512: t * D + (n + 1) * 512],
                         to[e % 2][:, :], AF.Copy, bias=0.0, scale=1.0,
                         accum_out=osb[e % 2][:, 0:1],
                         waits=[(S["v_oad"], e + 1), (S["v_osta"], e - 1)])
                    p.op("scalar", "activation", sqe[e % 2][:, :], to[e % 2][:, :],
                         AF.Square, bias=0.0, scale=1.0,
                         accum_out=osb[e % 2][:, 1:2], inc=S["s_ost"])
            e = NO * 8 - 1
            p.op("vector", "tensor_tensor", sac2[:, e % 8:e % 8 + 1],
                 sac2[:, e % 8:e % 8 + 1], osb[e % 2][:, 0:1], ALU.add,
                 waits=[(S["s_ost"], e + 1)])
            p.op("vector", "tensor_tensor", qac2[:, e % 8:e % 8 + 1],
                 qac2[:, e % 8:e % 8 + 1], osb[e % 2][:, 1:2], ALU.add,
                 inc=S["v_osta"])
            for t in range(NO):
                p.op("vector", "tensor_scalar", mn2[:, :], sac2[:, t:t + 1],
                     1.0 / D, None, ALU.mult,
                     waits=([(S["v_osta"], NO * 8)] if t == 0 else
                            [(S["s_sqrt2"], t)]))
                p.op("vector", "tensor_scalar", vr2[:, :], qac2[:, t:t + 1],
                     1.0 / D, EPS, ALU.mult, ALU.add)
                p.op("vector", "tensor_tensor", sd2[:, :], mn2[:, :], mn2[:, :],
                     ALU.mult)
                p.op("vector", "tensor_scalar", mnn2[:, :], mn2[:, :], -1.0, None,
                     ALU.mult)
                p.op("vector", "tensor_tensor", vr2[:, :], vr2[:, :], sd2[:, :],
                     ALU.subtract, inc=S["v_var2"])
                p.op("scalar", "activation", sd2[:, :], vr2[:, :], AF.Sqrt,
                     bias=0.0, scale=1.0, waits=[(S["v_var2"], t + 1)],
                     inc=S["s_sqrt2"])
                p.op("vector", "reciprocal", rs2[:, :], sd2[:, :],
                     waits=[(S["s_sqrt2"], t + 1)])
                p.op("vector", "drain")
                p.op("vector", "tensor_scalar", outsb[t % 2][:, :],
                     r1sb[:, t * D:(t + 1) * D], mnn2[:, :], rs2[:, :],
                     ALU.add, ALU.mult,
                     waits=[(S["d_out"][t % 2], S["d_out"][t % 2].v)],
                     inc=S["v_ln2"])
                p.op("scalar", "dma_start", out[t * 128:(t + 1) * 128, :],
                     outsb[t % 2][:, :], waits=[(S["v_ln2"], t + 1)],
                     inc=S["d_out"][t % 2], inck=16)
            for i in range(6):
                p.wait("gpsimd", S["d_w2"][i], S["d_w2"][i].v)
                p.wait("sync", S["d_st"][i], S["d_st"][i].v)
            p.wait("scalar", S["d_out"][0], S["d_out"][0].v)
            p.wait("scalar", S["d_out"][1], S["d_out"][1].v)
            p.wait("gpsimd", S["d_bwo"], S["d_bwo"].v)
            p.wait("gpsimd", S["d_bw1"], S["d_bw1"].v)
            p.wait("gpsimd", S["d_bw2"], S["d_bw2"].v)
            p.run()

        _st9.close()
    es.close()
    return nc


def _host_tables():
    rope_theta, rope_factor, high_ff, low_ff, orig_max = 500000.0, 8.0, 4.0, 1.0, 8192
    inv = 1.0 / rope_theta ** (np.arange(0, HD, 2, dtype=np.float32) / HD)
    wavelen = 2.0 * np.pi / inv
    low_wl = orig_max / low_ff
    high_wl = orig_max / high_ff
    inv_scaled = np.where(wavelen > low_wl, inv / rope_factor, inv)
    smooth = (orig_max / wavelen - low_ff) / (high_ff - low_ff)
    smoothed = (1.0 - smooth) / rope_factor * inv + smooth * inv
    is_medium = (wavelen <= low_wl) & (wavelen >= high_wl)
    inv_freq = np.where(is_medium, smoothed, inv_scaled).astype(np.float32)
    pos = np.arange(T, dtype=np.float32)
    freqs = pos[:, None] * inv_freq[None, :]
    emb = np.concatenate([freqs, freqs], axis=-1)
    cos_t = np.ascontiguousarray(np.cos(emb).T.astype(np.float32))
    sin_t = np.ascontiguousarray(np.sin(emb).T.astype(np.float32))
    return cos_t, sin_t


def kernel(L0_attn_x, L0_attn_Wq, L0_attn_Wk, L0_attn_Wv, L0_attn_Wo,
           L0_ffn_c1_w, L0_ffn_c2_w, L0_ffn_w1, L0_ffn_w2t):
    x = np.ascontiguousarray(np.asarray(L0_attn_x, dtype=np.float32))
    Wq = np.asarray(L0_attn_Wq, dtype=np.float32)
    Wk = np.asarray(L0_attn_Wk, dtype=np.float32)
    Wv = np.asarray(L0_attn_Wv, dtype=np.float32)
    Wo = np.asarray(L0_attn_Wo, dtype=np.float32)
    c1 = np.asarray(L0_ffn_c1_w, dtype=np.float32)
    c2 = np.asarray(L0_ffn_c2_w, dtype=np.float32)
    w1 = np.asarray(L0_ffn_w1, dtype=np.float32)
    w2t = np.ascontiguousarray(np.asarray(L0_ffn_w2t, dtype=np.float32))

    wqT = Wq.T
    wq_t = np.ascontiguousarray(
        wqT.reshape(KC, 128, NQH, 128).transpose(2, 1, 0, 3).reshape(NQH * 128, D))
    wk_img = np.ascontiguousarray(
        Wk.T.reshape(KC, 128, NKV, 128).transpose(1, 0, 2, 3).reshape(128, KC * NKV * 128))
    wv_img = np.ascontiguousarray(
        Wv.T.reshape(KC, 128, NKV, 128).transpose(1, 0, 2, 3).reshape(128, KC * NKV * 128))
    woT = np.ascontiguousarray(Wo.T)
    w1T = np.ascontiguousarray(w1.T)
    c1_img = np.ascontiguousarray(
        c1.T.reshape(KC, 128, RANK).transpose(1, 0, 2).reshape(128, KC * RANK))
    c2T = np.ascontiguousarray(c2.T)

    cos_t, sin_t = _host_tables()
    scale = 1.0 / np.sqrt(np.float32(HD))

    idb = np.eye(128, dtype=ml_dtypes.bfloat16)
    idf = np.eye(128, dtype=np.float32)
    R = np.zeros((128, 128), dtype=np.float32)
    R[np.arange(64), np.arange(64) + 64] = -1.0
    R[np.arange(64) + 64, np.arange(64)] = 1.0
    rotT = np.ascontiguousarray(R.T)
    iota256 = np.tile(np.arange(256, dtype=np.float32), (128, 1))

    halves = []
    for hh in range(2):
        pos_own = np.concatenate([np.arange(t0 * 128, (t0 + 1) * 128)
                                  for t0 in QT[hh]])
        cosq = np.ascontiguousarray(cos_t[:, pos_own] * scale)
        sinq = np.ascontiguousarray(sin_t[:, pos_own] * scale)
        cm = np.zeros((128, NO * 512), dtype=np.float32)
        for o in range(NO):
            i = QT[hh][o]
            lp = LP[o]
            qpos = i * 128 + np.arange(128)[:, None]
            kpos = lp - 256 + np.arange(256)[None, :]
            cm[:, o * 512 + 256:(o + 1) * 512] = np.where(kpos <= qpos, 0.0, NEG)
        halves.append((cosq, sinq, cm))

    nc = build_nc()
    in_maps = []
    for core in range(8):
        b, hh = core // 2, core % 2
        cosq, sinq, cm = halves[hh]
        xb = np.ascontiguousarray(x[b])
        x_own = np.ascontiguousarray(
            np.concatenate([x[b, t0 * 128:(t0 + 1) * 128] for t0 in QT[hh]], axis=0))
        in_maps.append({
            "xb": xb, "x_own": x_own, "wq_t": wq_t, "wk_img": wk_img,
            "wv_img": wv_img, "woT": woT, "w1T": w1T, "w2t_in": w2t,
            "c1_img": c1_img, "c2T": c2T, "cosk": cos_t, "sink": sin_t,
            "cosq": cosq, "sinq": sinq, "cmask": cm, "idb_in": idb,
            "idf_in": idf, "rotT_in": rotT, "iota_in": iota256,
        })
    res = run_bass_kernel_spmd(nc, in_maps, core_ids=list(range(8)))
    outp = np.empty((4, T, D), dtype=np.float32)
    for core in range(8):
        b, hh = core // 2, core % 2
        oc = res.results[core]["out"]
        for o in range(NO):
            t0 = QT[hh][o]
            outp[b, t0 * 128:(t0 + 1) * 128] = oc[o * 128:(o + 1) * 128]
    return outp
